# version 37 (final)
# baseline (speedup 1.0000x reference)
import contextlib
import numpy as np

import concourse.bass as bass
import concourse.mybir as mybir
from concourse.bass_utils import run_bass_kernel_spmd

F32 = mybir.dt.float32
BF16 = mybir.dt.bfloat16
AF = mybir.ActivationFunctionType
ALU = mybir.AluOpType
AX = mybir.AxisListType

D = 1024
NCORE = 8
HEADS = 16
HD = 64
NG = 4
CH = D // 128

S = 4096
TRUNC = 4
DEBUG_DUMP = None
FFN_CUT = 99


def _derived():
    B = min(512, S)
    return B, S // B, S // 128



def _perm_eo_grouped():
    perm = []
    for g in range(NG):
        for h in range(4 * g, 4 * g + 4):
            perm.extend(h * HD + 2 * j for j in range(32))
        for h in range(4 * g, 4 * g + 4):
            perm.extend(h * HD + 2 * j + 1 for j in range(32))
    return np.array(perm, dtype=np.int64)


def _perm_half_grouped():
    perm = []
    for g in range(NG):
        for h in range(4 * g, 4 * g + 4):
            perm.extend(h * HD + j for j in range(32))
        for h in range(4 * g, 4 * g + 4):
            perm.extend(h * HD + 32 + j for j in range(32))
    return np.array(perm, dtype=np.int64)


def _host_consts():
    B, NB, NTC = _derived()
    c = {}
    mel_max = np.float32(2595.0 * np.log10(1.0 + 4000.0 / 200.0))
    scale = np.power(np.float32(10.0),
                     np.linspace(np.float32(0.0), mel_max, 32, dtype=np.float32)
                     / np.float32(2595.0)) - np.float32(1.0)
    fb = np.float32(200.0) * scale / np.float32(1000.0)
    t = np.arange(S, dtype=np.float32)
    freqs = (t[:, None] * fb[None, :]).astype(np.float32)
    c["COSA"] = np.ascontiguousarray(np.tile(np.cos(freqs).astype(np.float32).T, (4, 1)))
    c["SINA"] = np.ascontiguousarray(np.tile(np.sin(freqs).astype(np.float32).T, (4, 1)))
    inv_freq = (np.float32(1.0)
                / np.power(np.float32(10000.0),
                           np.arange(0, HD, 2, dtype=np.float32) / np.float32(HD)))
    ang = (t[:, None] * inv_freq[None, :]).astype(np.float32)
    c["COSB"] = np.ascontiguousarray(np.tile(np.cos(ang).astype(np.float32).T, (4, 1)))
    c["SINB"] = np.ascontiguousarray(np.tile(np.sin(ang).astype(np.float32).T, (4, 1)))
    kk = np.arange(128)[:, None]
    qq = np.arange(B)[None, :]
    c["CMASK"] = np.concatenate(
        [(kk <= qq - 128 * j).astype(np.float32) for j in range(4)], axis=1)
    c["IDENT"] = np.eye(128, dtype=np.float32)
    c["ONESF"] = np.ones((128, 128), dtype=np.float32)
    ind4 = np.zeros((128, 4), dtype=np.float32)
    for p in range(128):
        ind4[p, p // 32] = 1.0
    c["IND4"] = ind4
    c["IND4M"] = ind4 / np.float32(64.0)
    c["IND4T"] = np.ascontiguousarray(ind4.T)
    ind8 = np.zeros((8, 8 * 128), dtype=np.float32)
    for e in range(8):
        ind8[e, e * 128:(e + 1) * 128] = 1.0
    c["IND8"] = ind8
    return c


def _prep_inputs(inp):
    f = np.float32
    p_eo = _perm_eo_grouped()
    p_hf = _perm_half_grouped()
    m = dict(_host_consts())

    m["AQW"] = np.ascontiguousarray(np.asarray(inp["L0_attn_a_q_w"], dtype=f)[:, p_eo])
    kvw = np.asarray(inp["L0_attn_a_kv_w"], dtype=f)
    kperm = np.concatenate([p_eo, 1024 + np.arange(1024)])
    m["AKVW"] = np.ascontiguousarray(kvw[:, kperm])
    m["AOW"] = np.ascontiguousarray(np.asarray(inp["L0_attn_a_out_w"], dtype=f))
    m["ARW"] = np.ascontiguousarray(
        np.asarray(inp["L0_attn_a_rh_w"], dtype=f).reshape(CH, 128).T)
    m["ARB"] = np.asarray(inp["L0_attn_a_rh_b"], dtype=f).reshape(1, 1)

    def colvec(name, key, perm=None):
        val = np.asarray(inp[key], dtype=f).reshape(-1)
        if perm is not None:
            val = val[perm]
        m[name] = np.ascontiguousarray(val.reshape(-1, 128).T)

    def rowvec(name, key, perm=None):
        val = np.asarray(inp[key], dtype=f).reshape(-1)
        if perm is not None:
            val = val[perm]
        m[name] = np.ascontiguousarray(val.reshape(1, -1))

    rowvec("AQBR", "L0_attn_a_q_b", p_eo)
    rowvec("AKVBR", "L0_attn_a_kv_b", kperm)
    colvec("AOB", "L0_attn_a_out_b")
    colvec("AQNW", "L0_attn_a_qn_w")
    colvec("AQNB", "L0_attn_a_qn_b")
    colvec("AKVNW", "L0_attn_a_kvn_w")
    colvec("AKVNB", "L0_attn_a_kvn_b")
    lnw = np.asarray(inp["L0_attn_a_ln_w"], dtype=f)
    lnb = np.asarray(inp["L0_attn_a_ln_b"], dtype=f)
    m["ALN"] = np.ascontiguousarray(np.stack(
        [np.tile(lnw[0::2], 4), np.tile(lnb[0::2], 4),
         np.tile(lnw[1::2], 4), np.tile(lnb[1::2], 4)], axis=1))

    qkvT = np.asarray(inp["L0_attn_b_qkv"], dtype=f).T
    bperm = np.concatenate([p_hf, 1024 + p_hf, 2048 + np.arange(1024)])
    m["BQKV"] = np.ascontiguousarray(qkvT[:, bperm])
    m["BWO"] = np.ascontiguousarray(np.asarray(inp["L0_attn_b_wo"], dtype=f).T)

    m["FWQ"] = np.ascontiguousarray(np.asarray(inp["L0_ffn_Wq"], dtype=f).T)
    m["FWK"] = np.ascontiguousarray(np.asarray(inp["L0_ffn_Wkc"], dtype=f).T)
    m["FWV"] = np.ascontiguousarray(np.asarray(inp["L0_ffn_Wvc"], dtype=f).T)
    m["FWO"] = np.ascontiguousarray(np.asarray(inp["L0_ffn_Wo"], dtype=f).T)
    m["FWR"] = np.ascontiguousarray(np.asarray(inp["L0_ffn_Wr"], dtype=f).T)
    m["FEB"] = np.asarray(inp["L0_ffn_expert_bias"], dtype=f).reshape(8, 1)
    colvec("FG1", "L0_ffn_g1")
    colvec("FG2", "L0_ffn_g2")
    m["SW1"] = np.ascontiguousarray(np.asarray(inp["L0_ffn_sW1"], dtype=f).T)
    m["SW3"] = np.ascontiguousarray(np.asarray(inp["L0_ffn_sW3"], dtype=f).T)
    m["SW2"] = np.ascontiguousarray(np.asarray(inp["L0_ffn_sW2"], dtype=f).T)
    m["EW1"] = np.ascontiguousarray(
        np.asarray(inp["L0_ffn_eW1"], dtype=f).transpose(0, 2, 1))
    m["EW3"] = np.ascontiguousarray(
        np.asarray(inp["L0_ffn_eW3"], dtype=f).transpose(0, 2, 1))
    m["EW2"] = np.ascontiguousarray(
        np.asarray(inp["L0_ffn_eW2"], dtype=f).transpose(0, 2, 1))

    xT = np.ascontiguousarray(
        np.asarray(inp["L0_attn_a_x"], dtype=f).transpose(0, 2, 1))
    return m, xT



class Ctr:
    def __init__(self, nc, es, name):
        self.h = es.enter_context(nc.semaphore(name))
        self.n = 0
        self._last_auto_id = None

    def inc(self, instr, k=1):
        if instr is not None and id(instr) == self._last_auto_id:
            return self.n
        instr.then_inc(self.h, k)
        self.n += k
        return self.n

    def wait(self, eng, n=None):
        eng.wait_ge(self.h, self.n if n is None else n)


_WRAP_OPS = {
    "tensor_scalar", "tensor_tensor", "tensor_add", "tensor_sub", "tensor_mul",
    "tensor_max", "tensor_copy", "tensor_reduce", "reduce_max", "reduce_sum",
    "reciprocal", "match_replace", "scalar_tensor_tensor", "select", "memset",
    "memzero", "tensor_scalar_add", "tensor_scalar_mul", "tensor_scalar_max",
    "tensor_scalar_min", "tensor_single_scalar", "activation", "copy", "add",
    "mul", "square", "sqrt", "sign", "tensor_relu",
}


class EngProxy:
    def __init__(self, eng, ctr):
        self._e = eng
        self._c = ctr
        self._dirty = {}
        self._waited = 0

    def __getattr__(self, k):
        f = getattr(self._e, k)
        if k not in _WRAP_OPS:
            return f

        def call(*args, **kw):
            need = 0
            for a in args:
                t = getattr(a, "tensor", None)
                if t is not None:
                    need = max(need, self._dirty.get(t.name, 0))
            for a in kw.values():
                t = getattr(a, "tensor", None)
                if t is not None:
                    need = max(need, self._dirty.get(t.name, 0))
            if need > self._waited:
                self._e.wait_ge(self._c.h, need)
                self._waited = need
            instr = f(*args, **kw)
            instr.then_inc(self._c.h, 1)
            self._c.n += 1
            self._c._last_auto_id = id(instr)
            out = args[0]
            t = getattr(out, "tensor", None)
            if t is not None:
                self._dirty[t.name] = self._c.n
            return instr
        return call


class Buf:
    def __init__(self, ap):
        self.ap = ap
        self.rel = []

    def wait(self, *engs):
        for ctr, val in self.rel:
            for e in engs:
                e.wait_ge(ctr.h, val)
        self.rel = []

    def release(self, ctr, val=None):
        self.rel.append((ctr, ctr.n if val is None else val))


class Pool:
    def __init__(self, aps, p=None, tag=None):
        self.bufs = [Buf(a) for a in aps]
        self.i = 0
        if p is not None:
            for idx, b in enumerate(self.bufs):
                b.ctr = p.ctr(f"{tag}{idx}")

    def get(self, *engs):
        b = self.bufs[self.i % len(self.bufs)]
        self.i += 1
        b.wait(*engs)
        return b


class Prog:
    def __init__(self, nc, es):
        self.nc = nc
        self.es = es
        self.sy = nc.sync
        self.g = nc.gpsimd
        self.v = nc.vector
        self.sc = nc.scalar
        self.te = nc.tensor
        self._n = 0
        self._named = {}

    def ctr(self, key=None):
        if key is not None and key in self._named:
            return self._named[key]
        self._n += 1
        c = Ctr(self.nc, self.es, f"c{self._n}")
        if key is not None:
            self._named[key] = c
        return c

    def bar(self):
        self.nc.all_engine_barrier()

    @contextlib.contextmanager
    def phase(self):
        with contextlib.ExitStack() as ps:
            def alloc(shape, dtype, name=None):
                self._n += 1
                nm = name or f"t{self._n}"
                return ps.enter_context(
                    self.nc.sbuf_tensor(nm, list(shape), dtype))
            yield alloc



def build_program(trunc=None, debug_dump=None):
    if trunc is None:
        trunc = TRUNC
    if debug_dump is None:
        debug_dump = DEBUG_DUMP
    B, NB, NTC = _derived()

    es = contextlib.ExitStack()
    nc = bass.Bass("TRN2", target_bir_lowering=False,
                   detect_race_conditions=False)
    p = Prog(nc, es)
    ld = p.ctr("ld")
    st = p.ctr("st")
    tec = p.ctr("tec")
    vec = p.ctr("vec")
    sec = p.ctr("sec")
    aux = p.ctr("aux")
    pec = p.ctr("pec")
    v = p.v = EngProxy(nc.vector, vec)
    sc = p.sc = EngProxy(nc.scalar, sec)
    g = p.g = EngProxy(nc.gpsimd, pec)
    sy, te = p.sy, p.te

    def load(dst, src):
        return ld.inc(sy.dma_start(dst, src), 16)

    def store(dst, src):
        return st.inc(g.dma_start(dst, src), 16)

    def din(name, shape):
        return nc.dram_tensor(name, list(shape), F32, kind="ExternalInput").ap()

    x_in = din("x", (D, S))
    cd = {}
    for name, shape in [
        ("COSA", (128, S)), ("SINA", (128, S)), ("COSB", (128, S)), ("SINB", (128, S)),
        ("CMASK", (128, 4 * B)), ("IDENT", (128, 128)), ("ONESF", (128, 128)),
        ("IND4", (128, 4)), ("IND4M", (128, 4)), ("IND4T", (4, 128)),
        ("IND8", (8, 8 * 128)),
        ("AQW", (D, D)), ("AKVW", (D, 2 * D)), ("AOW", (D, D)), ("ARW", (128, CH)),
        ("ARB", (1, 1)), ("AQBR", (1, D)), ("AKVBR", (1, 2 * D)),
        ("AOB", (128, CH)), ("AQNW", (128, CH)), ("AQNB", (128, CH)),
        ("AKVNW", (128, CH)), ("AKVNB", (128, CH)), ("ALN", (128, 4)),
        ("BQKV", (D, 3 * D)), ("BWO", (D, D)),
        ("FWQ", (D, D)), ("FWK", (D, 256)), ("FWV", (D, 256)), ("FWO", (D, D)),
        ("FWR", (D, 8)), ("FEB", (8, 1)), ("FG1", (128, CH)), ("FG2", (128, CH)),
        ("SW1", (D, D)), ("SW3", (D, D)), ("SW2", (D, D)),
        ("EW1", (8, D, D)), ("EW3", (8, D, D)), ("EW2", (8, D, D)),
    ]:
        cd[name] = din(name, shape)
    out_d = nc.dram_tensor("out", [D, S], F32, kind="ExternalOutput").ap()

    x_hbm = nc.dram_tensor("x_hbm", [D, S], F32).ap()
    qn_sp = nc.dram_tensor("qn_sp", [HEADS, HD, S], BF16).ap()
    kn_sp = nc.dram_tensor("kn_sp", [HEADS, HD, S], BF16).ap()
    v_sp = nc.dram_tensor("v_sp", [HEADS, NTC, 128, HD], BF16).ap()
    o_sp = nc.dram_tensor("o_sp", [D, S], BF16).ap()
    cq_hbm = nc.dram_tensor("cq_hbm", [HEADS, S], BF16).ap()
    xmid_hbm = nc.dram_tensor("xmid_hbm", [D, S], F32).ap()
    zf_hbm = nc.dram_tensor("zf_hbm", [D, S], BF16).ap()
    vrow_hbm = nc.dram_tensor("vrow_hbm", [16, S], F32).ap()
    q_sp = nc.dram_tensor("q_sp", [4, 256, S], BF16).ap()
    dbg_scratch = {"x_hbm": x_hbm, "qn_sp": qn_sp, "kn_sp": kn_sp, "v_sp": v_sp,
                   "o_sp": o_sp, "cq_hbm": cq_hbm, "xmid_hbm": xmid_hbm,
                   "zf_hbm": zf_hbm, "q_sp": q_sp, "vrow_hbm": vrow_hbm}

    sb = nc.alloc_sbuf_tensor
    IDE = sb("ide", [128, 128], F32)
    ONE = sb("one", [128, 128], F32)
    ONEB = sb("oneb", [128, 128], BF16)
    IND4 = sb("ind4", [128, 4], F32)
    IND4M = sb("ind4m", [128, 4], F32)
    IND4T = sb("ind4t", [36, 128], F32)
    IND4TB = sb("ind4tb", [4, 128], BF16)
    IND4B = sb("ind4b", [128, 4], BF16)
    IND4MB = sb("ind4mb", [128, 4], BF16)
    IND8B = sb("ind8b", [8, 8 * 128], BF16)
    ALN = sb("aln", [128, 4], F32)
    SVEC = sb("svec", [128, 16], F32)
    BIAS = sb("biasp", [128, 32], F32)
    S11 = sb("s11", [1, 8], F32)
    NVT = 16
    VT = sb("vtt", [128, NTC * NVT], F32)

    def vt(i):
        return VT[:, NTC * i:NTC * (i + 1)]

    PS2 = [nc.alloc_psum_tensor(f"ps2_{i}", [128, 1024], F32) for i in range(3)]
    PS1 = [nc.alloc_psum_tensor("ps1_0", [128, 512], F32),
           nc.alloc_psum_tensor("ps1_1", [128, 512], F32),
           PS2[2][:, 0:512], PS2[2][:, 512:1024]]

    for dst, name in [(IDE, "IDENT"), (ONE, "ONESF"), (IND4, "IND4"),
                      (IND4M, "IND4M"), (ALN, "ALN")]:
        load(dst[:], cd[name][:])
    load(IND4T[0:4, :], cd["IND4T"][:])
    load(IND4T[32:36, :], cd["IND4T"][:])
    store(IND4TB[:], cd["IND4T"][:])
    store(IND4B[:], cd["IND4"][:])
    store(IND4MB[:], cd["IND4M"][:])
    store(ONEB[:], cd["ONESF"][:])
    store(IND8B[:], cd["IND8"][:])
    load(x_hbm[:], x_in[:])
    sy.wait_ge(ld.h, ld.n)
    g.wait_ge(st.h, st.n)
    p.bar()

    VT_MU, VT_VAR, VT_RHO, VT_SC, VT_T1, VT_T2, VT_T3, VT_T4 = range(8)
    VT_X1, VT_X2, VT_X3 = 8, 9, 10


    def bcast_scalar(dst_col, src11):
        te.wait_ge(sec.h, sec.n)
        mm = te.matmul(PS1[3][:, 0:1], ONE[0:1, :], src11, start=True, stop=True)
        tv = tec.inc(mm)
        sc.wait_ge(tec.h, tv)
        sec.inc(sc.copy(dst_col, PS1[3][:, 0:1]))

    def rsqrt_ops(dst, src, eps, tmp1, tmp2, newton=True):
        v.tensor_scalar(tmp1, src, float(eps), None, ALU.add)
        nv = vec.inc(v.tensor_copy(tmp2, tmp1))
        sc.wait_ge(vec.h, nv)
        sc.activation(tmp2, tmp2, AF.Ln)
        ns = sec.inc(sc.activation(dst, tmp2, AF.Exp, scale=-0.5))
        v.wait_ge(sec.h, ns)
        if not newton:
            return v.engine_nop()
        v.tensor_mul(tmp2, dst, dst)
        v.tensor_mul(tmp2, tmp2, tmp1)
        v.tensor_scalar(tmp2, tmp2, -0.5, 1.5, ALU.mult, ALU.add)
        return v.tensor_mul(dst, dst, tmp2)

    vrb = p.ctr("vrb")

    def row_to_vt(slot, row_ap, lane=0):
        g.wait_ge(ld.h, ld.n)
        n = vrb.inc(g.dma_start(vrow_hbm[lane:lane + 1, :], row_ap), 16)
        sy.wait_ge(vrb.h, n)
        return load(vt(slot),
                    vrow_hbm[lane:lane + 1, :].rearrange(
                        "o (pp tc) -> (o pp) tc", tc=NTC))

    def rows_to_vt2(items):
        g.wait_ge(ld.h, ld.n)
        n = 0
        for slot_, row_ap, lane in items:
            n = vrb.inc(g.dma_start(vrow_hbm[lane:lane + 1, :], row_ap), 16)
        sy.wait_ge(vrb.h, n)
        for slot_, row_ap, lane in items:
            load(vt(slot_), vrow_hbm[lane:lane + 1, :].rearrange(
                "o (pp tc) -> (o pp) tc", tc=NTC))

    def vt_to_rows2(items):
        g.wait_ge(ld.h, ld.n)
        n = 0
        for row_ap, slot_, lane in items:
            n = vrb.inc(g.dma_start(vrow_hbm[lane:lane + 1, :].rearrange(
                "o (pp tc) -> (o pp) tc", tc=NTC), vt(slot_)), 16)
        sy.wait_ge(vrb.h, n)
        for row_ap, slot_, lane in items:
            load(row_ap, vrow_hbm[lane:lane + 1, :])

    def vt_to_row(row_ap, slot, lane=0):
        g.wait_ge(ld.h, ld.n)
        n = vrb.inc(g.dma_start(vrow_hbm[lane:lane + 1, :].rearrange(
            "o (pp tc) -> (o pp) tc", tc=NTC), vt(slot)), 16)
        sy.wait_ge(vrb.h, n)
        return load(row_ap, vrow_hbm[lane:lane + 1, :])

    def x_stats(src, alloc, rows_ready=False):
        if rows_ready:
            load(vt(VT_T1), vrow_hbm[0:1, :].rearrange(
                "o (pp tc) -> (o pp) tc", tc=NTC))
            load(vt(VT_T2), vrow_hbm[1:2, :].rearrange(
                "o (pp tc) -> (o pp) tc", tc=NTC))
            sy.wait_ge(ld.h, ld.n)
            p.bar()
            v.tensor_scalar(vt(VT_MU), vt(VT_T1), 1.0 / D, None, ALU.mult)
            v.tensor_scalar(vt(VT_T2), vt(VT_T2), 1.0 / D, None, ALU.mult)
            v.tensor_mul(vt(VT_T3), vt(VT_MU), vt(VT_MU))
            v.tensor_sub(vt(VT_VAR), vt(VT_T2), vt(VT_T3))
            rsqrt_ops(vt(VT_RHO), vt(VT_VAR), 1e-5, vt(VT_T1), vt(VT_T2))
            p.bar()
            return
        XT = [alloc((128, S), F32) for _ in range(2)]
        SQT = [alloc((128, B), F32) for _ in range(2)]
        SROW = alloc((1, S), F32)
        QROW = alloc((1, S), F32)
        xp, sqp = Pool(XT, p, "xr"), Pool(SQT)
        BKS = [PS1[0], PS1[1], PS2[0][:, 0:B], PS2[0][:, B:2 * B],
               PS2[1][:, 0:B], PS2[1][:, B:2 * B], PS1[2], PS1[3]]
        sq_mm = [0, 0]
        for c in range(CH):
            xb = xp.get(sy)
            n_ld = xb.ctr.inc(sy.dma_start(xb.ap[:], src[c * 128:(c + 1) * 128, :]), 16)
            sc.wait_ge(xb.ctr.h, n_ld)
            te.wait_ge(xb.ctr.h, n_ld)
            for b in range(NB):
                blk = slice(b * B, (b + 1) * B)
                sq = sqp.get(sc)
                sc.wait_ge(tec.h, sq_mm[b % 2])
                sv = sec.inc(sc.activation(sq.ap[:], xb.ap[:, blk], AF.Square))
                te.matmul(BKS[b][0:1, 0:B], ONE[:, 0:1], xb.ap[:, blk],
                          start=(c == 0), stop=(c == CH - 1))
                te.wait_ge(sec.h, sv)
                mm = te.matmul(BKS[b][32:33, 0:B], ONE[:, 0:1], sq.ap[:],
                               start=(c == 0), stop=(c == CH - 1))
                sq_mm[b % 2] = tec.inc(mm)
                if c == CH - 1:
                    v.wait_ge(tec.h, sq_mm[b % 2])
                    v.tensor_copy(SROW[:, blk], BKS[b][0:1, 0:B])
                    vec.inc(v.tensor_copy(QROW[:, blk], BKS[b][32:33, 0:B]))
            xb.release(tec, tec.n)
            xb.release(sec, sec.n)
        p.bar()
        rows_to_vt2([(VT_T1, SROW[:], 0), (VT_T2, QROW[:], 1)])
        sy.wait_ge(ld.h, ld.n)
        p.bar()
        v.tensor_scalar(vt(VT_MU), vt(VT_T1), 1.0 / D, None, ALU.mult)
        v.tensor_scalar(vt(VT_T2), vt(VT_T2), 1.0 / D, None, ALU.mult)
        v.tensor_mul(vt(VT_T3), vt(VT_MU), vt(VT_MU))
        v.tensor_sub(vt(VT_VAR), vt(VT_T2), vt(VT_T3))
        rsqrt_ops(vt(VT_RHO), vt(VT_VAR), 1e-5, vt(VT_T1), vt(VT_T2))
        p.bar()

    def build_norm_operand(YH, scale_slot, src, alloc, gcol=None):
        MUB = alloc((128, S), F32)
        SCB = alloc((128, S), F32)
        with p.phase() as alloc_r:
            MROW = alloc_r((1, S), F32)
            CROW = alloc_r((1, S), F32)
            vt_to_rows2([(MROW[:], VT_MU, 0), (CROW[:], scale_slot, 1)])
            sy.wait_ge(ld.h, ld.n)
            p.bar()
            bp = Pool([ps[:, 0:B] for ps in PS1])
            for b in range(NB):
                blk = slice(b * B, (b + 1) * B)
                bm = bp.get(te)
                bs = bp.get(te)
                te.matmul(bm.ap, ONE[0:1, :], MROW[:, blk], start=True, stop=True)
                mm = te.matmul(bs.ap, ONE[0:1, :], CROW[:, blk], start=True, stop=True)
                tv = tec.inc(mm)
                sc.wait_ge(tec.h, tv)
                sc.copy(MUB[:, blk], bm.ap)
                ss = sec.inc(sc.copy(SCB[:, blk], bs.ap))
                bm.release(sec, ss)
                bs.release(sec, ss)
            p.bar()
        XT = [alloc((128, S), F32) for _ in range(2)]
        STG = alloc((128, B), F32)
        STGP = alloc((128, B), F32)
        xp = Pool(XT, p, "xr")
        for c in range(CH):
            xb = xp.get(sy)
            n_ld = xb.ctr.inc(sy.dma_start(xb.ap[:], src[c * 128:(c + 1) * 128, :]), 16)
            e, scr = (v, STG) if c % 8 < 7 else (g, STGP)
            ec = vec if c % 8 < 7 else pec
            for b in range(NB):
                blk = slice(b * B, (b + 1) * B)
                e.wait_ge(xb.ctr.h, n_ld)
                e.tensor_sub(scr[:], xb.ap[:, blk], MUB[:, blk])
                if gcol is None:
                    ins = e.tensor_mul(YH[c][:, blk], scr[:], SCB[:, blk])
                else:
                    e.tensor_mul(scr[:], scr[:], SCB[:, blk])
                    ins = e.tensor_scalar(YH[c][:, blk], scr[:],
                                          gcol[:, c:c + 1], None, ALU.mult)
                ec.inc(ins)
            xb.release(ec, ec.n)
        p.bar()

    def prep_weight_cast(dst_bf, w_dram, col0, ncols):
        for c in range(CH):
            store(dst_bf[:, c * ncols:(c + 1) * ncols],
                  w_dram[c * 128:(c + 1) * 128, col0:col0 + ncols])
        return st.n

    def prep_weight_scaled(dst_bf, w_dram, col0, ncols, rowscale, wst_pool,
                           bias_rhs=None, bias_psum=None):
        if bias_rhs is not None:
            te.wait_ge(vec.h, vec.n)
        for c in range(CH):
            wb = wst_pool.get(sy)
            n_ld = wb.ctr.inc(sy.dma_start(
                wb.ap[:, 0:ncols],
                w_dram[c * 128:(c + 1) * 128, col0:col0 + ncols]), 16)
            if bias_rhs is not None:
                te.wait_ge(wb.ctr.h, n_ld)
                off = 0
                mm = None
                for psl in bias_psum:
                    w_ = min(512, ncols - off)
                    mm = te.matmul(psl[:, 0:w_], bias_rhs[:, c:c + 1],
                                   wb.ap[:, off:off + w_],
                                   start=(c == 0), stop=(c == CH - 1))
                    off += w_
                wb.release(tec, tec.inc(mm))
            v.wait_ge(wb.ctr.h, n_ld)
            ins = v.tensor_scalar(dst_bf[:, c * ncols:(c + 1) * ncols],
                                  wb.ap[:, 0:ncols], rowscale[:, c:c + 1],
                                  None, ALU.mult)
            wb.release(vec, vec.inc(ins))

    def attn_core(causal):
        with p.phase() as alloc:
            KNB = [alloc((65, S), BF16) for _ in range(2)]
            QNB = [alloc((65, S), BF16) for _ in range(2)]
            VBUF = [alloc((128, 65 * NTC), BF16) for _ in range(2)]
            PT = alloc((128, NTC * B), BF16)
            OH = [alloc((64, S), BF16) for _ in range(2)]
            RDNR = alloc((1, S), F32)
            RDNB2 = alloc((1, S), BF16)
            if causal:
                CMB2 = alloc((128, 4 * B), BF16)
                store(CMB2[:], cd["CMASK"][:])
            for t_ in KNB + QNB:
                g.memset(t_[64:65, :], 1.0)
            for t_ in VBUF:
                g.memset(t_[:], 0.0)
            g.wait_ge(st.h, st.n)
            p.bar()
            for t_ in VBUF:
                for j in range(NTC):
                    g.memset(t_[:, 65 * j + 64:65 * j + 65], 1.0)
            p.bar()
            knp = Pool(KNB, p, "knp")
            qnp = Pool(QNB, p, "qnp")
            vbp = Pool(VBUF, p, "vbp")
            ohp = Pool(OH, p, "ohp")
            SLOT3 = [PS2[0][:, 0:2 * B], PS2[1][:, 0:2 * B], PS2[2][:, 0:2 * B]]
            ACCF = [PS1[0], PS1[1]]
            st8 = {"evac": 0, "bcst": 0, "guard": [0, 0], "gg": 0,
                   "slot_guard": [0, 0, 0]}
            todo = []
            exp_val = {}

            def mk_recip(qsl, ev, nxt):
                def fn():
                    v.reciprocal(RDNR[0:1, qsl], RDNR[0:1, qsl])
                    rdnb = vec.inc(v.tensor_copy(RDNB2[0:1, qsl], RDNR[0:1, qsl]))
                    nxt(rdnb)
                return fn

            def mk_norm(ob, h, qblk, gready):
                def sched(rdnb):
                    def fn():
                        nql = slice(qblk * B, (qblk + 1) * B)
                        gnow = st8["gg"]
                        t_ = (gnow + 2) % 3
                        gl = next(gnow - d for d in range(3)
                                  if (gnow - d) % 3 == t_)
                        te.wait_ge(vec.h, max(rdnb, st8["bcst"]))
                        if gl in exp_val:
                            te.wait_ge(sec.h, exp_val[gl])
                        mm2 = te.matmul(SLOT3[t_][64:128, 0:B], ONEB[0:1, 0:64],
                                        RDNB2[0:1, nql], start=True, stop=True)
                        tv2 = tec.inc(mm2)
                        v.wait_ge(tec.h, tv2)
                        ins = v.tensor_mul(ob.ap[:, nql], ob.ap[:, nql],
                                           SLOT3[t_][64:128, 0:B])
                        st8["bcst"] = st8["slot_guard"][t_] = vec.inc(ins)
                        if qblk == NB - 1:
                            todo.append((gready + 2, mk_store(ob, h)))
                    todo.append((gready + 2, fn))
                return sched

            def mk_store(ob, h):
                def fn():
                    g.wait_ge(vec.h, st8["bcst"])
                    n_st = ob.ctr.inc(g.dma_start(
                        o_sp[h * 64:(h + 1) * 64, :], ob.ap[:]), 16)
                    ob.release(ob.ctr, n_st)
                return fn

            for h in range(HEADS):
                kb = knp.get(sy)
                nk = kb.ctr.inc(sy.dma_start(kb.ap[0:64, :], kn_sp[h, :, :]), 16)
                qb = qnp.get(sy)
                qb.ctr.inc(sy.dma_start(qb.ap[0:64, :], qn_sp[h, :, :]), 16)
                nq = qb.ctr.inc(sy.dma_start(qb.ap[64:65, :], cq_hbm[h:h + 1, :]), 16)
                vb = vbp.get(sy)
                nv3 = vb.ctr.inc(sy.dma_start(
                    vb.ap.rearrange("pp (tc cc) -> pp tc cc", cc=65)[:, :, 0:64],
                    v_sp[h, :, :, :].rearrange("tc pp dd -> pp tc dd")), 16)
                te.wait_ge(kb.ctr.h, nk)
                te.wait_ge(qb.ctr.h, nq)
                te.wait_ge(vb.ctr.h, nv3)
                ob = ohp.get(v)
                groups = []
                for qblk in range(NB):
                    ntc_v = 4 * (qblk + 1) if causal else NTC
                    for gi in range(ntc_v // 2):
                        groups.append((qblk, gi, ntc_v))
                ggs = []
                mask_done = {}
                maskw = {}

                def emit_av_pair(idx):
                    qblk, gi, ntc_v = groups[idx]
                    acc_i = qblk % 2
                    qsl = slice(qblk * B, (qblk + 1) * B)
                    gg = ggs[idx]
                    for k2 in range(2):
                        tc = 2 * gi + k2
                        if (qblk, tc) in mask_done:
                            te.wait_ge(pec.h, mask_done[(qblk, tc)])
                        else:
                            te.wait_ge(sec.h, exp_val[gg])
                        if tc == 0:
                            te.wait_ge(vec.h, st8["guard"][acc_i])
                        mm = te.matmul(ACCF[acc_i][0:65, 0:B],
                                       vb.ap[:, 65 * tc:65 * tc + 65],
                                       PT[:, tc * B:(tc + 1) * B],
                                       start=(tc == 0), stop=(tc == ntc_v - 1))
                        if tc == ntc_v - 1:
                            tv = tec.inc(mm)
                            v.wait_ge(tec.h, tv)
                            v.tensor_copy(ob.ap[:, qsl], ACCF[acc_i][0:64, 0:B])
                            ins = v.tensor_copy(RDNR[0:1, qsl],
                                                ACCF[acc_i][64:65, 0:B])
                            st8["evac"] = st8["guard"][acc_i] = vec.inc(ins)
                            gnow = st8["gg"]
                            todo.append((gnow + 2, mk_recip(
                                qsl, st8["evac"], mk_norm(ob, h, qblk, gnow + 2))))

                for idx, (qblk, gi, ntc_v) in enumerate(groups):
                    gg = st8["gg"]
                    st8["gg"] += 1
                    ggs.append(gg)
                    slot = SLOT3[gg % 3]
                    if gg >= 3:
                        te.wait_ge(sec.h, exp_val[gg - 3])
                    sg = st8["slot_guard"][gg % 3]
                    if sg:
                        te.wait_ge(vec.h, sg)
                        st8["slot_guard"][gg % 3] = 0
                    qsl = slice(qblk * B, (qblk + 1) * B)
                    for k2 in range(2):
                        tc = 2 * gi + k2
                        mm = te.matmul(slot[:, B * k2:B * (k2 + 1)],
                                       kb.ap[:, tc * 128:(tc + 1) * 128],
                                       qb.ap[:, qsl], start=True, stop=True)
                    tv = tec.inc(mm)
                    mw = max(maskw.get(2 * gi, 0), maskw.get(2 * gi + 1, 0))
                    if mw:
                        sc.wait_ge(pec.h, mw)
                    sc.wait_ge(tec.h, tv)
                    ins = sc.activation(PT[:, 2 * gi * B:(2 * gi + 2) * B],
                                        slot[:, 0:2 * B], AF.Exp)
                    exp_val[gg] = sec.inc(ins)
                    if causal:
                        for k2 in range(2):
                            tc = 2 * gi + k2
                            jm = tc - 4 * qblk
                            if jm >= 0:
                                g.wait_ge(sec.h, exp_val[gg])
                                insm = g.tensor_mul(
                                    PT[:, tc * B:(tc + 1) * B],
                                    PT[:, tc * B:(tc + 1) * B],
                                    CMB2[:, jm * B:(jm + 1) * B])
                                mask_done[(qblk, tc)] = maskw[tc] = pec.inc(insm)
                    if idx >= 2:
                        emit_av_pair(idx - 2)
                    while todo and todo[0][0] <= st8["gg"]:
                        todo.pop(0)[1]()
                emit_av_pair(len(groups) - 2)
                emit_av_pair(len(groups) - 1)
                kb.release(tec, tec.n)
                qb.release(tec, tec.n)
                vb.release(tec, tec.n)
            while todo:
                todo.pop(0)[1]()
            for b_ in ohp.bufs:
                g.wait_ge(b_.ctr.h, b_.ctr.n)
            p.bar()


    def attn_a(is_last):
        with p.phase() as alloc:
            x_stats(x_hbm, alloc)
        if debug_dump == "vrow_hbm":
            n = vrb.inc(g.dma_start(vrow_hbm[13:15, :], vrow_hbm[0:2, :]), 16)
            g.wait_ge(vrb.h, n)
            p.bar()
        v.tensor_mul(vt(VT_T1), vt(VT_RHO), vt(VT_RHO))
        v.tensor_mul(vt(VT_T1), vt(VT_T1), vt(VT_VAR))
        rsqrt_ops(vt(VT_T2), vt(VT_T1), 1e-5, vt(VT_T3), vt(VT_T4))
        v.tensor_mul(vt(VT_SC), vt(VT_RHO), vt(VT_T2))
        v.reciprocal(vt(VT_X1), vt(VT_T2))
        p.bar()
        if debug_dump == "vrow_hbm":
            for lane_, slot_ in ((8, VT_MU), (9, VT_VAR), (10, VT_RHO), (11, VT_SC)):
                g.wait_ge(ld.h, ld.n)
                nn = vrb.inc(g.dma_start(vrow_hbm[lane_:lane_ + 1, :].rearrange(
                    "o (pp tc) -> (o pp) tc", tc=NTC), vt(slot_)), 16)
            g.wait_ge(vrb.h, vrb.n)
            p.bar()

        with p.phase() as alloc_o:
            YH = [alloc_o((128, S), BF16) for _ in range(CH)]
            with p.phase() as alloc:
                build_norm_operand(YH, VT_SC, x_hbm, alloc)
            if debug_dump == "zf_hbm":
                for c in range(CH):
                    store(zf_hbm[c * 128:(c + 1) * 128, :], YH[c][:])
                g.wait_ge(st.h, st.n)
                p.bar()

            S4ALL = alloc_o((4, S), BF16)
            n = load(S11[0:1, 0:1], cd["ARB"][:])
            sy.wait_ge(ld.h, n)
            p.bar()
            bcast_scalar(SVEC[:, 0:1], S11[0:1, 0:1])
            with p.phase() as alloc:
                ARWS = alloc((128, CH), F32)
                ARWB = alloc((128, CH), BF16)
                PROW = alloc((1, S), F32)
                SROW = alloc((1, S), F32)
                n = load(ARWS[:], cd["ARW"][:])
                v.wait_ge(ld.h, n)
                v.tensor_copy(ARWB[:], ARWS[:])
                g.memset(PROW[:], 0.0)
                p.bar()
                bp = Pool([ps[0:1, 0:B] for ps in PS1])
                for c in range(CH):
                    for b in range(NB):
                        blk = slice(b * B, (b + 1) * B)
                        bank = bp.get(te)
                        mm = te.matmul(bank.ap, ARWB[:, c:c + 1], YH[c][:, blk],
                                       start=True, stop=True)
                        tv = tec.inc(mm)
                        v.wait_ge(tec.h, tv)
                        ins = v.tensor_add(PROW[:, blk], PROW[:, blk], bank.ap)
                        bank.release(vec, vec.inc(ins))
                p.bar()
                row_to_vt(VT_T1, PROW[:])
                sy.wait_ge(ld.h, ld.n)
                p.bar()
                v.tensor_mul(vt(VT_T1), vt(VT_T1), vt(VT_X1))
                v.tensor_scalar(vt(VT_T1), vt(VT_T1), SVEC[:, 0:1], None, ALU.add)
                v.tensor_scalar(vt(VT_T1), vt(VT_T1), 0.01, None, ALU.mult)
                v.tensor_scalar(vt(VT_T1), vt(VT_T1), 0.5, None, ALU.max)
                v.tensor_scalar(vt(VT_X2), vt(VT_T1), 2.0, None, ALU.min)
                p.bar()
                vt_to_row(SROW[:], VT_X2)
                sy.wait_ge(ld.h, ld.n)
                p.bar()
                for b in range(NB):
                    blk = slice(b * B, (b + 1) * B)
                    te.wait_ge(sec.h, sec.n)
                    mm = te.matmul(PS1[0][0:4, 0:B], ONE[0:1, 0:4], SROW[:, blk],
                                   start=True, stop=True)
                    tv = tec.inc(mm)
                    sc.wait_ge(tec.h, tv)
                    sec.inc(sc.copy(S4ALL[:, blk], PS1[0][0:4, 0:B]))
                p.bar()
            v.tensor_scalar(BIAS[:, 0:4], ALN[:, 0:4], 0.125, None, ALU.mult)
            g.memset(BIAS[0:4, 16:20], 0.0)
            p.bar()

            def head_groups(is_q):
                wsrc = cd["AQW"] if is_q else cd["AKVW"]
                nw = cd["AQNW"] if is_q else cd["AKVNW"]
                nb_ = cd["AQNB"] if is_q else cd["AKVNB"]
                br = cd["AQBR"] if is_q else cd["AKVBR"]
                spill = qn_sp if is_q else kn_sp
                lnsrc = BIAS if is_q else ALN
                with p.phase() as alloc:
                    NWS = alloc((128, CH), F32)
                    NBS = alloc((128, CH), F32)
                    BRS = [alloc((1, 256), F32) for _ in range(2)]
                    TCOS = alloc((128, S), BF16)
                    TSIN = alloc((128, S), BF16)
                    RA = alloc((128, S), BF16)
                    RB = alloc((128, S), BF16)
                    STG = [alloc((128, 2 * B), BF16) for _ in range(2)]
                    SQP = [alloc((128, B), BF16) for _ in range(2)]
                    SQ2P = [alloc((128, B), BF16) for _ in range(2)]
                    WQG = [alloc((128, 2048), BF16) for _ in range(2)]
                    WST = [alloc((128, 256), F32) for _ in range(2)]
                    QNE = alloc((128, S), BF16)
                    QNO = alloc((128, S), BF16)
                    BCS = [alloc((128, 2 * B), BF16) for _ in range(2)]
                    M0R = [alloc((4, S), BF16) for _ in range(2)]
                    GR = [alloc((4, S), BF16) for _ in range(2)]
                    CR = alloc((4, S), BF16)
                    SB = alloc((4, 2 * B), F32)

                    def sbt(par):
                        return SB[:, par * B:(par + 1) * B]

                    load(NWS[:], nw[:])
                    load(NBS[:], nb_[:])
                    store(TCOS[:], cd["COSA"][:])
                    store(TSIN[:], cd["SINA"][:])
                    sy.wait_ge(ld.h, ld.n)
                    g.wait_ge(st.h, st.n)
                    p.bar()
                    PJ = [PS2[1][:, 0:B], PS2[1][:, B:2 * B]]
                    P0 = [PS2[0][:, 0:B], PS2[0][:, B:2 * B]]
                    BC0 = [PS1[0][:, 0:B], PS1[1][:, 0:B]]
                    BC1 = [PS1[2][:, 0:B], PS1[3][:, 0:B]]
                    pjp = Pool(PJ)
                    gd = {"bias_ld": 0, "spill_st": 0, "cq_st": 0, "cr_free": 0,
                          "apply": [0] * NB, "applyp": [0] * NB,
                          "rot": [0] * NB, "rotp": [0] * NB,
                          "proj": [0, 0], "bc": [0, 0], "bcp": [0, 0],
                          "bcv": [0, 0], "sq_read": [0, 0],
                          "p0v": [0, 0], "p0c": [0, 0],
                          "gr_done": [0, 0], "m0_done": [0, 0],
                          "gr_read": [0, 0]}

                    def prep_chunk(gi, c):
                        par = gi % 2
                        wb = wstp.get(sy)
                        n_ld = wb.ctr.inc(sy.dma_start(
                            wb.ap[:, 0:256],
                            wsrc[c * 128:(c + 1) * 128,
                                 256 * gi:256 * gi + 256]), 16)
                        if c == 0:
                            n_br = load(BRS[par][:],
                                        br[:, 256 * gi:256 * gi + 256])
                            gd["brld"] = n_br
                        bank = pjp.get(te)
                        te.wait_ge(wb.ctr.h, n_ld)
                        mm = te.matmul(bank.ap[0:1, 0:256], NBS[:, c:c + 1],
                                       wb.ap[:, 0:256], start=True, stop=True)
                        tv = tec.inc(mm)
                        v.wait_ge(tec.h, tv)
                        if c == 0:
                            v.wait_ge(ld.h, gd["brld"])
                        ins = v.tensor_add(BRS[par][:], BRS[par][:],
                                           bank.ap[0:1, 0:256])
                        bank.release(vec, vec.inc(ins))
                        v.wait_ge(wb.ctr.h, n_ld)
                        if c == 0:
                            v.wait_ge(tec.h, gd["proj"][par])
                        ins = v.tensor_scalar(
                            WQG[par][:, c * 256:(c + 1) * 256],
                            wb.ap[:, 0:256], NWS[:, c:c + 1], None, ALU.mult)
                        wb.release(vec, vec.inc(ins))

                    def prep_finish(gi):
                        par = gi % 2
                        g.wait_ge(vec.h, vec.n)
                        lane = 2 + par
                        n1 = vrb.inc(g.dma_start(
                            vrow_hbm[lane:lane + 1, 0:256], BRS[par][:]), 16)
                        sy.wait_ge(vrb.h, n1)
                        load(BIAS[:, 8 + 2 * par:9 + 2 * par],
                             vrow_hbm[lane:lane + 1, 0:128].rearrange("a b -> b a"))
                        gd["bias_ld"] = load(
                            BIAS[:, 9 + 2 * par:10 + 2 * par],
                            vrow_hbm[lane:lane + 1, 128:256].rearrange("a b -> b a"))

                    def proj_block(gi, b):
                        par = gi % 2
                        blk = slice(b * B, (b + 1) * B)
                        for half, dst in ((0, RA), (1, RB)):
                            bank = pjp.get(te)
                            for c in range(CH):
                                mm = te.matmul(
                                    bank.ap,
                                    WQG[par][:, c * 256 + 128 * half:
                                              c * 256 + 128 * (half + 1)],
                                    YH[c][:, blk], start=(c == 0),
                                    stop=(c == CH - 1))
                            tv = tec.inc(mm)
                            if b == 0 and half == 0:
                                sc.wait_ge(ld.h, gd["bias_ld"])
                            sc.wait_ge(tec.h, tv)
                            if half == 0:
                                sc.wait_ge(vec.h, gd["apply"][b])
                                sc.wait_ge(pec.h, gd["applyp"][b])
                            ins = sc.activation(
                                dst[:, blk], bank.ap, AF.Identity,
                                bias=BIAS[:, 8 + 2 * par + half:
                                          9 + 2 * par + half])
                            bank.release(sec, sec.inc(ins))
                        gd["proj"][par] = tec.n
                        return sec.n

                    def rotary_block(b, bias_done):
                        par = b % 2
                        blk = slice(b * B, (b + 1) * B)
                        v.wait_ge(sec.h, bias_done)
                        v.wait_ge(tec.h, gd["sq_read"][par])
                        v.tensor_mul(SQP[par][:], RA[:, blk], TCOS[:, blk])
                        v.tensor_mul(SQ2P[par][:], RB[:, blk], TSIN[:, blk])
                        g.wait_ge(sec.h, bias_done)
                        g.wait_ge(vec.h, gd["apply"][b])
                        g.tensor_mul(STG[par][:, 0:B], RA[:, blk], TSIN[:, blk])
                        ins = g.tensor_mul(STG[par][:, B:2 * B], RB[:, blk],
                                           TCOS[:, blk])
                        gr = pec.inc(ins)
                        v.wait_ge(pec.h, gr)
                        ins = v.tensor_sub(RA[:, blk], SQP[par][:], SQ2P[par][:])
                        gd["rot"][b] = vec.inc(ins)
                        g.wait_ge(vec.h, gd["rot"][b])
                        ins = g.tensor_add(RB[:, blk], STG[par][:, 0:B],
                                           STG[par][:, B:2 * B])
                        gd["rotp"][b] = pec.inc(ins)

                    def stats_a(gi, b):
                        par = b % 2
                        gpar = gi % 2
                        blk = slice(b * B, (b + 1) * B)
                        sc.wait_ge(vec.h, gd["rot"][b])
                        sc.wait_ge(pec.h, gd["rotp"][b])
                        sc.wait_ge(tec.h, gd["sq_read"][par])
                        sc.activation(SQP[par][:], RA[:, blk], AF.Square)
                        s2 = sec.inc(sc.activation(SQ2P[par][:], RB[:, blk],
                                                   AF.Square))
                        te.wait_ge(sec.h, max(s2, gd["p0c"][par]))
                        te.wait_ge(vec.h, gd["p0v"][par])
                        te.matmul(P0[par][0:4, 0:B], IND4MB[:], RA[:, blk],
                                  start=True, stop=False)
                        te.matmul(P0[par][0:4, 0:B], IND4MB[:], RB[:, blk],
                                  start=False, stop=True)
                        te.matmul(P0[par][32:36, 0:B], IND4MB[:], SQP[par][:],
                                  start=True, stop=False)
                        mm = te.matmul(P0[par][32:36, 0:B], IND4MB[:],
                                       SQ2P[par][:], start=False, stop=True)
                        tv = tec.inc(mm)
                        gd["sq_read"][par] = tv
                        sc.wait_ge(tec.h, max(tv, gd["gr_read"][gpar]))
                        m0c = sec.inc(sc.copy(M0R[gpar][:, blk],
                                              P0[par][0:4, 0:B]))
                        gd["m0_done"][gpar] = m0c
                        v.wait_ge(sec.h, m0c)
                        v.wait_ge(tec.h, gd["gr_read"][gpar])
                        v.tensor_mul(sbt(par), M0R[gpar][:, blk],
                                     M0R[gpar][:, blk])
                        v.tensor_sub(sbt(par), P0[par][32:36, 0:B], sbt(par))
                        v.tensor_mul(sbt(par), sbt(par), S4ALL[:, blk])
                        v.tensor_mul(sbt(par), sbt(par), S4ALL[:, blk])
                        ins = v.tensor_scalar(GR[gpar][:, blk], sbt(par), 1e-5,
                                              None, ALU.add)
                        gd["p0v"][par] = vec.inc(ins)

                    def derive_mid(gi):
                        gpar = gi % 2
                        sc.wait_ge(vec.h, gd["p0v"][0])
                        sc.wait_ge(vec.h, gd["p0v"][1])
                        sc.activation(GR[gpar][:], GR[gpar][:], AF.Ln)
                        lg = sec.inc(sc.activation(GR[gpar][:], GR[gpar][:],
                                                   AF.Exp, scale=-0.5))
                        v.wait_ge(sec.h, lg)
                        gd["gr_done"][gpar] = vec.inc(
                            v.tensor_mul(GR[gpar][:], GR[gpar][:], S4ALL[:]))

                    def apply_block(gi, b):
                        par = b % 2
                        gpar = gi % 2
                        blk = slice(b * B, (b + 1) * B)
                        te.wait_ge(sec.h, gd["bc"][par])
                        if b == 0:
                            te.wait_ge(vec.h, gd["gr_done"][gpar])
                            te.wait_ge(sec.h, gd["m0_done"][gpar])
                        te.matmul(BC0[par], IND4TB[:], M0R[gpar][:, blk],
                                  start=True, stop=True)
                        mm = te.matmul(BC1[par], IND4TB[:], GR[gpar][:, blk],
                                       start=True, stop=True)
                        tv = tec.inc(mm)
                        sc.wait_ge(tec.h, tv)
                        sc.wait_ge(vec.h, gd["bcv"][par])
                        sc.wait_ge(pec.h, gd["bcp"][par])
                        sc.copy(BCS[par][:, 0:B], BC0[par])
                        cpd = sec.inc(sc.copy(BCS[par][:, B:2 * B], BC1[par]))
                        gd["bc"][par] = cpd
                        v.wait_ge(sec.h, cpd)
                        v.wait_ge(st.h, gd["spill_st"])
                        v.wait_ge(pec.h, gd["rotp"][b])
                        v.tensor_sub(STG[par][:, 0:B], RA[:, blk],
                                     BCS[par][:, 0:B])
                        v.tensor_mul(STG[par][:, 0:B], STG[par][:, 0:B],
                                     BCS[par][:, B:2 * B])
                        ins = v.tensor_scalar(QNE[:, blk], STG[par][:, 0:B],
                                              lnsrc[:, 0:1], lnsrc[:, 1:2],
                                              ALU.mult, ALU.add)
                        gd["apply"][b] = gd["bcv"][par] = vec.inc(ins)
                        g.wait_ge(sec.h, cpd)
                        g.wait_ge(st.h, gd["spill_st"])
                        g.tensor_sub(STG[par][:, B:2 * B], RB[:, blk],
                                     BCS[par][:, 0:B])
                        g.tensor_mul(STG[par][:, B:2 * B], STG[par][:, B:2 * B],
                                     BCS[par][:, B:2 * B])
                        ins = g.tensor_scalar(QNO[:, blk], STG[par][:, B:2 * B],
                                              lnsrc[:, 2:3], lnsrc[:, 3:4],
                                              ALU.mult, ALU.add)
                        gd["applyp"][b] = gd["bcp"][par] = pec.inc(ins)
                        sc.wait_ge(vec.h, gd["apply"][b])
                        sc.wait_ge(pec.h, gd["applyp"][b])
                        sc.wait_ge(tec.h, gd["sq_read"][par])
                        sc.activation(SQP[par][:], QNE[:, blk], AF.Square)
                        s2 = sec.inc(sc.activation(SQ2P[par][:], QNO[:, blk],
                                                   AF.Square))
                        te.wait_ge(sec.h, max(s2, gd["p0c"][par]))
                        te.matmul(P0[par][64:68, 0:B], IND4B[:], SQP[par][:],
                                  start=True, stop=False)
                        mm = te.matmul(P0[par][64:68, 0:B], IND4B[:],
                                       SQ2P[par][:], start=False, stop=True)
                        tv = tec.inc(mm)
                        gd["sq_read"][par] = tv
                        sc.wait_ge(tec.h, tv)
                        sc.wait_ge(st.h, gd["cq_st"])
                        sc.wait_ge(vec.h, gd["cr_free"])
                        gd["p0c"][par] = sec.inc(
                            sc.copy(CR[:, blk], P0[par][64:68, 0:B]))

                    def d_tail(gi):
                        if is_q:
                            sq = sec.inc(sc.activation(CR[:], CR[:], AF.Sqrt))
                            v.wait_ge(sec.h, sq)
                            ins = v.tensor_scalar(CR[:], CR[:],
                                                  SVEC[0:4, 4 + gi:5 + gi],
                                                  None, ALU.mult)
                            gd["cr_free"] = vec.inc(ins)
                            g.wait_ge(vec.h, gd["cr_free"])
                            for h4 in range(4):
                                store(cq_hbm[4 * gi + h4:4 * gi + h4 + 1, :],
                                      CR[h4:h4 + 1, :])
                            gd["cq_st"] = st.n
                        else:
                            v.wait_ge(sec.h, max(gd["p0c"]))
                            v.tensor_reduce(sbt(0)[:, 0:1], CR[:], AX.X,
                                            ALU.max)
                            ins = v.tensor_max(BIAS[0:4, 16 + gi:17 + gi],
                                               BIAS[0:4, 16 + gi:17 + gi],
                                               sbt(0)[:, 0:1])
                            gd["cr_free"] = vec.inc(ins)
                        g.wait_ge(vec.h, gd["apply"][NB - 1])
                        g.wait_ge(pec.h, gd["applyp"][NB - 1])
                        for h4 in range(4):
                            rs = slice(32 * h4, 32 * (h4 + 1))
                            store(spill[4 * gi + h4, 0:32, :], QNE[rs, :])
                            store(spill[4 * gi + h4, 32:64, :], QNO[rs, :])
                        gd["spill_st"] = st.n

                    wstp = Pool(WST, p, "wst")
                    for c in range(CH):
                        prep_chunk(0, c)
                    prep_finish(0)
                    for b in range(NB):
                        bias_done = proj_block(0, b)
                        rotary_block(b, bias_done)
                        if NG > 1:
                            prep_chunk(1, b)
                        if b >= 1:
                            stats_a(0, b - 1)
                    stats_a(0, NB - 1)
                    derive_mid(0)
                    if NG > 1:
                        prep_finish(1)
                    for gi in range(NG):
                        for b in range(NB):
                            apply_block(gi, b)
                            if gi + 1 < NG:
                                bias_done = proj_block(gi + 1, b)
                                rotary_block(b, bias_done)
                                if gi + 2 < NG:
                                    prep_chunk(gi + 2, b)
                                if b >= 1:
                                    stats_a(gi + 1, b - 1)
                        gd["gr_read"][gi % 2] = tec.n
                        if gi + 1 < NG:
                            stats_a(gi + 1, NB - 1)
                            derive_mid(gi + 1)
                            if gi + 2 < NG:
                                prep_finish(gi + 2)
                        d_tail(gi)
                    g.wait_ge(st.h, st.n)
                    p.bar()
                    if not is_q:
                        ss = sec.inc(sc.activation(BIAS[0:4, 20:24],
                                                   BIAS[0:4, 16:20], AF.Sqrt))
                        v.wait_ge(sec.h, ss)
                        v.tensor_scalar(SVEC[0:4, 4:8], BIAS[0:4, 20:24],
                                        -1.0, None, ALU.mult)
                        p.bar()

            head_groups(is_q=False)
            head_groups(is_q=True)

            with p.phase() as alloc:
                WVB = alloc((128, CH * 1024), BF16)
                WST = [alloc((128, 1024), F32) for _ in range(2)]
                NWS = alloc((128, CH), F32)
                NBS = alloc((128, CH), F32)
                KVBR = alloc((1, 1024), F32)
                BVB = alloc((1, 1024), BF16)
                VST = [alloc((128, 512), BF16) for _ in range(2)]
                load(NWS[:], cd["AKVNW"][:])
                load(NBS[:], cd["AKVNB"][:])
                n3 = load(KVBR[:], cd["AKVBR"][:, 1024:2048])
                sy.wait_ge(ld.h, ld.n)
                p.bar()
                wstp = Pool(WST, p, "wst")
                prep_weight_scaled(WVB, cd["AKVW"], 1024, 1024, NWS[:], wstp,
                                   bias_rhs=NBS[:],
                                   bias_psum=[PS2[1][0:1, 0:512], PS2[1][0:1, 512:1024]])
                tv = tec.n
                v.wait_ge(tec.h, tv)
                v.tensor_add(KVBR[:], KVBR[:], PS2[1][0:1, 0:1024])
                nv = vec.inc(v.tensor_copy(BVB[:], KVBR[:]))
                p.bar()
                vp = Pool(VST, p, "vst")
                pp = Pool([ps[:, 0:512] for ps in PS1[0:3]])
                for tc in range(NTC):
                    tsl = slice(tc * 128, (tc + 1) * 128)
                    for vb2 in range(2):
                        bank = pp.get(te)
                        for c in range(CH):
                            te.matmul(bank.ap, YH[c][:, tsl],
                                      WVB[:, c * 1024 + 512 * vb2:c * 1024 + 512 * (vb2 + 1)],
                                      start=(c == 0), stop=False)
                        mm = te.matmul(bank.ap, ONEB[0:1, :],
                                       BVB[:, 512 * vb2:512 * (vb2 + 1)],
                                       start=False, stop=True)
                        tv = tec.inc(mm)
                        vbuf = vp.get(sc)
                        sc.wait_ge(tec.h, tv)
                        sv = sec.inc(sc.copy(vbuf.ap[:], bank.ap))
                        bank.release(sec, sv)
                        g.wait_ge(sec.h, sv)
                        n_st = vbuf.ctr.inc(g.dma_start(
                            v_sp[8 * vb2:8 * (vb2 + 1), tc:tc + 1, :, :].rearrange(
                                "h o pp dd -> (o pp) h dd"),
                            vbuf.ap.rearrange("pp (h dd) -> pp h dd", h=8)), 16)
                        vbuf.release(vbuf.ctr, n_st)
                g.wait_ge(st.h, st.n)
                p.bar()

        attn_core(causal=False)

        with p.phase() as alloc:
            OALL = [alloc((128, S), BF16) for _ in range(CH)]
            WOB = [alloc((128, CH * 128), BF16) for _ in range(2)]
            XT = [alloc((128, S), F32) for _ in range(2)]
            XO = [alloc((128, S), F32) for _ in range(2)]
            OBS = alloc((128, CH), F32)
            load(OBS[:], cd["AOB"][:])
            for c in range(CH):
                load(OALL[c][:], o_sp[c * 128:(c + 1) * 128, :])
            sy.wait_ge(ld.h, ld.n)
            p.bar()
            SQF = [alloc((128, B), F32) for _ in range(2)]
            SROWX = alloc((1, S), F32)
            QROWX = alloc((1, S), F32)
            xp = Pool(XT, p, "xr")
            xop = Pool(XO, p, "xo")
            wcc = p.ctr("wcc")
            pp = Pool([ps[:, 0:B] for ps in PS1[0:3]])
            SBK = [PS2[0][:, 0:B], PS2[0][:, B:2 * B]]
            sg_read = [0, 0]
            sq_rd = [0, 0]
            pend_stat = []

            def do_fold_stats(mch, xo, item):
                b, nv = item
                blk = slice(b * B, (b + 1) * B)
                bpar = b % 2
                sc.wait_ge(vec.h, nv)
                sc.wait_ge(tec.h, sq_rd[bpar])
                sq2 = sec.inc(sc.activation(SQF[bpar][:], xo.ap[:, blk],
                                            AF.Square))
                te.wait_ge(sec.h, sq2)
                te.wait_ge(vec.h, sg_read[bpar])
                te.matmul(SBK[bpar][0:1, 0:B], ONE[:, 0:1], xo.ap[:, blk],
                          start=True, stop=True)
                mm2 = te.matmul(SBK[bpar][32:33, 0:B], ONE[:, 0:1],
                                SQF[bpar][:], start=True, stop=True)
                sq_rd[bpar] = tv2 = tec.inc(mm2)
                v.wait_ge(tec.h, tv2)
                if mch == 0:
                    v.tensor_copy(SROWX[:, blk], SBK[bpar][0:1, 0:B])
                    ins = v.tensor_copy(QROWX[:, blk], SBK[bpar][32:33, 0:B])
                else:
                    v.tensor_add(SROWX[:, blk], SROWX[:, blk],
                                 SBK[bpar][0:1, 0:B])
                    ins = v.tensor_add(QROWX[:, blk], QROWX[:, blk],
                                       SBK[bpar][32:33, 0:B])
                sg_read[bpar] = vec.inc(ins)
            tgt = out_d if is_last else x_hbm
            wob_guard = [0, 0]
            for mch in range(CH):
                par = mch % 2
                g.wait_ge(tec.h, wob_guard[par])
                for c in range(CH):
                    wcc.inc(g.dma_start(
                        WOB[par][:, c * 128:(c + 1) * 128],
                        cd["AOW"][c * 128:(c + 1) * 128, mch * 128:(mch + 1) * 128]), 16)
                nw = wcc.n
                xb = xp.get(sy)
                n_ld = xb.ctr.inc(sy.dma_start(
                    xb.ap[:], x_hbm[mch * 128:(mch + 1) * 128, :]), 16)
                xo = xop.get(v)
                for b in range(NB):
                    blk = slice(b * B, (b + 1) * B)
                    bank = pp.get(te)
                    te.wait_ge(wcc.h, nw)
                    for c in range(CH):
                        mm = te.matmul(bank.ap, WOB[par][:, c * 128:(c + 1) * 128],
                                       OALL[c][:, blk], start=(c == 0),
                                       stop=(c == CH - 1))
                    tv = tec.inc(mm)
                    v.wait_ge(tec.h, tv)
                    v.wait_ge(xb.ctr.h, n_ld)
                    ins = v.scalar_tensor_tensor(xo.ap[:, blk], bank.ap,
                                                 OBS[:, mch:mch + 1], xb.ap[:, blk],
                                                 ALU.add, ALU.add)
                    nv = vec.inc(ins)
                    bank.release(vec, nv)
                    pend_stat.append((b, nv))
                    if len(pend_stat) > 1:
                        do_fold_stats(mch, xo, pend_stat.pop(0))
                while pend_stat:
                    do_fold_stats(mch, xo, pend_stat.pop(0))
                wob_guard[par] = tec.n
                xb.release(vec, vec.n)
                g.wait_ge(vec.h, vec.n)
                n_st = xo.ctr.inc(g.dma_start(
                    tgt[mch * 128:(mch + 1) * 128, :], xo.ap[:]), 16)
                xo.release(xo.ctr, n_st)
            if not is_last:
                g.wait_ge(vec.h, vec.n)
                vrb.inc(g.dma_start(vrow_hbm[0:1, :], SROWX[:]), 16)
                nvr = vrb.inc(g.dma_start(vrow_hbm[1:2, :], QROWX[:]), 16)
                g.wait_ge(vrb.h, nvr)
            for b_ in xop.bufs:
                g.wait_ge(b_.ctr.h, b_.ctr.n)
            p.bar()


    def ffn_layer(is_last):
        with p.phase() as alloc:
            x_stats(x_hbm, alloc, rows_ready=True)
        v.tensor_mul(vt(VT_T1), vt(VT_RHO), vt(VT_RHO))
        v.tensor_mul(vt(VT_T1), vt(VT_T1), vt(VT_VAR))
        rsqrt_ops(vt(VT_T2), vt(VT_T1), 1e-6, vt(VT_T3), vt(VT_T4))
        v.tensor_mul(vt(VT_SC), vt(VT_RHO), vt(VT_T2))
        p.bar()

        with p.phase() as alloc_m, contextlib.ExitStack() as kv_stack:
            CETF = alloc_m((8, S), BF16)
            alloc_kv = kv_stack.enter_context(p.phase())
            KCB = [alloc_kv((128, S), BF16) for _ in range(2)]
            VCT = alloc_kv((128, 256 * NTC), BF16)

            with p.phase() as alloc:
                YH = [alloc((128, S), BF16) for _ in range(CH)]
                G1S = alloc((128, CH), F32)
                n1 = load(G1S[:], cd["FG1"][:])
                sy.wait_ge(ld.h, n1)
                with p.phase() as alloc_n:
                    build_norm_operand(YH, VT_SC, x_hbm, alloc_n, gcol=G1S[:])
                WQB = alloc((128, CH * 1024), BF16)
                WKB = alloc((128, CH * 256), BF16)
                WVB = alloc((128, CH * 256), BF16)
                QST = [alloc((128, S), BF16) for _ in range(2)]
                SQ = alloc((128, B), F32)
                CQR = alloc((1, S), F32)
                prep_weight_cast(WQB, cd["FWQ"], 0, 1024)
                prep_weight_cast(WKB, cd["FWK"], 0, 256)
                prep_weight_cast(WVB, cd["FWV"], 0, 256)
                g.wait_ge(st.h, st.n)
                p.bar()
                pp = Pool([ps[:, 0:B] for ps in PS1[0:3]])
                for mch in range(2):
                    for b in range(NB):
                        blk = slice(b * B, (b + 1) * B)
                        bank = pp.get(te)
                        for c in range(CH):
                            mm = te.matmul(bank.ap,
                                           WKB[:, c * 256 + 128 * mch:c * 256 + 128 * (mch + 1)],
                                           YH[c][:, blk], start=(c == 0), stop=(c == CH - 1))
                        tv = tec.inc(mm)
                        sc.wait_ge(tec.h, tv)
                        ins = sc.copy(KCB[mch][:, blk], bank.ap)
                        bank.release(sec, sec.inc(ins))
                p.bar()
                for b in range(NB):
                    blk = slice(b * B, (b + 1) * B)
                    te.wait_ge(vec.h, vec.n)
                    for mch in range(2):
                        sv = sec.inc(sc.activation(SQ[:], KCB[mch][:, blk], AF.Square))
                        te.wait_ge(sec.h, sv)
                        mm = te.matmul(PS1[3][0:1, 0:B], ONE[:, 0:1], SQ[:],
                                       start=(mch == 0), stop=(mch == 1))
                    tv = tec.inc(mm)
                    v.wait_ge(tec.h, tv)
                    vec.inc(v.tensor_copy(CQR[:, blk], PS1[3][0:1, 0:B]))
                p.bar()
                v.tensor_reduce(S11[0:1, 1:2], CQR[:], AX.X, ALU.max)
                nv = vec.inc(v.engine_nop())
                sc.wait_ge(vec.h, nv)
                ss = sec.inc(sc.activation(S11[0:1, 2:3], S11[0:1, 1:2], AF.Sqrt))
                v.wait_ge(sec.h, ss)
                v.tensor_scalar(S11[0:1, 3:4], S11[0:1, 2:3], -1.0, None, ALU.mult)
                p.bar()
                bcast_scalar(SVEC[:, 1:2], S11[0:1, 3:4])
                p.bar()
                qsp = Pool(QST, p, "vst")
                for h in range(4):
                    g.wait_ge(vec.h, vec.n)
                    g.wait_ge(st.h, st.n)
                    gm = pec.inc(g.memset(CQR[:], 0.0))
                    for half in range(2):
                        mch = 2 * h + half
                        qsb = qsp.get(sc)
                        for b in range(NB):
                            blk = slice(b * B, (b + 1) * B)
                            bank = pp.get(te)
                            for c in range(CH):
                                mm = te.matmul(bank.ap,
                                               WQB[:, c * 1024 + 128 * mch:c * 1024 + 128 * (mch + 1)],
                                               YH[c][:, blk], start=(c == 0), stop=(c == CH - 1))
                            tv = tec.inc(mm)
                            sc.wait_ge(tec.h, tv)
                            ins = sc.activation(qsb.ap[:, blk], bank.ap, AF.Identity,
                                                scale=0.0625)
                            bank.release(sec, sec.inc(ins))
                        sv = sec.n
                        g.wait_ge(sec.h, sv)
                        n_st = qsb.ctr.inc(g.dma_start(
                            q_sp[h, 128 * half:128 * (half + 1), :], qsb.ap[:]), 16)
                        qsb.release(qsb.ctr, n_st)
                        for b in range(NB):
                            blk = slice(b * B, (b + 1) * B)
                            s2 = sec.inc(sc.activation(SQ[:], qsb.ap[:, blk], AF.Square))
                            bank = pp.get(te)
                            te.wait_ge(sec.h, s2)
                            mm = te.matmul(bank.ap[0:1, 0:B], ONE[:, 0:1], SQ[:],
                                           start=True, stop=True)
                            tv = tec.inc(mm)
                            v.wait_ge(tec.h, tv)
                            v.wait_ge(pec.h, gm)
                            ins = v.tensor_add(CQR[:, blk], CQR[:, blk],
                                               bank.ap[0:1, 0:B])
                            bank.release(vec, vec.inc(ins))
                    sc.wait_ge(vec.h, vec.n)
                    ss = sec.inc(sc.activation(CQR[:], CQR[:], AF.Sqrt))
                    v.wait_ge(sec.h, ss)
                    ins = v.tensor_scalar(CQR[:], CQR[:], SVEC[0:1, 1:2], None, ALU.mult)
                    nv = vec.inc(ins)
                    g.wait_ge(vec.h, nv)
                    ncq = store(cq_hbm[h:h + 1, :], CQR[:])
                g.wait_ge(st.h, st.n)
                p.bar()
                for tc in range(NTC):
                    tsl = slice(tc * 128, (tc + 1) * 128)
                    for vb2 in range(1):
                        bank = pp.get(te)
                        for c in range(CH):
                            mm = te.matmul(bank.ap[:, 0:256], YH[c][:, tsl],
                                           WVB[:, c * 256:(c + 1) * 256],
                                           start=(c == 0), stop=(c == CH - 1))
                        tv = tec.inc(mm)
                        sc.wait_ge(tec.h, tv)
                        ins = sc.copy(VCT[:, 256 * tc:256 * (tc + 1)], bank.ap[:, 0:256])
                        bank.release(sec, sec.inc(ins))
                p.bar()
            if FFN_CUT < 2:
                return
            with p.phase() as alloc:
                QHB = [alloc((128, 2 * S), BF16) for _ in range(2)]
                CQB = [alloc((1, S), BF16) for _ in range(2)]
                PT = alloc((128, NTC * B), BF16)
                OH2 = [alloc((128, 2 * S), BF16) for _ in range(2)]
                RDNR = alloc((1, S), F32)
                RDNB2 = alloc((1, S), BF16)
                p.bar()
                qhp = Pool(QHB, p, "qnp")
                cqp = Pool(CQB, p, "knp")
                ohp = Pool(OH2, p, "ohp")
                SLOT = [PS2[0][:, 0:B], PS2[0][:, B:2 * B],
                        PS2[1][:, 0:B], PS2[1][:, B:2 * B]]
                A0 = PS1[0][:, 0:B]
                A1 = PS1[1][:, 0:B]
                DEN = PS1[2][0:1, 0:B]
                BCST = PS1[3][:, 0:B]
                exp_val = {}
                state = {"evac": 0, "bcst": 0}
                todo = []

                def mk_recip(qsl, ev, nxt):
                    def fn():
                        sc.wait_ge(vec.h, ev)
                        sc.activation(RDNR[0:1, qsl], RDNR[0:1, qsl], AF.Ln)
                        rd = sec.inc(sc.activation(RDNR[0:1, qsl], RDNR[0:1, qsl],
                                                   AF.Exp, scale=-1.0))
                        v.wait_ge(sec.h, rd)
                        rdnb = vec.inc(v.tensor_copy(RDNB2[0:1, qsl],
                                                     RDNR[0:1, qsl]))
                        nxt(rdnb)
                    return fn

                def mk_norm(pob, ph, qblk, gready):
                    def sched(rdnb):
                        def fn():
                            nql = slice(qblk * B, (qblk + 1) * B)
                            te.wait_ge(vec.h, max(rdnb, state["bcst"]))
                            mm2 = te.matmul(BCST, ONEB[0:1, :],
                                            RDNB2[0:1, nql], start=True, stop=True)
                            tv2 = tec.inc(mm2)
                            v.wait_ge(tec.h, tv2)
                            v.tensor_mul(pob.ap[:, nql], pob.ap[:, nql], BCST)
                            ins = v.tensor_mul(
                                pob.ap[:, S + qblk * B:S + (qblk + 1) * B],
                                pob.ap[:, S + qblk * B:S + (qblk + 1) * B], BCST)
                            state["bcst"] = vec.inc(ins)
                            if qblk == NB - 1:
                                todo.append((gready + 4, mk_store(pob, ph)))
                        todo.append((gready + 2, fn))
                    return sched

                def mk_store(pob, ph):
                    def fn():
                        g.wait_ge(vec.h, state["bcst"])
                        pob.ctr.inc(g.dma_start(o_sp[ph * 256:ph * 256 + 128, :],
                                                pob.ap[:, 0:S]), 16)
                        n_st = pob.ctr.inc(g.dma_start(
                            o_sp[ph * 256 + 128:ph * 256 + 256, :],
                            pob.ap[:, S:2 * S]), 16)
                        pob.release(pob.ctr, n_st)
                    return fn

                nsteps = NB * NTC
                for h in range(4):
                    qb_ = qhp.get(sy)
                    qb_.ctr.inc(sy.dma_start(qb_.ap[:, 0:S], q_sp[h, 0:128, :]), 16)
                    nq = qb_.ctr.inc(sy.dma_start(qb_.ap[:, S:2 * S],
                                                  q_sp[h, 128:256, :]), 16)
                    cqb = cqp.get(sy)
                    ncq = cqb.ctr.inc(sy.dma_start(cqb.ap[:], cq_hbm[h:h + 1, :]), 16)
                    te.wait_ge(qb_.ctr.h, nq)
                    te.wait_ge(cqb.ctr.h, ncq)
                    ob = ohp.get(v)

                    def emit_av(step):
                        qblk, tc = divmod(step, NTC)
                        qsl = slice(qblk * B, (qblk + 1) * B)
                        te.wait_ge(sec.h, exp_val[h * nsteps + step])
                        if tc == 0 and state["evac"]:
                            te.wait_ge(vec.h, state["evac"])
                        st_, sp_ = (tc == 0), (tc == NTC - 1)
                        ptt = PT[:, tc * B:(tc + 1) * B]
                        te.matmul(A0, VCT[:, 256 * tc:256 * tc + 128], ptt,
                                  start=st_, stop=sp_)
                        te.matmul(A1, VCT[:, 256 * tc + 128:256 * tc + 256], ptt,
                                  start=st_, stop=sp_)
                        mm = te.matmul(DEN, ONEB[:, 0:1], ptt, start=st_, stop=sp_)
                        if sp_:
                            tv = tec.inc(mm)
                            v.wait_ge(tec.h, tv)
                            v.tensor_copy(ob.ap[:, qsl], A0)
                            v.tensor_copy(ob.ap[:, S + qblk * B:S + (qblk + 1) * B],
                                          A1)
                            ins = v.tensor_copy(RDNR[0:1, qsl], DEN)
                            state["evac"] = vec.inc(ins)
                            gnow = h * nsteps + step + 2
                            todo.append((gnow + 2, mk_recip(
                                qsl, state["evac"], mk_norm(ob, h, qblk, gnow + 2))))

                    for step in range(nsteps):
                        qblk, tc = divmod(step, NTC)
                        gstep = h * nsteps + step
                        slot = SLOT[step % 4]
                        if gstep >= 4:
                            te.wait_ge(sec.h, exp_val[gstep - 4])
                        qsl = slice(qblk * B, (qblk + 1) * B)
                        te.matmul(slot, KCB[0][:, tc * 128:(tc + 1) * 128],
                                  qb_.ap[:, qsl], start=True, stop=False)
                        te.matmul(slot, KCB[1][:, tc * 128:(tc + 1) * 128],
                                  qb_.ap[:, S + qblk * B:S + (qblk + 1) * B],
                                  start=False, stop=False)
                        mm = te.matmul(slot, ONEB[0:1, :], cqb.ap[:, qsl],
                                       start=False, stop=True)
                        tv = tec.inc(mm)
                        if step % 2 == 1:
                            sc.wait_ge(tec.h, tv)
                            ins = sc.activation(PT[:, (tc - 1) * B:(tc + 1) * B],
                                                PS2[(step // 2) % 2][:, 0:2 * B],
                                                AF.Exp)
                            sv = sec.inc(ins)
                            exp_val[gstep] = exp_val[gstep - 1] = sv
                        if step >= 2:
                            emit_av(step - 2)
                        while todo and todo[0][0] <= gstep:
                            todo.pop(0)[1]()
                    emit_av(nsteps - 2)
                    emit_av(nsteps - 1)
                    qb_.release(tec, tec.n)
                    cqb.release(tec, tec.n)
                while todo:
                    todo.pop(0)[1]()
                for b_ in ohp.bufs:
                    g.wait_ge(b_.ctr.h, b_.ctr.n)
                p.bar()

            if FFN_CUT < 3:
                return
            kv_stack.close()
            with p.phase() as alloc:
                OALL = [alloc((128, S), BF16) for _ in range(CH)]
                WOB = [alloc((128, CH * 128), BF16) for _ in range(2)]
                MUB = alloc((128, S), F32)
                SCB = alloc((128, S), F32)
                LGT = alloc((8, S), F32)
                SQROW = alloc((1, S), F32)
                g.memset(LGT[:], 0.0)
                g.memset(SQROW[:], 0.0)
                WRGF = alloc((128, CH * 8), F32)
                SQV = alloc((128, B), F32)
                SQS = [alloc((128, B), F32) for _ in range(2)]
                FG2S = alloc((128, CH), F32)
                for c in range(CH):
                    load(OALL[c][:], o_sp[c * 128:(c + 1) * 128, :])
                load(FG2S[:], cd["FG2"][:])
                load(WRGF[:].rearrange("p (c e) -> p c e", e=8),
                     cd["FWR"][:].rearrange("(c pp) e -> pp c e", pp=128))
                with p.phase() as alloc_r:
                    MR = alloc_r((1, S), F32)
                    RR = alloc_r((1, S), F32)
                    vt_to_rows2([(MR[:], VT_MU, 0), (RR[:], VT_RHO, 1)])
                    sy.wait_ge(ld.h, ld.n)
                    p.bar()
                    for c in range(CH):
                        v.tensor_scalar(WRGF[:, c * 8:(c + 1) * 8],
                                        WRGF[:, c * 8:(c + 1) * 8],
                                        FG2S[:, c:c + 1], None, ALU.mult)
                    bp = Pool([PS2[0][:, 0:B], PS2[0][:, B:2 * B],
                               PS2[1][:, 0:B], PS2[1][:, B:2 * B]])
                    for b in range(NB):
                        blk = slice(b * B, (b + 1) * B)
                        bm = bp.get(te)
                        bs = bp.get(te)
                        te.matmul(bm.ap, ONE[0:1, :], MR[:, blk],
                                  start=True, stop=True)
                        mm = te.matmul(bs.ap, ONE[0:1, :], RR[:, blk],
                                       start=True, stop=True)
                        tv = tec.inc(mm)
                        sc.wait_ge(tec.h, tv)
                        sc.copy(MUB[:, blk], bm.ap)
                        ss = sec.inc(sc.copy(SCB[:, blk], bs.ap))
                        bm.release(sec, ss)
                        bs.release(sec, ss)
                    p.bar()
                XT = [alloc((128, S), F32)]
                XO = [alloc((128, S), F32)]
                xp = Pool(XT, p, "xr")
                xop = Pool(XO, p, "xo")
                wcc = p.ctr("wcc")
                pp = Pool([PS2[0][:, 0:B], PS2[0][:, B:2 * B],
                           PS2[1][:, 0:B], PS2[1][:, B:2 * B]])
                STAT = [(PS1[0][0:1, 0:B], PS1[1][0:8, 0:B]),
                        (PS1[2][0:1, 0:B], PS1[3][0:8, 0:B])]
                stat_guard = [0, 0]
                wob_guard = [0, 0]
                sq_done = [0, 0]
                sq_read = [0, 0]
                for mch in range(CH):
                    par = mch % 2
                    g.wait_ge(tec.h, wob_guard[par])
                    for c in range(CH):
                        wcc.inc(g.dma_start(
                            WOB[par][:, c * 128:(c + 1) * 128],
                            cd["FWO"][c * 128:(c + 1) * 128, mch * 128:(mch + 1) * 128]), 16)
                    nw = wcc.n
                    xb = xp.get(sy)
                    n_ld = xb.ctr.inc(sy.dma_start(
                        xb.ap[:], x_hbm[mch * 128:(mch + 1) * 128, :]), 16)
                    xo = xop.get(v)

                    def do_stats(pb):
                        sr, lg = STAT[pb % 2]
                        blkp = slice(pb * B, (pb + 1) * B)
                        te.wait_ge(sec.h, sq_done[pb % 2])
                        te.wait_ge(vec.h, stat_guard[pb % 2])
                        te.matmul(sr, ONE[:, 0:1], SQS[pb % 2][:],
                                  start=True, stop=True)
                        mm3 = te.matmul(lg, WRGF[:, mch * 8:(mch + 1) * 8],
                                        xo.ap[:, blkp], start=True, stop=True)
                        tv3 = tec.inc(mm3)
                        sq_read[pb % 2] = tv3
                        v.wait_ge(tec.h, tv3)
                        v.tensor_add(SQROW[:, blkp], SQROW[:, blkp], sr)
                        ins = v.tensor_add(LGT[:, blkp], LGT[:, blkp], lg)
                        stat_guard[pb % 2] = vec.inc(ins)

                    for b in range(NB):
                        blk = slice(b * B, (b + 1) * B)
                        bank = pp.get(te)
                        te.wait_ge(wcc.h, nw)
                        for c in range(CH):
                            mm = te.matmul(bank.ap, WOB[par][:, c * 128:(c + 1) * 128],
                                           OALL[c][:, blk], start=(c == 0),
                                           stop=(c == CH - 1))
                        tv = tec.inc(mm)
                        v.wait_ge(tec.h, tv)
                        v.wait_ge(xb.ctr.h, n_ld)
                        v.tensor_sub(SQV[:], xb.ap[:, blk], MUB[:, blk])
                        v.tensor_mul(SQV[:], SQV[:], SCB[:, blk])
                        ins = v.tensor_add(xo.ap[:, blk], SQV[:], bank.ap)
                        nv = vec.inc(ins)
                        bank.release(vec, nv)
                        sc.wait_ge(vec.h, nv)
                        sc.wait_ge(tec.h, sq_read[b % 2])
                        sq_done[b % 2] = sec.inc(
                            sc.activation(SQS[b % 2][:], xo.ap[:, blk], AF.Square))
                        if b >= 1:
                            do_stats(b - 1)
                    do_stats(NB - 1)
                    wob_guard[par] = tec.n
                    xb.release(vec, vec.n)
                    g.wait_ge(vec.h, vec.n)
                    n_st = xo.ctr.inc(g.dma_start(
                        xmid_hbm[mch * 128:(mch + 1) * 128, :], xo.ap[:]), 16)
                    xo.release(xo.ctr, n_st)
                for b_ in xop.bufs:
                    g.wait_ge(b_.ctr.h, b_.ctr.n)
                p.bar()
                g.wait_ge(vec.h, vec.n)
                nlg = vrb.inc(g.dma_start(vrow_hbm[4:12, :], LGT[:]), 16)
                row_to_vt(VT_T1, SQROW[:], lane=0)
                sy.wait_ge(ld.h, ld.n)
                sy.wait_ge(vrb.h, nlg)
                p.bar()
            v.tensor_scalar(vt(VT_T1), vt(VT_T1), 1.0 / D, None, ALU.mult)
            rsqrt_ops(vt(VT_X1), vt(VT_T1), 1e-6, vt(VT_T2), vt(VT_T3))
            p.bar()
            if FFN_CUT < 4:
                return
            with p.phase() as alloc:
                R2R = alloc((1, S), F32)
                LGT = alloc((8, S), F32)
                load(LGT[:], vrow_hbm[4:12, :])
                XT = [alloc((128, S), F32) for _ in range(2)]
                ZST = [alloc((128, S), BF16) for _ in range(2)]
                SQ3 = alloc((128, B), F32)
                RT = [alloc((128, 8), F32) for _ in range(3)]
                RW = [alloc((128, 8), F32) for _ in range(3)]
                RW2 = [alloc((128, 8), F32) for _ in range(3)]
                MM1 = [alloc((128, 4), F32) for _ in range(3)]
                CMB = [alloc((128, 8), F32) for _ in range(3)]
                FG2Z = alloc((128, CH), F32)
                R2B = alloc((128, S), F32)
                FEBS = alloc((8, 1), F32)
                vt_to_row(R2R[:], VT_X1, lane=2)
                nz1 = load(FG2Z[:], cd["FG2"][:])
                load(FEBS[:], cd["FEB"][:])
                sy.wait_ge(ld.h, ld.n)
                p.bar()
                bp2 = Pool([ps[:, 0:B] for ps in PS1])
                for b in range(NB):
                    blk = slice(b * B, (b + 1) * B)
                    bank = bp2.get(te)
                    mm = te.matmul(bank.ap, ONE[0:1, :], R2R[:, blk],
                                   start=True, stop=True)
                    tv = tec.inc(mm)
                    sc.wait_ge(tec.h, tv)
                    ss = sec.inc(sc.copy(R2B[:, blk], bank.ap))
                    bank.release(sec, ss)
                r2b_done = sec.n
                te.wait_ge(sec.h, r2b_done)
                v.wait_ge(sec.h, r2b_done)
                lgt_done = {}
                for b in range(NB):
                    blk = slice(b * B, (b + 1) * B)
                    bank = PS1[2] if b % 2 == 0 else PS1[3]
                    if b >= 2:
                        te.wait_ge(vec.h, lgt_done[b - 2])
                    mm = te.matmul(bank[0:8, 0:B], ONE[0:1, 0:8], R2R[:, blk],
                                   start=True, stop=True)
                    tv = tec.inc(mm)
                    v.wait_ge(tec.h, tv)
                    v.tensor_mul(LGT[:, blk], LGT[:, blk], bank[0:8, 0:B])
                    ins = v.tensor_scalar(LGT[:, blk], LGT[:, blk], FEBS[:, 0:1],
                                          None, ALU.add)
                    lgt_done[b] = vec.inc(ins)
                xp = Pool(XT, p, "xr")
                zp = Pool(ZST, p, "xo")
                zf_items = []
                for c in range(CH):
                    for b in range(NB):
                        zf_items.append((c, b))
                zf_state = {"xb": None, "zb": None, "n_ld": 0}

                def zf_step():
                    if not zf_items:
                        return
                    c, b = zf_items.pop(0)
                    if b == 0:
                        zf_state["xb"] = xp.get(sy)
                        zf_state["n_ld"] = zf_state["xb"].ctr.inc(sy.dma_start(
                            zf_state["xb"].ap[:],
                            xmid_hbm[c * 128:(c + 1) * 128, :]), 16)
                        zf_state["zb"] = zp.get(v)
                    blk = slice(b * B, (b + 1) * B)
                    v.wait_ge(zf_state["xb"].ctr.h, zf_state["n_ld"])
                    v.tensor_mul(SQ3[:], zf_state["xb"].ap[:, blk], R2B[:, blk])
                    ins = v.tensor_scalar(zf_state["zb"].ap[:, blk], SQ3[:],
                                          FG2Z[:, c:c + 1], None, ALU.mult)
                    nv = vec.inc(ins)
                    if b == NB - 1:
                        zf_state["xb"].release(vec, nv)
                        g.wait_ge(vec.h, nv)
                        n_st = zf_state["zb"].ctr.inc(g.dma_start(
                            zf_hbm[c * 128:(c + 1) * 128, :],
                            zf_state["zb"].ap[:]), 16)
                        zf_state["zb"].release(zf_state["zb"].ctr, n_st)

                rt_copy = {}
                SLOTT = [PS1[0], PS1[1], PS2[0][:, 0:B]]
                for tc in range(S // 128):
                    tsl = slice(tc * 128, (tc + 1) * 128)
                    i2 = tc % 3
                    slot = SLOTT[i2]
                    if tc >= 3:
                        te.wait_ge(sec.h, rt_copy[tc - 3])
                    te.wait_ge(vec.h, lgt_done[tc // 4])
                    mm = te.transpose(slot[0:128, 0:8], LGT[0:8, tsl],
                                      IDE[0:8, 0:8])
                    tv = tec.inc(mm)
                    sc.wait_ge(tec.h, tv)
                    rt_copy[tc] = sec.inc(sc.copy(RT[i2][:], slot[0:128, 0:8]))
                    ss = sec.inc(sc.activation(RW[i2][:], RT[i2][:], AF.Exp))
                    v.wait_ge(sec.h, ss)
                    v.tensor_reduce(MM1[i2][:, 1:2], RW[i2][:], AX.X, ALU.add)
                    v.reciprocal(MM1[i2][:, 1:2], MM1[i2][:, 1:2])
                    v.tensor_scalar(RW[i2][:], RW[i2][:], MM1[i2][:, 1:2],
                                    None, ALU.mult)
                    v.tensor_reduce(MM1[i2][:, 2:3], RW[i2][:], AX.X, ALU.max)
                    v.tensor_scalar(CMB[i2][:], RW[i2][:], 0.0, MM1[i2][:, 2:3],
                                    ALU.mult, ALU.add)
                    v.match_replace(RW2[i2][:], CMB[i2][:], RW[i2][:], -1e30)
                    v.tensor_reduce(MM1[i2][:, 3:4], RW2[i2][:], AX.X, ALU.max)
                    nv = vec.inc(v.tensor_sub(MM1[i2][:, 0:1], MM1[i2][:, 3:4],
                                              MM1[i2][:, 2:3]))
                    sc.wait_ge(vec.h, nv)
                    ss = sec.inc(sc.activation(MM1[i2][:, 0:1], MM1[i2][:, 0:1],
                                               AF.Sigmoid))
                    zf_step()
                    v.wait_ge(sec.h, ss)
                    v.tensor_scalar(MM1[i2][:, 1:2], MM1[i2][:, 0:1], -1.0, 1.0,
                                    ALU.mult, ALU.add)
                    v.tensor_scalar(CMB[i2][:], RW[i2][:], MM1[i2][:, 2:3],
                                    None, ALU.is_equal)
                    v.tensor_scalar(CMB[i2][:], CMB[i2][:], MM1[i2][:, 1:2],
                                    None, ALU.mult)
                    v.tensor_scalar(RW2[i2][:], RW2[i2][:], MM1[i2][:, 3:4],
                                    None, ALU.is_equal)
                    v.tensor_scalar(RW2[i2][:], RW2[i2][:], MM1[i2][:, 0:1],
                                    None, ALU.mult)
                    nv = vec.inc(v.tensor_add(CMB[i2][:], CMB[i2][:], RW2[i2][:]))
                    te.wait_ge(vec.h, nv)
                    mm = te.transpose(slot[0:8, 8:136], CMB[i2][:], IDE[:])
                    tv = tec.inc(mm)
                    sc.wait_ge(tec.h, tv)
                    rt_copy[tc] = sec.inc(sc.copy(CETF[0:8, tsl],
                                                  slot[0:8, 8:136]))
                    zf_step()
                while zf_items:
                    zf_step()
                sc.wait_ge(sec.h, sec.n)
                for b_ in zp.bufs:
                    g.wait_ge(b_.ctr.h, b_.ctr.n)
                p.bar()

            if FFN_CUT < 5:
                return
            if True:
                SH = S // 2
                BH = min(B, SH)
                NBH = SH // BH
                with p.phase() as alloc2:
                    ZF = [alloc2((128, SH), BF16) for _ in range(CH)]
                    SPEC = [alloc2((128, SH), F32) for _ in range(CH)]
                    EW1B = alloc2((128, CH * 1024), BF16)
                    EW3B = alloc2((128, CH * 1024), BF16)
                    EW2B = alloc2((128, CH * 1024), BF16)
                    HBK = alloc2((128, 2 * CH * BH), BF16)
                    SA = alloc2((128, 2 * BH), F32)
                    TB = alloc2((128, 2 * BH), F32)
                    CEBS = alloc2((128, NBH * BH), F32)
                    p.bar()
                    zlc = p.ctr("zlc")
                    wcc = p.ctr("wcc2")

                    def ew_src(e, which):
                        if e == 0:
                            return cd[{1: "SW1", 3: "SW3", 2: "SW2"}[which]]
                        return cd[{1: "EW1", 3: "EW3", 2: "EW2"}[which]][e - 1]

                    def prefetch_w13(e, w_done):
                        g.wait_ge(tec.h, w_done)
                        for c in range(CH):
                            wcc.inc(g.dma_start(
                                EW1B[:, c * 1024:(c + 1) * 1024],
                                ew_src(e, 1)[c * 128:(c + 1) * 128, :]), 16)
                        for c in range(CH):
                            wcc.inc(g.dma_start(
                                EW3B[:, c * 1024:(c + 1) * 1024],
                                ew_src(e, 3)[c * 128:(c + 1) * 128, :]), 16)
                        return wcc.n

                    def prefetch_w2(e, s_done):
                        g.wait_ge(tec.h, s_done)
                        for c in range(CH):
                            wcc.inc(g.dma_start(
                                EW2B[:, c * 1024:(c + 1) * 1024],
                                ew_src(e, 2)[c * 128:(c + 1) * 128, :]), 16)
                        return wcc.n

                    XT2 = [alloc2((128, BH), F32) for _ in range(2)]
                    XM2 = [alloc2((128, BH), F32) for _ in range(2)]
                    XN = [alloc2((128, BH), F32) for _ in range(2)]
                    xp2 = Pool(XT2, p, "xr")
                    xm2 = Pool(XM2, p, "xm2")
                    xn2 = Pool(XN, p, "xo")
                    silu_done = [0, 0]
                    tb_done = [0, 0]
                    h_done = [0, 0]
                    spec_tec = [0, 0]
                    ceb_read = 0
                    specp = Pool([PS1[0][:, 0:BH], PS1[1][:, 0:BH]])
                    for c in range(CH):
                        zlc.inc(sy.dma_start(
                            ZF[c][:], zf_hbm[c * 128:(c + 1) * 128, 0:SH]), 16)
                    nz = zlc.n
                    te.wait_ge(zlc.h, nz)
                    w13_ready = prefetch_w13(0, tec.n)
                    w2_cur = prefetch_w2(0, tec.n)
                    for tg in range(2):
                        toff = tg * SH
                        te.wait_ge(zlc.h, nz)
                        order = list(range(9)) if tg == 0 else list(range(8, -1, -1))
                        for ei, e in enumerate(order):
                            if e > 0:
                                te.wait_ge(vec.h, ceb_read)
                                cps = []
                                for b in range(NBH):
                                    bnk = PS1[2] if b % 2 == 0 else PS1[3]
                                    if b >= 2:
                                        te.wait_ge(sec.h, cps[b - 2])
                                    mm = te.matmul(bnk[:, 0:BH],
                                                   IND8B[:, (e - 1) * 128:e * 128],
                                                   CETF[:, toff + b * BH:toff + (b + 1) * BH],
                                                   start=True, stop=True)
                                    tv = tec.inc(mm)
                                    sc.wait_ge(tec.h, tv)
                                    ins = sc.copy(CEBS[:, b * BH:(b + 1) * BH],
                                                  bnk[:, 0:BH])
                                    cps.append(sec.inc(ins))
                            ceb_ready = sec.n

                            def emit_spec(b):
                                blk = slice(b * BH, (b + 1) * BH)
                                hoff = (b % 2) * CH * BH
                                te.wait_ge(wcc.h, w2_cur)
                                for mch in range(CH):
                                    bank = specp.get(te)
                                    te.wait_ge(pec.h, h_done_blk[b])
                                    for fc in range(CH):
                                        mm = te.matmul(
                                            bank.ap,
                                            EW2B[:, fc * 1024 + 128 * mch:fc * 1024 + 128 * (mch + 1)],
                                            HBK[:, hoff + fc * BH:hoff + (fc + 1) * BH],
                                            start=(fc == 0), stop=(fc == CH - 1))
                                    tv = tec.inc(mm)
                                    v.wait_ge(tec.h, tv)
                                    if ei == 0:
                                        ins = v.tensor_copy(SPEC[mch][:, blk],
                                                            bank.ap)
                                    else:
                                        ins = v.tensor_add(SPEC[mch][:, blk],
                                                           SPEC[mch][:, blk],
                                                           bank.ap)
                                    bank.release(vec, vec.inc(ins))
                                spec_tec[b % 2] = tec.n

                            h_done_blk = [0] * NBH
                            for b in range(NBH):
                                blk = slice(b * BH, (b + 1) * BH)
                                hoff = (b % 2) * CH * BH
                                for fc in range(CH):
                                    par = fc % 2
                                    psl = slice(par * BH, (par + 1) * BH)
                                    bankt = PS2[par]
                                    te.wait_ge(sec.h, silu_done[par])
                                    te.wait_ge(vec.h, tb_done[par])
                                    if fc == 0 and b == 0:
                                        te.wait_ge(wcc.h, w13_ready)
                                    for c in range(CH):
                                        te.matmul(bankt[:, 0:BH],
                                                  EW1B[:, c * 1024 + 128 * fc:c * 1024 + 128 * (fc + 1)],
                                                  ZF[c][:, blk], start=(c == 0),
                                                  stop=(c == CH - 1))
                                    for c in range(CH):
                                        mm = te.matmul(bankt[:, BH:2 * BH],
                                                       EW3B[:, c * 1024 + 128 * fc:c * 1024 + 128 * (fc + 1)],
                                                       ZF[c][:, blk], start=(c == 0),
                                                       stop=(c == CH - 1))
                                    tv = tec.inc(mm)
                                    sc.wait_ge(tec.h, tv)
                                    sc.wait_ge(pec.h, h_done[par])
                                    ss = sec.inc(sc.activation(SA[:, psl],
                                                               bankt[:, 0:BH], AF.Silu))
                                    silu_done[par] = ss
                                    v.wait_ge(tec.h, tv)
                                    v.wait_ge(pec.h, h_done[par])
                                    if BH < 512:
                                        v.wait_ge(sec.h, silu_done[par])
                                    if e > 0:
                                        v.wait_ge(sec.h, ceb_ready)
                                        ins = v.tensor_mul(TB[:, psl], bankt[:, BH:2 * BH],
                                                           CEBS[:, b * BH:(b + 1) * BH])
                                    else:
                                        ins = v.tensor_copy(TB[:, psl], bankt[:, BH:2 * BH])
                                    tb_done[par] = vec.inc(ins)
                                    g.wait_ge(sec.h, silu_done[par])
                                    g.wait_ge(vec.h, tb_done[par])
                                    g.wait_ge(tec.h, spec_tec[b % 2])
                                    ins = g.tensor_mul(HBK[:, hoff + fc * BH:hoff + (fc + 1) * BH],
                                                       SA[:, psl], TB[:, psl])
                                    h_done[par] = pec.inc(ins)
                                h_done_blk[b] = pec.n
                                if b > 0:
                                    emit_spec(b - 1)
                            w_done = tec.n
                            emit_spec(NBH - 1)
                            s_done = tec.n
                            ceb_read = vec.n
                            if ei < 8:
                                w13_ready = prefetch_w13(order[ei + 1], w_done)
                                w2_cur = prefetch_w2(order[ei + 1], s_done)
                        if tg == 0:
                            g.wait_ge(tec.h, w_done)
                            for c in range(CH):
                                zlc.inc(g.dma_start(
                                    ZF[c][:], zf_hbm[c * 128:(c + 1) * 128,
                                                     SH:2 * SH]), 16)
                            nz = zlc.n
                        tgt = out_d if is_last else x_hbm
                        for c in range(CH):
                            for b in range(NBH):
                                csl = slice(c * 128, (c + 1) * 128)
                                hsl = slice(toff + b * BH, toff + (b + 1) * BH)
                                xb = xp2.get(sy)
                                n1 = xb.ctr.inc(sy.dma_start(
                                    xb.ap[:], x_hbm[csl, hsl]), 16)
                                xmb = xm2.get(sy)
                                n2 = xmb.ctr.inc(sy.dma_start(
                                    xmb.ap[:], xmid_hbm[csl, hsl]), 16)
                                xnb = xn2.get(v)
                                v.wait_ge(xb.ctr.h, n1)
                                v.wait_ge(xmb.ctr.h, n2)
                                v.tensor_add(xnb.ap[:], xb.ap[:], xmb.ap[:])
                                ins = v.tensor_add(
                                    xnb.ap[:], xnb.ap[:],
                                    SPEC[c][:, b * BH:(b + 1) * BH])
                                nv = vec.inc(ins)
                                xb.release(vec, nv)
                                xmb.release(vec, nv)
                                g.wait_ge(vec.h, nv)
                                n_st = xnb.ctr.inc(g.dma_start(
                                    tgt[csl, hsl], xnb.ap[:]), 16)
                                xnb.release(xnb.ctr, n_st)
                    for b_ in xn2.bufs:
                        g.wait_ge(b_.ctr.h, b_.ctr.n)
                    p.bar()


    def attn_b(is_last):
        with p.phase() as alloc:
            x_stats(x_hbm, alloc)
        p.bar()

        with p.phase() as alloc_o:
            YH = [alloc_o((128, S), BF16) for _ in range(CH)]
            with p.phase() as alloc_n:
                build_norm_operand(YH, VT_RHO, x_hbm, alloc_n)

            g.memset(BIAS[0:4, 16:20], 0.0)
            p.bar()

            def b_groups(is_q):
                col0 = 0 if is_q else 1024
                spill = qn_sp if is_q else kn_sp
                qs = 0.125 if is_q else 1.0
                with p.phase() as alloc:
                    TCOS = alloc((128, S), BF16)
                    TSIN = alloc((128, S), BF16)
                    RA = alloc((128, S), F32)
                    RB = alloc((128, S), F32)
                    STG = [alloc((128, 2 * B), F32) for _ in range(2)]
                    SQP = [alloc((128, B), F32) for _ in range(2)]
                    SQ2P = [alloc((128, B), F32) for _ in range(2)]
                    WQG = [alloc((128, 2048), BF16) for _ in range(2)]
                    QNE = alloc((128, S), BF16)
                    QNO = alloc((128, S), BF16)
                    SB = alloc((4, 4 * B), F32)

                    def sbt(i, par):
                        return SB[:, (2 * par + i) * B:(2 * par + i + 1) * B]

                    store(TCOS[:], cd["COSB"][:])
                    store(TSIN[:], cd["SINB"][:])
                    g.wait_ge(st.h, st.n)
                    p.bar()
                    if is_q:
                        v.tensor_scalar(TCOS[:], TCOS[:], qs, None, ALU.mult)
                        v.tensor_scalar(TSIN[:], TSIN[:], qs, None, ALU.mult)
                        p.bar()
                    PJ = [PS2[1][:, 0:B], PS2[1][:, B:2 * B], PS2[2][:, 0:B]]
                    P0 = [PS2[0][0:4, 0:B], PS2[0][0:4, B:2 * B]]
                    pjp = Pool(PJ)
                    gd = {"spill_st": 0, "cq_st": [0, 0], "wq": [0, 0],
                          "rot": [0] * NB, "rotp": [0] * NB,
                          "sq_read": [0, 0], "p0q": [(None, 0), (None, 0)]}

                    def wq_chunk(gi, c):
                        par = gi % 2
                        if c == 0:
                            g.wait_ge(tec.h, gd["wq"][par])
                        store(WQG[par][:, c * 256:(c + 1) * 256],
                              cd["BQKV"][c * 128:(c + 1) * 128,
                                         col0 + 256 * gi:col0 + 256 * gi + 256])
                        return st.n

                    def proj_block(gi, b):
                        par = gi % 2
                        blk = slice(b * B, (b + 1) * B)
                        for half, dst in ((0, RA), (1, RB)):
                            bank = pjp.get(te)
                            if b == 0 and half == 0:
                                te.wait_ge(st.h, gd["wq_rdy"])
                            for c in range(CH):
                                mm = te.matmul(
                                    bank.ap,
                                    WQG[par][:, c * 256 + 128 * half:
                                              c * 256 + 128 * (half + 1)],
                                    YH[c][:, blk], start=(c == 0),
                                    stop=(c == CH - 1))
                            tv = tec.inc(mm)
                            sc.wait_ge(tec.h, tv)
                            if half == 0:
                                sc.wait_ge(vec.h, gd["rot"][b])
                                sc.wait_ge(pec.h, gd["rotp"][b])
                            ins = sc.copy(dst[:, blk], bank.ap)
                            bank.release(sec, sec.inc(ins))
                        gd["wq"][par] = tec.n
                        return sec.n

                    def rope_block(b, copy_done):
                        par = b % 2
                        blk = slice(b * B, (b + 1) * B)
                        v.wait_ge(sec.h, copy_done)
                        v.wait_ge(tec.h, gd["sq_read"][par])
                        v.wait_ge(st.h, gd["spill_st"])
                        v.tensor_mul(SQP[par][:], RA[:, blk], TCOS[:, blk])
                        v.tensor_mul(SQ2P[par][:], RB[:, blk], TSIN[:, blk])
                        g.wait_ge(sec.h, copy_done)
                        g.wait_ge(st.h, gd["spill_st"])
                        g.tensor_mul(STG[par][:, 0:B], RA[:, blk], TSIN[:, blk])
                        ins = g.tensor_mul(STG[par][:, B:2 * B], RB[:, blk],
                                           TCOS[:, blk])
                        gp = pec.inc(ins)
                        ins = v.tensor_sub(QNE[:, blk], SQP[par][:], SQ2P[par][:])
                        gd["rot"][b] = vec.inc(ins)
                        ins = g.tensor_add(QNO[:, blk], STG[par][:, 0:B],
                                           STG[par][:, B:2 * B])
                        gd["rotp"][b] = pec.inc(ins)

                    def cstat_block(gi, b):
                        par = b % 2
                        blk = slice(b * B, (b + 1) * B)
                        sc.wait_ge(vec.h, gd["rot"][b])
                        sc.wait_ge(pec.h, gd["rotp"][b])
                        sc.wait_ge(tec.h, gd["sq_read"][par])
                        sc.activation(SQP[par][:], QNE[:, blk], AF.Square)
                        s2 = sec.inc(sc.activation(SQ2P[par][:], QNO[:, blk],
                                                   AF.Square))
                        peng, pval = gd["p0q"][par]
                        if peng is not None:
                            te.wait_ge(peng.h, pval)
                        te.wait_ge(sec.h, s2)
                        te.matmul(P0[par], IND4[:], SQP[par][:],
                                  start=True, stop=False)
                        mm = te.matmul(P0[par], IND4[:], SQ2P[par][:],
                                       start=False, stop=True)
                        tv = tec.inc(mm)
                        gd["sq_read"][par] = tv
                        if is_q:
                            sc.wait_ge(tec.h, tv)
                            ss = sec.inc(sc.activation(sbt(0, par), P0[par],
                                                       AF.Sqrt))
                            gd["p0q"][par] = (sec, ss)
                            v.wait_ge(sec.h, ss)
                            v.wait_ge(st.h, gd["cq_st"][par])
                            ins = v.tensor_scalar(sbt(1, par), sbt(0, par),
                                                  SVEC[0:4, 4 + gi:5 + gi],
                                                  None, ALU.mult)
                            nv = vec.inc(ins)
                            g.wait_ge(vec.h, nv)
                            for h4 in range(4):
                                store(cq_hbm[4 * gi + h4:4 * gi + h4 + 1, blk],
                                      SB[h4:h4 + 1, (2 * par + 1) * B:
                                         (2 * par + 2) * B])
                            gd["cq_st"][par] = st.n
                        else:
                            v.wait_ge(tec.h, tv)
                            v.tensor_reduce(sbt(0, par)[:, 0:1], P0[par],
                                            AX.X, ALU.max)
                            ins = v.tensor_max(BIAS[0:4, 16 + gi:17 + gi],
                                               BIAS[0:4, 16 + gi:17 + gi],
                                               sbt(0, par)[:, 0:1])
                            gd["p0q"][par] = (vec, vec.inc(ins))

                    for c in range(CH):
                        gd["wq_rdy"] = wq_chunk(0, c)
                    for gi in range(NG):
                        for b in range(NB):
                            copy_done = proj_block(gi, b)
                            rope_block(b, copy_done)
                            if gi + 1 < NG:
                                gd["wq_rdy"] = wq_chunk(gi + 1, b)
                            if b >= 1:
                                cstat_block(gi, b - 1)
                        cstat_block(gi, NB - 1)
                        g.wait_ge(vec.h, gd["rot"][NB - 1])
                        g.wait_ge(pec.h, gd["rotp"][NB - 1])
                        for h4 in range(4):
                            rs = slice(32 * h4, 32 * (h4 + 1))
                            store(spill[4 * gi + h4, 0:32, :], QNE[rs, :])
                            store(spill[4 * gi + h4, 32:64, :], QNO[rs, :])
                        gd["spill_st"] = st.n
                    g.wait_ge(st.h, st.n)
                    p.bar()
                    if not is_q:
                        ss = sec.inc(sc.activation(BIAS[0:4, 20:24],
                                                   BIAS[0:4, 16:20], AF.Sqrt))
                        v.wait_ge(sec.h, ss)
                        v.tensor_scalar(SVEC[0:4, 4:8], BIAS[0:4, 20:24],
                                        -1.0, None, ALU.mult)
                        p.bar()

            b_groups(is_q=False)
            b_groups(is_q=True)

            with p.phase() as alloc:
                WVB = alloc((128, CH * 1024), BF16)
                VST = [alloc((128, 512), BF16) for _ in range(2)]
                g.wait_ge(tec.h, tec.n)
                for c in range(CH):
                    store(WVB[:, c * 1024:(c + 1) * 1024],
                          cd["BQKV"][c * 128:(c + 1) * 128, 2048:3072])
                nw = st.n
                p.bar()
                vp = Pool(VST, p, "vst")
                pp = Pool([ps[:, 0:512] for ps in PS1[0:3]])
                for tc in range(NTC):
                    tsl = slice(tc * 128, (tc + 1) * 128)
                    for vb2 in range(2):
                        bank = pp.get(te)
                        te.wait_ge(st.h, nw)
                        for c in range(CH):
                            mm = te.matmul(bank.ap, YH[c][:, tsl],
                                           WVB[:, c * 1024 + 512 * vb2:c * 1024 + 512 * (vb2 + 1)],
                                           start=(c == 0), stop=(c == CH - 1))
                        tv = tec.inc(mm)
                        vbuf = vp.get(sc)
                        sc.wait_ge(tec.h, tv)
                        sv = sec.inc(sc.copy(vbuf.ap[:], bank.ap))
                        bank.release(sec, sv)
                        g.wait_ge(sec.h, sv)
                        n_st = vbuf.ctr.inc(g.dma_start(
                            v_sp[8 * vb2:8 * (vb2 + 1), tc:tc + 1, :, :].rearrange(
                                "h o pp dd -> (o pp) h dd"),
                            vbuf.ap.rearrange("pp (h dd) -> pp h dd", h=8)), 16)
                        vbuf.release(vbuf.ctr, n_st)
                for b_ in vp.bufs:
                    g.wait_ge(b_.ctr.h, b_.ctr.n)
                p.bar()

        attn_core(causal=True)

        with p.phase() as alloc:
            OALL = [alloc((128, S), BF16) for _ in range(CH)]
            WOB = [alloc((128, CH * 128), BF16) for _ in range(2)]
            XT = [alloc((128, S), F32) for _ in range(2)]
            XO = [alloc((128, S), F32) for _ in range(2)]
            for c in range(CH):
                load(OALL[c][:], o_sp[c * 128:(c + 1) * 128, :])
            sy.wait_ge(ld.h, ld.n)
            p.bar()
            xp = Pool(XT, p, "xr")
            xop = Pool(XO, p, "xo")
            wcc = p.ctr("wcc")
            pp = Pool([ps[:, 0:B] for ps in PS1[0:3]])
            SQF = [alloc((128, B), F32) for _ in range(2)]
            SROWX = alloc((1, S), F32)
            QROWX = alloc((1, S), F32)
            SBK = [PS2[0][:, 0:B], PS2[0][:, B:2 * B]]
            sg_read = [0, 0]
            sq_rd = [0, 0]
            pend_stat = []

            def do_fold_stats(mch, xo, item):
                b, nv = item
                blk = slice(b * B, (b + 1) * B)
                bpar = b % 2
                sc.wait_ge(vec.h, nv)
                sc.wait_ge(tec.h, sq_rd[bpar])
                sq2 = sec.inc(sc.activation(SQF[bpar][:], xo.ap[:, blk],
                                            AF.Square))
                te.wait_ge(sec.h, sq2)
                te.wait_ge(vec.h, sg_read[bpar])
                te.matmul(SBK[bpar][0:1, 0:B], ONE[:, 0:1], xo.ap[:, blk],
                          start=True, stop=True)
                mm2 = te.matmul(SBK[bpar][32:33, 0:B], ONE[:, 0:1],
                                SQF[bpar][:], start=True, stop=True)
                sq_rd[bpar] = tv2 = tec.inc(mm2)
                v.wait_ge(tec.h, tv2)
                if mch == 0:
                    v.tensor_copy(SROWX[:, blk], SBK[bpar][0:1, 0:B])
                    ins = v.tensor_copy(QROWX[:, blk], SBK[bpar][32:33, 0:B])
                else:
                    v.tensor_add(SROWX[:, blk], SROWX[:, blk],
                                 SBK[bpar][0:1, 0:B])
                    ins = v.tensor_add(QROWX[:, blk], QROWX[:, blk],
                                       SBK[bpar][32:33, 0:B])
                sg_read[bpar] = vec.inc(ins)
            tgt = out_d if is_last else x_hbm
            wob_guard = [0, 0]
            for mch in range(CH):
                par = mch % 2
                g.wait_ge(tec.h, wob_guard[par])
                for c in range(CH):
                    wcc.inc(g.dma_start(
                        WOB[par][:, c * 128:(c + 1) * 128],
                        cd["BWO"][c * 128:(c + 1) * 128, mch * 128:(mch + 1) * 128]), 16)
                nw = wcc.n
                xb = xp.get(sy)
                n_ld = xb.ctr.inc(sy.dma_start(
                    xb.ap[:], x_hbm[mch * 128:(mch + 1) * 128, :]), 16)
                xo = xop.get(v)
                for b in range(NB):
                    blk = slice(b * B, (b + 1) * B)
                    bank = pp.get(te)
                    te.wait_ge(wcc.h, nw)
                    for c in range(CH):
                        mm = te.matmul(bank.ap, WOB[par][:, c * 128:(c + 1) * 128],
                                       OALL[c][:, blk], start=(c == 0),
                                       stop=(c == CH - 1))
                    tv = tec.inc(mm)
                    v.wait_ge(tec.h, tv)
                    v.wait_ge(xb.ctr.h, n_ld)
                    ins = v.tensor_add(xo.ap[:, blk], bank.ap, xb.ap[:, blk])
                    nv = vec.inc(ins)
                    bank.release(vec, nv)
                    pend_stat.append((b, nv))
                    if len(pend_stat) > 1:
                        do_fold_stats(mch, xo, pend_stat.pop(0))
                while pend_stat:
                    do_fold_stats(mch, xo, pend_stat.pop(0))
                wob_guard[par] = tec.n
                xb.release(vec, vec.n)
                g.wait_ge(vec.h, vec.n)
                n_st = xo.ctr.inc(g.dma_start(
                    tgt[mch * 128:(mch + 1) * 128, :], xo.ap[:]), 16)
                xo.release(xo.ctr, n_st)
            if not is_last:
                g.wait_ge(vec.h, vec.n)
                vrb.inc(g.dma_start(vrow_hbm[0:1, :], SROWX[:]), 16)
                nvr = vrb.inc(g.dma_start(vrow_hbm[1:2, :], QROWX[:]), 16)
                g.wait_ge(vrb.h, nvr)
            for b_ in xop.bufs:
                g.wait_ge(b_.ctr.h, b_.ctr.n)
            p.bar()

    attn_a(is_last=(trunc == 1))
    if trunc >= 2:
        ffn_layer(is_last=(trunc == 2))
    if trunc >= 3:
        attn_b(is_last=(trunc == 3))
    if trunc >= 4:
        ffn_layer(is_last=True)

    if debug_dump is not None:
        src = dbg_scratch[debug_dump]
        dbg = nc.dram_tensor("dbg", list(src.tensor.shape),
                             src.tensor.dtype, kind="ExternalOutput").ap()
        n = store(dbg[:], src[:])
        g.wait_ge(st.h, n)
        p.bar()

    if trunc < 1:
        with p.phase() as alloc:
            XT = alloc((128, S), F32)
            for c in range(CH):
                n = load(XT[:], x_hbm[c * 128:(c + 1) * 128, :])
                sy.wait_ge(ld.h, n)
                m_ = store(out_d[c * 128:(c + 1) * 128, :], XT[:])
                sy.wait_ge(st.h, m_)
            g.wait_ge(st.h, st.n)
    p.bar()
    g.wait_ge(st.h, st.n)
    sy.wait_ge(st.h, st.n)
    return nc, es


def kernel(**inputs):
    m, xT = _prep_inputs(inputs)
    nc, es = build_program()
    in_maps = []
    for core in range(NCORE):
        d = dict(m)
        d["x"] = xT[core]
        in_maps.append(d)
    res = run_bass_kernel_spmd(nc, in_maps, core_ids=list(range(NCORE)))
    es.close()
    out = np.stack([np.asarray(r["out"]) for r in res.results], axis=0)
    return np.ascontiguousarray(out.transpose(0, 2, 1))


# version 38
# speedup vs baseline: 1.0235x; 1.0235x over previous
import contextlib
import numpy as np

import concourse.bass as bass
import concourse.mybir as mybir
from concourse.bass_utils import run_bass_kernel_spmd

F32 = mybir.dt.float32
BF16 = mybir.dt.bfloat16
AF = mybir.ActivationFunctionType
ALU = mybir.AluOpType
AX = mybir.AxisListType

D = 1024
NCORE = 8
HEADS = 16
HD = 64
NG = 4
CH = D // 128

S = 4096
TRUNC = 4
DEBUG_DUMP = None
FFN_CUT = 99


def _derived():
    B = min(512, S)
    return B, S // B, S // 128



def _perm_eo_grouped():
    perm = []
    for g in range(NG):
        for h in range(4 * g, 4 * g + 4):
            perm.extend(h * HD + 2 * j for j in range(32))
        for h in range(4 * g, 4 * g + 4):
            perm.extend(h * HD + 2 * j + 1 for j in range(32))
    return np.array(perm, dtype=np.int64)


def _perm_half_grouped():
    perm = []
    for g in range(NG):
        for h in range(4 * g, 4 * g + 4):
            perm.extend(h * HD + j for j in range(32))
        for h in range(4 * g, 4 * g + 4):
            perm.extend(h * HD + 32 + j for j in range(32))
    return np.array(perm, dtype=np.int64)


def _host_consts():
    B, NB, NTC = _derived()
    c = {}
    mel_max = np.float32(2595.0 * np.log10(1.0 + 4000.0 / 200.0))
    scale = np.power(np.float32(10.0),
                     np.linspace(np.float32(0.0), mel_max, 32, dtype=np.float32)
                     / np.float32(2595.0)) - np.float32(1.0)
    fb = np.float32(200.0) * scale / np.float32(1000.0)
    t = np.arange(S, dtype=np.float32)
    freqs = (t[:, None] * fb[None, :]).astype(np.float32)
    c["COSA"] = np.ascontiguousarray(np.tile(np.cos(freqs).astype(np.float32).T, (4, 1)))
    c["SINA"] = np.ascontiguousarray(np.tile(np.sin(freqs).astype(np.float32).T, (4, 1)))
    inv_freq = (np.float32(1.0)
                / np.power(np.float32(10000.0),
                           np.arange(0, HD, 2, dtype=np.float32) / np.float32(HD)))
    ang = (t[:, None] * inv_freq[None, :]).astype(np.float32)
    c["COSB"] = np.ascontiguousarray(np.tile(np.cos(ang).astype(np.float32).T, (4, 1)))
    c["SINB"] = np.ascontiguousarray(np.tile(np.sin(ang).astype(np.float32).T, (4, 1)))
    kk = np.arange(128)[:, None]
    qq = np.arange(B)[None, :]
    c["CMASK"] = np.concatenate(
        [(kk <= qq - 128 * j).astype(np.float32) for j in range(4)], axis=1)
    c["IDENT"] = np.eye(128, dtype=np.float32)
    c["ONESF"] = np.ones((128, 128), dtype=np.float32)
    ind4 = np.zeros((128, 4), dtype=np.float32)
    for p in range(128):
        ind4[p, p // 32] = 1.0
    c["IND4"] = ind4
    c["IND4M"] = ind4 / np.float32(64.0)
    c["IND4T"] = np.ascontiguousarray(ind4.T)
    ind8 = np.zeros((8, 8 * 128), dtype=np.float32)
    for e in range(8):
        ind8[e, e * 128:(e + 1) * 128] = 1.0
    c["IND8"] = ind8
    return c


def _prep_inputs(inp):
    f = np.float32
    p_eo = _perm_eo_grouped()
    p_hf = _perm_half_grouped()
    m = dict(_host_consts())

    m["AQW"] = np.ascontiguousarray(np.asarray(inp["L0_attn_a_q_w"], dtype=f)[:, p_eo])
    kvw = np.asarray(inp["L0_attn_a_kv_w"], dtype=f)
    kperm = np.concatenate([p_eo, 1024 + np.arange(1024)])
    m["AKVW"] = np.ascontiguousarray(kvw[:, kperm])
    m["AOW"] = np.ascontiguousarray(np.asarray(inp["L0_attn_a_out_w"], dtype=f))
    m["ARW"] = np.ascontiguousarray(
        np.asarray(inp["L0_attn_a_rh_w"], dtype=f).reshape(CH, 128).T)
    m["ARB"] = np.asarray(inp["L0_attn_a_rh_b"], dtype=f).reshape(1, 1)

    def colvec(name, key, perm=None):
        val = np.asarray(inp[key], dtype=f).reshape(-1)
        if perm is not None:
            val = val[perm]
        m[name] = np.ascontiguousarray(val.reshape(-1, 128).T)

    def rowvec(name, key, perm=None):
        val = np.asarray(inp[key], dtype=f).reshape(-1)
        if perm is not None:
            val = val[perm]
        m[name] = np.ascontiguousarray(val.reshape(1, -1))

    rowvec("AQBR", "L0_attn_a_q_b", p_eo)
    rowvec("AKVBR", "L0_attn_a_kv_b", kperm)
    colvec("AOB", "L0_attn_a_out_b")
    colvec("AQNW", "L0_attn_a_qn_w")
    colvec("AQNB", "L0_attn_a_qn_b")
    colvec("AKVNW", "L0_attn_a_kvn_w")
    colvec("AKVNB", "L0_attn_a_kvn_b")
    lnw = np.asarray(inp["L0_attn_a_ln_w"], dtype=f)
    lnb = np.asarray(inp["L0_attn_a_ln_b"], dtype=f)
    m["ALN"] = np.ascontiguousarray(np.stack(
        [np.tile(lnw[0::2], 4), np.tile(lnb[0::2], 4),
         np.tile(lnw[1::2], 4), np.tile(lnb[1::2], 4)], axis=1))

    qkvT = np.asarray(inp["L0_attn_b_qkv"], dtype=f).T
    bperm = np.concatenate([p_hf, 1024 + p_hf, 2048 + np.arange(1024)])
    m["BQKV"] = np.ascontiguousarray(qkvT[:, bperm])
    m["BWO"] = np.ascontiguousarray(np.asarray(inp["L0_attn_b_wo"], dtype=f).T)

    m["FWQ"] = np.ascontiguousarray(np.asarray(inp["L0_ffn_Wq"], dtype=f).T)
    m["FWK"] = np.ascontiguousarray(np.asarray(inp["L0_ffn_Wkc"], dtype=f).T)
    m["FWV"] = np.ascontiguousarray(np.asarray(inp["L0_ffn_Wvc"], dtype=f).T)
    m["FWO"] = np.ascontiguousarray(np.asarray(inp["L0_ffn_Wo"], dtype=f).T)
    m["FWR"] = np.ascontiguousarray(np.asarray(inp["L0_ffn_Wr"], dtype=f).T)
    m["FEB"] = np.asarray(inp["L0_ffn_expert_bias"], dtype=f).reshape(8, 1)
    colvec("FG1", "L0_ffn_g1")
    colvec("FG2", "L0_ffn_g2")
    m["SW1"] = np.ascontiguousarray(np.asarray(inp["L0_ffn_sW1"], dtype=f).T)
    m["SW3"] = np.ascontiguousarray(np.asarray(inp["L0_ffn_sW3"], dtype=f).T)
    m["SW2"] = np.ascontiguousarray(np.asarray(inp["L0_ffn_sW2"], dtype=f).T)
    m["EW1"] = np.ascontiguousarray(
        np.asarray(inp["L0_ffn_eW1"], dtype=f).transpose(0, 2, 1))
    m["EW3"] = np.ascontiguousarray(
        np.asarray(inp["L0_ffn_eW3"], dtype=f).transpose(0, 2, 1))
    m["EW2"] = np.ascontiguousarray(
        np.asarray(inp["L0_ffn_eW2"], dtype=f).transpose(0, 2, 1))

    xT = np.ascontiguousarray(
        np.asarray(inp["L0_attn_a_x"], dtype=f).transpose(0, 2, 1))
    return m, xT



class Ctr:
    def __init__(self, nc, es, name):
        self.h = es.enter_context(nc.semaphore(name))
        self.n = 0
        self._last_auto_id = None

    def inc(self, instr, k=1):
        if instr is not None and id(instr) == self._last_auto_id:
            return self.n
        instr.then_inc(self.h, k)
        self.n += k
        return self.n

    def wait(self, eng, n=None):
        eng.wait_ge(self.h, self.n if n is None else n)


_WRAP_OPS = {
    "tensor_scalar", "tensor_tensor", "tensor_add", "tensor_sub", "tensor_mul",
    "tensor_max", "tensor_copy", "tensor_reduce", "reduce_max", "reduce_sum",
    "reciprocal", "match_replace", "scalar_tensor_tensor", "select", "memset",
    "memzero", "tensor_scalar_add", "tensor_scalar_mul", "tensor_scalar_max",
    "tensor_scalar_min", "tensor_single_scalar", "activation", "copy", "add",
    "mul", "square", "sqrt", "sign", "tensor_relu",
}


class EngProxy:
    def __init__(self, eng, ctr):
        self._e = eng
        self._c = ctr
        self._dirty = {}
        self._waited = 0

    def __getattr__(self, k):
        f = getattr(self._e, k)
        if k not in _WRAP_OPS:
            return f

        def call(*args, **kw):
            need = 0
            for a in args:
                t = getattr(a, "tensor", None)
                if t is not None:
                    need = max(need, self._dirty.get(t.name, 0))
            for a in kw.values():
                t = getattr(a, "tensor", None)
                if t is not None:
                    need = max(need, self._dirty.get(t.name, 0))
            if need > self._waited:
                self._e.wait_ge(self._c.h, need)
                self._waited = need
            instr = f(*args, **kw)
            instr.then_inc(self._c.h, 1)
            self._c.n += 1
            self._c._last_auto_id = id(instr)
            out = args[0]
            t = getattr(out, "tensor", None)
            if t is not None:
                self._dirty[t.name] = self._c.n
            return instr
        return call


class Buf:
    def __init__(self, ap):
        self.ap = ap
        self.rel = []

    def wait(self, *engs):
        for ctr, val in self.rel:
            for e in engs:
                e.wait_ge(ctr.h, val)
        self.rel = []

    def release(self, ctr, val=None):
        self.rel.append((ctr, ctr.n if val is None else val))


class Pool:
    def __init__(self, aps, p=None, tag=None):
        self.bufs = [Buf(a) for a in aps]
        self.i = 0
        if p is not None:
            for idx, b in enumerate(self.bufs):
                b.ctr = p.ctr(f"{tag}{idx}")

    def get(self, *engs):
        b = self.bufs[self.i % len(self.bufs)]
        self.i += 1
        b.wait(*engs)
        return b


class Prog:
    def __init__(self, nc, es):
        self.nc = nc
        self.es = es
        self.sy = nc.sync
        self.g = nc.gpsimd
        self.v = nc.vector
        self.sc = nc.scalar
        self.te = nc.tensor
        self._n = 0
        self._named = {}

    def ctr(self, key=None):
        if key is not None and key in self._named:
            return self._named[key]
        self._n += 1
        c = Ctr(self.nc, self.es, f"c{self._n}")
        if key is not None:
            self._named[key] = c
        return c

    def bar(self):
        self.nc.all_engine_barrier()

    @contextlib.contextmanager
    def phase(self):
        with contextlib.ExitStack() as ps:
            def alloc(shape, dtype, name=None):
                self._n += 1
                nm = name or f"t{self._n}"
                return ps.enter_context(
                    self.nc.sbuf_tensor(nm, list(shape), dtype))
            yield alloc



def build_program(trunc=None, debug_dump=None):
    if trunc is None:
        trunc = TRUNC
    if debug_dump is None:
        debug_dump = DEBUG_DUMP
    B, NB, NTC = _derived()

    es = contextlib.ExitStack()
    nc = bass.Bass("TRN2", target_bir_lowering=False,
                   detect_race_conditions=False)
    p = Prog(nc, es)
    ld = p.ctr("ld")
    st = p.ctr("st")
    tec = p.ctr("tec")
    vec = p.ctr("vec")
    sec = p.ctr("sec")
    aux = p.ctr("aux")
    pec = p.ctr("pec")
    v = p.v = EngProxy(nc.vector, vec)
    sc = p.sc = EngProxy(nc.scalar, sec)
    g = p.g = EngProxy(nc.gpsimd, pec)
    sy, te = p.sy, p.te

    def load(dst, src):
        return ld.inc(sy.dma_start(dst, src), 16)

    def store(dst, src):
        return st.inc(g.dma_start(dst, src), 16)

    def din(name, shape):
        return nc.dram_tensor(name, list(shape), F32, kind="ExternalInput").ap()

    x_in = din("x", (D, S))
    cd = {}
    for name, shape in [
        ("COSA", (128, S)), ("SINA", (128, S)), ("COSB", (128, S)), ("SINB", (128, S)),
        ("CMASK", (128, 4 * B)), ("IDENT", (128, 128)), ("ONESF", (128, 128)),
        ("IND4", (128, 4)), ("IND4M", (128, 4)), ("IND4T", (4, 128)),
        ("IND8", (8, 8 * 128)),
        ("AQW", (D, D)), ("AKVW", (D, 2 * D)), ("AOW", (D, D)), ("ARW", (128, CH)),
        ("ARB", (1, 1)), ("AQBR", (1, D)), ("AKVBR", (1, 2 * D)),
        ("AOB", (128, CH)), ("AQNW", (128, CH)), ("AQNB", (128, CH)),
        ("AKVNW", (128, CH)), ("AKVNB", (128, CH)), ("ALN", (128, 4)),
        ("BQKV", (D, 3 * D)), ("BWO", (D, D)),
        ("FWQ", (D, D)), ("FWK", (D, 256)), ("FWV", (D, 256)), ("FWO", (D, D)),
        ("FWR", (D, 8)), ("FEB", (8, 1)), ("FG1", (128, CH)), ("FG2", (128, CH)),
        ("SW1", (D, D)), ("SW3", (D, D)), ("SW2", (D, D)),
        ("EW1", (8, D, D)), ("EW3", (8, D, D)), ("EW2", (8, D, D)),
    ]:
        cd[name] = din(name, shape)
    out_d = nc.dram_tensor("out", [D, S], F32, kind="ExternalOutput").ap()

    x_hbm = nc.dram_tensor("x_hbm", [D, S], F32).ap()
    qn_sp = nc.dram_tensor("qn_sp", [HEADS, HD, S], BF16).ap()
    kn_sp = nc.dram_tensor("kn_sp", [HEADS, HD, S], BF16).ap()
    v_sp = nc.dram_tensor("v_sp", [HEADS, NTC, 128, HD], BF16).ap()
    o_sp = nc.dram_tensor("o_sp", [D, S], BF16).ap()
    cq_hbm = nc.dram_tensor("cq_hbm", [HEADS, S], BF16).ap()
    xmid_hbm = nc.dram_tensor("xmid_hbm", [D, S], F32).ap()
    zf_hbm = nc.dram_tensor("zf_hbm", [D, S], BF16).ap()
    vrow_hbm = nc.dram_tensor("vrow_hbm", [16, S], F32).ap()
    q_sp = nc.dram_tensor("q_sp", [4, 256, S], BF16).ap()
    dbg_scratch = {"x_hbm": x_hbm, "qn_sp": qn_sp, "kn_sp": kn_sp, "v_sp": v_sp,
                   "o_sp": o_sp, "cq_hbm": cq_hbm, "xmid_hbm": xmid_hbm,
                   "zf_hbm": zf_hbm, "q_sp": q_sp, "vrow_hbm": vrow_hbm}

    sb = nc.alloc_sbuf_tensor
    IDE = sb("ide", [128, 128], F32)
    ONE = sb("one", [128, 128], F32)
    ONEB = sb("oneb", [128, 128], BF16)
    IND4 = sb("ind4", [128, 4], F32)
    IND4M = sb("ind4m", [128, 4], F32)
    IND4T = sb("ind4t", [36, 128], F32)
    IND4TB = sb("ind4tb", [4, 128], BF16)
    IND4B = sb("ind4b", [128, 4], BF16)
    IND4MB = sb("ind4mb", [128, 4], BF16)
    IND8B = sb("ind8b", [8, 8 * 128], BF16)
    ALN = sb("aln", [128, 4], F32)
    SVEC = sb("svec", [128, 16], F32)
    BIAS = sb("biasp", [128, 32], F32)
    S11 = sb("s11", [1, 8], F32)
    NVT = 16
    VT = sb("vtt", [128, NTC * NVT], F32)

    def vt(i):
        return VT[:, NTC * i:NTC * (i + 1)]

    PS2 = [nc.alloc_psum_tensor(f"ps2_{i}", [128, 1024], F32) for i in range(3)]
    PS1 = [nc.alloc_psum_tensor("ps1_0", [128, 512], F32),
           nc.alloc_psum_tensor("ps1_1", [128, 512], F32),
           PS2[2][:, 0:512], PS2[2][:, 512:1024]]

    for dst, name in [(IDE, "IDENT"), (ONE, "ONESF"), (IND4, "IND4"),
                      (IND4M, "IND4M"), (ALN, "ALN")]:
        load(dst[:], cd[name][:])
    load(IND4T[0:4, :], cd["IND4T"][:])
    load(IND4T[32:36, :], cd["IND4T"][:])
    store(IND4TB[:], cd["IND4T"][:])
    store(IND4B[:], cd["IND4"][:])
    store(IND4MB[:], cd["IND4M"][:])
    store(ONEB[:], cd["ONESF"][:])
    store(IND8B[:], cd["IND8"][:])
    load(x_hbm[:], x_in[:])
    sy.wait_ge(ld.h, ld.n)
    g.wait_ge(st.h, st.n)
    p.bar()

    VT_MU, VT_VAR, VT_RHO, VT_SC, VT_T1, VT_T2, VT_T3, VT_T4 = range(8)
    VT_X1, VT_X2, VT_X3 = 8, 9, 10


    def bcast_scalar(dst_col, src11):
        te.wait_ge(sec.h, sec.n)
        mm = te.matmul(PS1[3][:, 0:1], ONE[0:1, :], src11, start=True, stop=True)
        tv = tec.inc(mm)
        sc.wait_ge(tec.h, tv)
        sec.inc(sc.copy(dst_col, PS1[3][:, 0:1]))

    def rsqrt_ops(dst, src, eps, tmp1, tmp2, newton=True):
        v.tensor_scalar(tmp1, src, float(eps), None, ALU.add)
        nv = vec.inc(v.tensor_copy(tmp2, tmp1))
        sc.wait_ge(vec.h, nv)
        sc.activation(tmp2, tmp2, AF.Ln)
        ns = sec.inc(sc.activation(dst, tmp2, AF.Exp, scale=-0.5))
        v.wait_ge(sec.h, ns)
        if not newton:
            return v.engine_nop()
        v.tensor_mul(tmp2, dst, dst)
        v.tensor_mul(tmp2, tmp2, tmp1)
        v.tensor_scalar(tmp2, tmp2, -0.5, 1.5, ALU.mult, ALU.add)
        return v.tensor_mul(dst, dst, tmp2)

    vrb = p.ctr("vrb")

    def row_to_vt(slot, row_ap, lane=0):
        g.wait_ge(ld.h, ld.n)
        n = vrb.inc(g.dma_start(vrow_hbm[lane:lane + 1, :], row_ap), 16)
        sy.wait_ge(vrb.h, n)
        return load(vt(slot),
                    vrow_hbm[lane:lane + 1, :].rearrange(
                        "o (pp tc) -> (o pp) tc", tc=NTC))

    def rows_to_vt2(items):
        g.wait_ge(ld.h, ld.n)
        n = 0
        for slot_, row_ap, lane in items:
            n = vrb.inc(g.dma_start(vrow_hbm[lane:lane + 1, :], row_ap), 16)
        sy.wait_ge(vrb.h, n)
        for slot_, row_ap, lane in items:
            load(vt(slot_), vrow_hbm[lane:lane + 1, :].rearrange(
                "o (pp tc) -> (o pp) tc", tc=NTC))

    def vt_to_rows2(items):
        g.wait_ge(ld.h, ld.n)
        n = 0
        for row_ap, slot_, lane in items:
            n = vrb.inc(g.dma_start(vrow_hbm[lane:lane + 1, :].rearrange(
                "o (pp tc) -> (o pp) tc", tc=NTC), vt(slot_)), 16)
        sy.wait_ge(vrb.h, n)
        for row_ap, slot_, lane in items:
            load(row_ap, vrow_hbm[lane:lane + 1, :])

    def vt_to_row(row_ap, slot, lane=0):
        g.wait_ge(ld.h, ld.n)
        n = vrb.inc(g.dma_start(vrow_hbm[lane:lane + 1, :].rearrange(
            "o (pp tc) -> (o pp) tc", tc=NTC), vt(slot)), 16)
        sy.wait_ge(vrb.h, n)
        return load(row_ap, vrow_hbm[lane:lane + 1, :])

    def x_stats(src, alloc, rows_ready=False):
        if rows_ready:
            load(vt(VT_T1), vrow_hbm[0:1, :].rearrange(
                "o (pp tc) -> (o pp) tc", tc=NTC))
            load(vt(VT_T2), vrow_hbm[1:2, :].rearrange(
                "o (pp tc) -> (o pp) tc", tc=NTC))
            sy.wait_ge(ld.h, ld.n)
            p.bar()
            v.tensor_scalar(vt(VT_MU), vt(VT_T1), 1.0 / D, None, ALU.mult)
            v.tensor_scalar(vt(VT_T2), vt(VT_T2), 1.0 / D, None, ALU.mult)
            v.tensor_mul(vt(VT_T3), vt(VT_MU), vt(VT_MU))
            v.tensor_sub(vt(VT_VAR), vt(VT_T2), vt(VT_T3))
            rsqrt_ops(vt(VT_RHO), vt(VT_VAR), 1e-5, vt(VT_T1), vt(VT_T2))
            p.bar()
            return
        XT = [alloc((128, S), F32) for _ in range(2)]
        SQT = [alloc((128, B), F32) for _ in range(2)]
        SROW = alloc((1, S), F32)
        QROW = alloc((1, S), F32)
        xp, sqp = Pool(XT, p, "xr"), Pool(SQT)
        BKS = [PS1[0], PS1[1], PS2[0][:, 0:B], PS2[0][:, B:2 * B],
               PS2[1][:, 0:B], PS2[1][:, B:2 * B], PS1[2], PS1[3]]
        sq_mm = [0, 0]
        for c in range(CH):
            xb = xp.get(sy)
            n_ld = xb.ctr.inc(sy.dma_start(xb.ap[:], src[c * 128:(c + 1) * 128, :]), 16)
            sc.wait_ge(xb.ctr.h, n_ld)
            te.wait_ge(xb.ctr.h, n_ld)
            for b in range(NB):
                blk = slice(b * B, (b + 1) * B)
                sq = sqp.get(sc)
                sc.wait_ge(tec.h, sq_mm[b % 2])
                sv = sec.inc(sc.activation(sq.ap[:], xb.ap[:, blk], AF.Square))
                te.matmul(BKS[b][0:1, 0:B], ONE[:, 0:1], xb.ap[:, blk],
                          start=(c == 0), stop=(c == CH - 1))
                te.wait_ge(sec.h, sv)
                mm = te.matmul(BKS[b][32:33, 0:B], ONE[:, 0:1], sq.ap[:],
                               start=(c == 0), stop=(c == CH - 1))
                sq_mm[b % 2] = tec.inc(mm)
                if c == CH - 1:
                    v.wait_ge(tec.h, sq_mm[b % 2])
                    v.tensor_copy(SROW[:, blk], BKS[b][0:1, 0:B])
                    vec.inc(v.tensor_copy(QROW[:, blk], BKS[b][32:33, 0:B]))
            xb.release(tec, tec.n)
            xb.release(sec, sec.n)
        p.bar()
        rows_to_vt2([(VT_T1, SROW[:], 0), (VT_T2, QROW[:], 1)])
        sy.wait_ge(ld.h, ld.n)
        p.bar()
        v.tensor_scalar(vt(VT_MU), vt(VT_T1), 1.0 / D, None, ALU.mult)
        v.tensor_scalar(vt(VT_T2), vt(VT_T2), 1.0 / D, None, ALU.mult)
        v.tensor_mul(vt(VT_T3), vt(VT_MU), vt(VT_MU))
        v.tensor_sub(vt(VT_VAR), vt(VT_T2), vt(VT_T3))
        rsqrt_ops(vt(VT_RHO), vt(VT_VAR), 1e-5, vt(VT_T1), vt(VT_T2))
        p.bar()

    def build_norm_operand(YH, scale_slot, src, alloc, gcol=None):
        MUB = alloc((128, S), F32)
        SCB = alloc((128, S), F32)
        with p.phase() as alloc_r:
            MROW = alloc_r((1, S), F32)
            CROW = alloc_r((1, S), F32)
            vt_to_rows2([(MROW[:], VT_MU, 0), (CROW[:], scale_slot, 1)])
            sy.wait_ge(ld.h, ld.n)
            p.bar()
            bp = Pool([ps[:, 0:B] for ps in PS1])
            for b in range(NB):
                blk = slice(b * B, (b + 1) * B)
                bm = bp.get(te)
                bs = bp.get(te)
                te.matmul(bm.ap, ONE[0:1, :], MROW[:, blk], start=True, stop=True)
                mm = te.matmul(bs.ap, ONE[0:1, :], CROW[:, blk], start=True, stop=True)
                tv = tec.inc(mm)
                sc.wait_ge(tec.h, tv)
                sc.copy(MUB[:, blk], bm.ap)
                ss = sec.inc(sc.copy(SCB[:, blk], bs.ap))
                bm.release(sec, ss)
                bs.release(sec, ss)
            p.bar()
        XT = [alloc((128, S), F32) for _ in range(2)]
        STG = alloc((128, B), F32)
        STGP = alloc((128, B), F32)
        xp = Pool(XT, p, "xr")
        for c in range(CH):
            xb = xp.get(sy)
            n_ld = xb.ctr.inc(sy.dma_start(xb.ap[:], src[c * 128:(c + 1) * 128, :]), 16)
            e, scr = (v, STG) if c % 8 < 7 else (g, STGP)
            ec = vec if c % 8 < 7 else pec
            for b in range(NB):
                blk = slice(b * B, (b + 1) * B)
                e.wait_ge(xb.ctr.h, n_ld)
                e.tensor_sub(scr[:], xb.ap[:, blk], MUB[:, blk])
                if gcol is None:
                    ins = e.tensor_mul(YH[c][:, blk], scr[:], SCB[:, blk])
                else:
                    e.tensor_mul(scr[:], scr[:], SCB[:, blk])
                    ins = e.tensor_scalar(YH[c][:, blk], scr[:],
                                          gcol[:, c:c + 1], None, ALU.mult)
                ec.inc(ins)
            xb.release(ec, ec.n)
        p.bar()

    def prep_weight_cast(dst_bf, w_dram, col0, ncols):
        for c in range(CH):
            store(dst_bf[:, c * ncols:(c + 1) * ncols],
                  w_dram[c * 128:(c + 1) * 128, col0:col0 + ncols])
        return st.n

    def prep_weight_scaled(dst_bf, w_dram, col0, ncols, rowscale, wst_pool,
                           bias_rhs=None, bias_psum=None):
        if bias_rhs is not None:
            te.wait_ge(vec.h, vec.n)
        for c in range(CH):
            wb = wst_pool.get(sy)
            n_ld = wb.ctr.inc(sy.dma_start(
                wb.ap[:, 0:ncols],
                w_dram[c * 128:(c + 1) * 128, col0:col0 + ncols]), 16)
            if bias_rhs is not None:
                te.wait_ge(wb.ctr.h, n_ld)
                off = 0
                mm = None
                for psl in bias_psum:
                    w_ = min(512, ncols - off)
                    mm = te.matmul(psl[:, 0:w_], bias_rhs[:, c:c + 1],
                                   wb.ap[:, off:off + w_],
                                   start=(c == 0), stop=(c == CH - 1))
                    off += w_
                wb.release(tec, tec.inc(mm))
            v.wait_ge(wb.ctr.h, n_ld)
            ins = v.tensor_scalar(dst_bf[:, c * ncols:(c + 1) * ncols],
                                  wb.ap[:, 0:ncols], rowscale[:, c:c + 1],
                                  None, ALU.mult)
            wb.release(vec, vec.inc(ins))

    def attn_core(causal):
        with p.phase() as alloc:
            KNB = [alloc((65, S), BF16) for _ in range(2)]
            QNB = [alloc((65, S), BF16) for _ in range(2)]
            VBUF = [alloc((128, 65 * NTC), BF16) for _ in range(2)]
            PT = alloc((128, NTC * B), BF16)
            OH = [alloc((64, S), BF16) for _ in range(2)]
            RDNR = alloc((1, S), F32)
            RDNB2 = alloc((1, S), BF16)
            if causal:
                CMB2 = alloc((128, 4 * B), BF16)
                store(CMB2[:], cd["CMASK"][:])
            for t_ in KNB + QNB:
                g.memset(t_[64:65, :], 1.0)
            for t_ in VBUF:
                g.memset(t_[:], 0.0)
            g.wait_ge(st.h, st.n)
            p.bar()
            for t_ in VBUF:
                for j in range(NTC):
                    g.memset(t_[:, 65 * j + 64:65 * j + 65], 1.0)
            p.bar()
            knp = Pool(KNB, p, "knp")
            qnp = Pool(QNB, p, "qnp")
            vbp = Pool(VBUF, p, "vbp")
            ohp = Pool(OH, p, "ohp")
            SLOT3 = [PS2[0][:, 0:2 * B], PS2[1][:, 0:2 * B], PS2[2][:, 0:2 * B]]
            ACCF = [PS1[0], PS1[1]]
            st8 = {"evac": 0, "bcst": 0, "guard": [0, 0], "gg": 0,
                   "slot_guard": [0, 0, 0]}
            todo = []
            exp_val = {}

            def mk_recip(qsl, ev, nxt):
                def fn():
                    v.reciprocal(RDNR[0:1, qsl], RDNR[0:1, qsl])
                    rdnb = vec.inc(v.tensor_copy(RDNB2[0:1, qsl], RDNR[0:1, qsl]))
                    nxt(rdnb)
                return fn

            def mk_norm(ob, h, qblk, gready):
                def sched(rdnb):
                    def fn():
                        nql = slice(qblk * B, (qblk + 1) * B)
                        gnow = st8["gg"]
                        t_ = (gnow + 2) % 3
                        gl = next(gnow - d for d in range(3)
                                  if (gnow - d) % 3 == t_)
                        te.wait_ge(vec.h, max(rdnb, st8["bcst"]))
                        if gl in exp_val:
                            te.wait_ge(sec.h, exp_val[gl])
                        mm2 = te.matmul(SLOT3[t_][64:128, 0:B], ONEB[0:1, 0:64],
                                        RDNB2[0:1, nql], start=True, stop=True)
                        tv2 = tec.inc(mm2)
                        v.wait_ge(tec.h, tv2)
                        ins = v.tensor_mul(ob.ap[:, nql], ob.ap[:, nql],
                                           SLOT3[t_][64:128, 0:B])
                        st8["bcst"] = st8["slot_guard"][t_] = vec.inc(ins)
                        if qblk == NB - 1:
                            todo.append((gready + 2, mk_store(ob, h)))
                    todo.append((gready + 2, fn))
                return sched

            def mk_store(ob, h):
                def fn():
                    g.wait_ge(vec.h, st8["bcst"])
                    n_st = ob.ctr.inc(g.dma_start(
                        o_sp[h * 64:(h + 1) * 64, :], ob.ap[:]), 16)
                    ob.release(ob.ctr, n_st)
                return fn

            for h in range(HEADS):
                kb = knp.get(sy)
                nk = kb.ctr.inc(sy.dma_start(kb.ap[0:64, :], kn_sp[h, :, :]), 16)
                qb = qnp.get(sy)
                qb.ctr.inc(sy.dma_start(qb.ap[0:64, :], qn_sp[h, :, :]), 16)
                nq = qb.ctr.inc(sy.dma_start(qb.ap[64:65, :], cq_hbm[h:h + 1, :]), 16)
                vb = vbp.get(sy)
                nv3 = vb.ctr.inc(sy.dma_start(
                    vb.ap.rearrange("pp (tc cc) -> pp tc cc", cc=65)[:, :, 0:64],
                    v_sp[h, :, :, :].rearrange("tc pp dd -> pp tc dd")), 16)
                te.wait_ge(kb.ctr.h, nk)
                te.wait_ge(qb.ctr.h, nq)
                te.wait_ge(vb.ctr.h, nv3)
                ob = ohp.get(v)
                groups = []
                for qblk in range(NB):
                    ntc_v = 4 * (qblk + 1) if causal else NTC
                    for gi in range(ntc_v // 2):
                        groups.append((qblk, gi, ntc_v))
                ggs = []
                mask_done = {}
                maskw = {}

                def emit_av_pair(idx):
                    qblk, gi, ntc_v = groups[idx]
                    acc_i = qblk % 2
                    qsl = slice(qblk * B, (qblk + 1) * B)
                    gg = ggs[idx]
                    for k2 in range(2):
                        tc = 2 * gi + k2
                        if (qblk, tc) in mask_done:
                            te.wait_ge(pec.h, mask_done[(qblk, tc)])
                        else:
                            te.wait_ge(sec.h, exp_val[gg])
                        if tc == 0:
                            te.wait_ge(vec.h, st8["guard"][acc_i])
                        mm = te.matmul(ACCF[acc_i][0:65, 0:B],
                                       vb.ap[:, 65 * tc:65 * tc + 65],
                                       PT[:, tc * B:(tc + 1) * B],
                                       start=(tc == 0), stop=(tc == ntc_v - 1))
                        if tc == ntc_v - 1:
                            tv = tec.inc(mm)
                            v.wait_ge(tec.h, tv)
                            v.tensor_copy(ob.ap[:, qsl], ACCF[acc_i][0:64, 0:B])
                            ins = v.tensor_copy(RDNR[0:1, qsl],
                                                ACCF[acc_i][64:65, 0:B])
                            st8["evac"] = st8["guard"][acc_i] = vec.inc(ins)
                            gnow = st8["gg"]
                            todo.append((gnow + 2, mk_recip(
                                qsl, st8["evac"], mk_norm(ob, h, qblk, gnow + 2))))

                for idx, (qblk, gi, ntc_v) in enumerate(groups):
                    gg = st8["gg"]
                    st8["gg"] += 1
                    ggs.append(gg)
                    slot = SLOT3[gg % 3]
                    if gg >= 3:
                        te.wait_ge(sec.h, exp_val[gg - 3])
                    sg = st8["slot_guard"][gg % 3]
                    if sg:
                        te.wait_ge(vec.h, sg)
                        st8["slot_guard"][gg % 3] = 0
                    qsl = slice(qblk * B, (qblk + 1) * B)
                    for k2 in range(2):
                        tc = 2 * gi + k2
                        mm = te.matmul(slot[:, B * k2:B * (k2 + 1)],
                                       kb.ap[:, tc * 128:(tc + 1) * 128],
                                       qb.ap[:, qsl], start=True, stop=True)
                    tv = tec.inc(mm)
                    mw = max(maskw.get(2 * gi, 0), maskw.get(2 * gi + 1, 0))
                    if mw:
                        sc.wait_ge(pec.h, mw)
                    sc.wait_ge(tec.h, tv)
                    ins = sc.activation(PT[:, 2 * gi * B:(2 * gi + 2) * B],
                                        slot[:, 0:2 * B], AF.Exp)
                    exp_val[gg] = sec.inc(ins)
                    if causal:
                        for k2 in range(2):
                            tc = 2 * gi + k2
                            jm = tc - 4 * qblk
                            if jm >= 0:
                                g.wait_ge(sec.h, exp_val[gg])
                                insm = g.tensor_mul(
                                    PT[:, tc * B:(tc + 1) * B],
                                    PT[:, tc * B:(tc + 1) * B],
                                    CMB2[:, jm * B:(jm + 1) * B])
                                mask_done[(qblk, tc)] = maskw[tc] = pec.inc(insm)
                    if idx >= 2:
                        emit_av_pair(idx - 2)
                    while todo and todo[0][0] <= st8["gg"]:
                        todo.pop(0)[1]()
                emit_av_pair(len(groups) - 2)
                emit_av_pair(len(groups) - 1)
                kb.release(tec, tec.n)
                qb.release(tec, tec.n)
                vb.release(tec, tec.n)
            while todo:
                todo.pop(0)[1]()
            for b_ in ohp.bufs:
                g.wait_ge(b_.ctr.h, b_.ctr.n)
            p.bar()


    def attn_a(is_last):
        with p.phase() as alloc:
            x_stats(x_hbm, alloc)
        if debug_dump == "vrow_hbm":
            n = vrb.inc(g.dma_start(vrow_hbm[13:15, :], vrow_hbm[0:2, :]), 16)
            g.wait_ge(vrb.h, n)
            p.bar()
        v.tensor_mul(vt(VT_T1), vt(VT_RHO), vt(VT_RHO))
        v.tensor_mul(vt(VT_T1), vt(VT_T1), vt(VT_VAR))
        rsqrt_ops(vt(VT_T2), vt(VT_T1), 1e-5, vt(VT_T3), vt(VT_T4))
        v.tensor_mul(vt(VT_SC), vt(VT_RHO), vt(VT_T2))
        v.reciprocal(vt(VT_X1), vt(VT_T2))
        p.bar()
        if debug_dump == "vrow_hbm":
            for lane_, slot_ in ((8, VT_MU), (9, VT_VAR), (10, VT_RHO), (11, VT_SC)):
                g.wait_ge(ld.h, ld.n)
                nn = vrb.inc(g.dma_start(vrow_hbm[lane_:lane_ + 1, :].rearrange(
                    "o (pp tc) -> (o pp) tc", tc=NTC), vt(slot_)), 16)
            g.wait_ge(vrb.h, vrb.n)
            p.bar()

        with p.phase() as alloc_o:
            YH = [alloc_o((128, S), BF16) for _ in range(CH)]
            with p.phase() as alloc:
                build_norm_operand(YH, VT_SC, x_hbm, alloc)
            if debug_dump == "zf_hbm":
                for c in range(CH):
                    store(zf_hbm[c * 128:(c + 1) * 128, :], YH[c][:])
                g.wait_ge(st.h, st.n)
                p.bar()

            S4ALL = alloc_o((4, S), BF16)
            n = load(S11[0:1, 0:1], cd["ARB"][:])
            sy.wait_ge(ld.h, n)
            p.bar()
            bcast_scalar(SVEC[:, 0:1], S11[0:1, 0:1])
            with p.phase() as alloc:
                ARWS = alloc((128, CH), F32)
                ARWB = alloc((128, CH), BF16)
                PROW = alloc((1, S), F32)
                SROW = alloc((1, S), F32)
                n = load(ARWS[:], cd["ARW"][:])
                v.wait_ge(ld.h, n)
                v.tensor_copy(ARWB[:], ARWS[:])
                g.memset(PROW[:], 0.0)
                p.bar()
                bp = Pool([ps[0:1, 0:B] for ps in PS1])
                for c in range(CH):
                    for b in range(NB):
                        blk = slice(b * B, (b + 1) * B)
                        bank = bp.get(te)
                        mm = te.matmul(bank.ap, ARWB[:, c:c + 1], YH[c][:, blk],
                                       start=True, stop=True)
                        tv = tec.inc(mm)
                        v.wait_ge(tec.h, tv)
                        ins = v.tensor_add(PROW[:, blk], PROW[:, blk], bank.ap)
                        bank.release(vec, vec.inc(ins))
                p.bar()
                row_to_vt(VT_T1, PROW[:])
                sy.wait_ge(ld.h, ld.n)
                p.bar()
                v.tensor_mul(vt(VT_T1), vt(VT_T1), vt(VT_X1))
                v.tensor_scalar(vt(VT_T1), vt(VT_T1), SVEC[:, 0:1], None, ALU.add)
                v.tensor_scalar(vt(VT_T1), vt(VT_T1), 0.01, None, ALU.mult)
                v.tensor_scalar(vt(VT_T1), vt(VT_T1), 0.5, None, ALU.max)
                v.tensor_scalar(vt(VT_X2), vt(VT_T1), 2.0, None, ALU.min)
                p.bar()
                vt_to_row(SROW[:], VT_X2)
                sy.wait_ge(ld.h, ld.n)
                p.bar()
                for b in range(NB):
                    blk = slice(b * B, (b + 1) * B)
                    te.wait_ge(sec.h, sec.n)
                    mm = te.matmul(PS1[0][0:4, 0:B], ONE[0:1, 0:4], SROW[:, blk],
                                   start=True, stop=True)
                    tv = tec.inc(mm)
                    sc.wait_ge(tec.h, tv)
                    sec.inc(sc.copy(S4ALL[:, blk], PS1[0][0:4, 0:B]))
                p.bar()
            v.tensor_scalar(BIAS[:, 0:4], ALN[:, 0:4], 0.125, None, ALU.mult)
            g.memset(BIAS[0:4, 16:20], 0.0)
            p.bar()

            def head_groups(is_q):
                wsrc = cd["AQW"] if is_q else cd["AKVW"]
                nw = cd["AQNW"] if is_q else cd["AKVNW"]
                nb_ = cd["AQNB"] if is_q else cd["AKVNB"]
                br = cd["AQBR"] if is_q else cd["AKVBR"]
                spill = qn_sp if is_q else kn_sp
                lnsrc = BIAS if is_q else ALN
                with p.phase() as alloc:
                    NWS = alloc((128, CH), F32)
                    NBS = alloc((128, CH), F32)
                    BRS = [alloc((1, 256), F32) for _ in range(2)]
                    TCOS = alloc((128, S), BF16)
                    TSIN = alloc((128, S), BF16)
                    RA = alloc((128, S), BF16)
                    RB = alloc((128, S), BF16)
                    STG = [alloc((128, 2 * B), BF16) for _ in range(2)]
                    SQP = [alloc((128, B), BF16) for _ in range(2)]
                    SQ2P = [alloc((128, B), BF16) for _ in range(2)]
                    WQG = [alloc((128, 2048), BF16) for _ in range(2)]
                    WST = [alloc((128, 256), F32) for _ in range(2)]
                    QNE = alloc((128, S), BF16)
                    QNO = alloc((128, S), BF16)
                    BCS = [alloc((128, 2 * B), BF16) for _ in range(2)]
                    M0R = [alloc((4, S), BF16) for _ in range(2)]
                    GR = [alloc((4, S), BF16) for _ in range(2)]
                    CR = alloc((4, S), BF16)
                    SB = alloc((4, 2 * B), F32)

                    def sbt(par):
                        return SB[:, par * B:(par + 1) * B]

                    load(NWS[:], nw[:])
                    load(NBS[:], nb_[:])
                    store(TCOS[:], cd["COSA"][:])
                    store(TSIN[:], cd["SINA"][:])
                    sy.wait_ge(ld.h, ld.n)
                    g.wait_ge(st.h, st.n)
                    p.bar()
                    PJ = [PS2[1][:, 0:B], PS2[1][:, B:2 * B]]
                    P0 = [PS2[0][:, 0:B], PS2[0][:, B:2 * B]]
                    BC0 = [PS1[0][:, 0:B], PS1[1][:, 0:B]]
                    BC1 = [PS1[2][:, 0:B], PS1[3][:, 0:B]]
                    pjp = Pool(PJ)
                    gd = {"bias_ld": 0, "spill_st": 0, "cq_st": 0, "cr_free": 0,
                          "apply": [0] * NB, "applyp": [0] * NB,
                          "rot": [0] * NB, "rotp": [0] * NB,
                          "proj": [0, 0], "bc": [0, 0], "bcp": [0, 0],
                          "bcv": [0, 0], "sq_read": [0, 0],
                          "p0v": [0, 0], "p0c": [0, 0],
                          "gr_done": [0, 0], "m0_done": [0, 0],
                          "gr_read": [0, 0]}

                    def prep_chunk(gi, c):
                        par = gi % 2
                        wb = wstp.get(sy)
                        n_ld = wb.ctr.inc(sy.dma_start(
                            wb.ap[:, 0:256],
                            wsrc[c * 128:(c + 1) * 128,
                                 256 * gi:256 * gi + 256]), 16)
                        if c == 0:
                            n_br = load(BRS[par][:],
                                        br[:, 256 * gi:256 * gi + 256])
                            gd["brld"] = n_br
                        bank = pjp.get(te)
                        te.wait_ge(wb.ctr.h, n_ld)
                        mm = te.matmul(bank.ap[0:1, 0:256], NBS[:, c:c + 1],
                                       wb.ap[:, 0:256], start=True, stop=True)
                        tv = tec.inc(mm)
                        v.wait_ge(tec.h, tv)
                        if c == 0:
                            v.wait_ge(ld.h, gd["brld"])
                        ins = v.tensor_add(BRS[par][:], BRS[par][:],
                                           bank.ap[0:1, 0:256])
                        bank.release(vec, vec.inc(ins))
                        v.wait_ge(wb.ctr.h, n_ld)
                        if c == 0:
                            v.wait_ge(tec.h, gd["proj"][par])
                        ins = v.tensor_scalar(
                            WQG[par][:, c * 256:(c + 1) * 256],
                            wb.ap[:, 0:256], NWS[:, c:c + 1], None, ALU.mult)
                        wb.release(vec, vec.inc(ins))

                    def prep_finish(gi):
                        par = gi % 2
                        g.wait_ge(vec.h, vec.n)
                        lane = 2 + par
                        n1 = vrb.inc(g.dma_start(
                            vrow_hbm[lane:lane + 1, 0:256], BRS[par][:]), 16)
                        sy.wait_ge(vrb.h, n1)
                        load(BIAS[:, 8 + 2 * par:9 + 2 * par],
                             vrow_hbm[lane:lane + 1, 0:128].rearrange("a b -> b a"))
                        gd["bias_ld"] = load(
                            BIAS[:, 9 + 2 * par:10 + 2 * par],
                            vrow_hbm[lane:lane + 1, 128:256].rearrange("a b -> b a"))

                    def proj_block(gi, b):
                        par = gi % 2
                        blk = slice(b * B, (b + 1) * B)
                        for half, dst in ((0, RA), (1, RB)):
                            bank = pjp.get(te)
                            for c in range(CH):
                                mm = te.matmul(
                                    bank.ap,
                                    WQG[par][:, c * 256 + 128 * half:
                                              c * 256 + 128 * (half + 1)],
                                    YH[c][:, blk], start=(c == 0),
                                    stop=(c == CH - 1))
                            tv = tec.inc(mm)
                            if b == 0 and half == 0:
                                sc.wait_ge(ld.h, gd["bias_ld"])
                            sc.wait_ge(tec.h, tv)
                            if half == 0:
                                sc.wait_ge(vec.h, gd["apply"][b])
                                sc.wait_ge(pec.h, gd["applyp"][b])
                            ins = sc.activation(
                                dst[:, blk], bank.ap, AF.Identity,
                                bias=BIAS[:, 8 + 2 * par + half:
                                          9 + 2 * par + half])
                            bank.release(sec, sec.inc(ins))
                        gd["proj"][par] = tec.n
                        return sec.n

                    def rotary_block(b, bias_done):
                        par = b % 2
                        blk = slice(b * B, (b + 1) * B)
                        v.wait_ge(sec.h, bias_done)
                        v.wait_ge(tec.h, gd["sq_read"][par])
                        v.tensor_mul(SQP[par][:], RA[:, blk], TCOS[:, blk])
                        v.tensor_mul(SQ2P[par][:], RB[:, blk], TSIN[:, blk])
                        g.wait_ge(sec.h, bias_done)
                        g.wait_ge(vec.h, gd["apply"][b])
                        g.tensor_mul(STG[par][:, 0:B], RA[:, blk], TSIN[:, blk])
                        ins = g.tensor_mul(STG[par][:, B:2 * B], RB[:, blk],
                                           TCOS[:, blk])
                        gr = pec.inc(ins)
                        v.wait_ge(pec.h, gr)
                        ins = v.tensor_sub(RA[:, blk], SQP[par][:], SQ2P[par][:])
                        gd["rot"][b] = vec.inc(ins)
                        g.wait_ge(vec.h, gd["rot"][b])
                        ins = g.tensor_add(RB[:, blk], STG[par][:, 0:B],
                                           STG[par][:, B:2 * B])
                        gd["rotp"][b] = pec.inc(ins)

                    def stats_a(gi, b):
                        par = b % 2
                        gpar = gi % 2
                        blk = slice(b * B, (b + 1) * B)
                        sc.wait_ge(vec.h, gd["rot"][b])
                        sc.wait_ge(pec.h, gd["rotp"][b])
                        sc.wait_ge(tec.h, gd["sq_read"][par])
                        sc.activation(SQP[par][:], RA[:, blk], AF.Square)
                        s2 = sec.inc(sc.activation(SQ2P[par][:], RB[:, blk],
                                                   AF.Square))
                        te.wait_ge(sec.h, max(s2, gd["p0c"][par]))
                        te.wait_ge(vec.h, gd["p0v"][par])
                        te.matmul(P0[par][0:4, 0:B], IND4MB[:], RA[:, blk],
                                  start=True, stop=False)
                        te.matmul(P0[par][0:4, 0:B], IND4MB[:], RB[:, blk],
                                  start=False, stop=True)
                        te.matmul(P0[par][32:36, 0:B], IND4MB[:], SQP[par][:],
                                  start=True, stop=False)
                        mm = te.matmul(P0[par][32:36, 0:B], IND4MB[:],
                                       SQ2P[par][:], start=False, stop=True)
                        tv = tec.inc(mm)
                        gd["sq_read"][par] = tv
                        sc.wait_ge(tec.h, max(tv, gd["gr_read"][gpar]))
                        m0c = sec.inc(sc.copy(M0R[gpar][:, blk],
                                              P0[par][0:4, 0:B]))
                        gd["m0_done"][gpar] = m0c
                        v.wait_ge(sec.h, m0c)
                        v.wait_ge(tec.h, gd["gr_read"][gpar])
                        v.tensor_mul(sbt(par), M0R[gpar][:, blk],
                                     M0R[gpar][:, blk])
                        v.tensor_sub(sbt(par), P0[par][32:36, 0:B], sbt(par))
                        v.tensor_mul(sbt(par), sbt(par), S4ALL[:, blk])
                        v.tensor_mul(sbt(par), sbt(par), S4ALL[:, blk])
                        ins = v.tensor_scalar(GR[gpar][:, blk], sbt(par), 1e-5,
                                              None, ALU.add)
                        gd["p0v"][par] = vec.inc(ins)

                    def derive_mid(gi):
                        gpar = gi % 2
                        sc.wait_ge(vec.h, gd["p0v"][0])
                        sc.wait_ge(vec.h, gd["p0v"][1])
                        sc.activation(GR[gpar][:], GR[gpar][:], AF.Ln)
                        lg = sec.inc(sc.activation(GR[gpar][:], GR[gpar][:],
                                                   AF.Exp, scale=-0.5))
                        v.wait_ge(sec.h, lg)
                        gd["gr_done"][gpar] = vec.inc(
                            v.tensor_mul(GR[gpar][:], GR[gpar][:], S4ALL[:]))

                    def apply_block(gi, b):
                        par = b % 2
                        gpar = gi % 2
                        blk = slice(b * B, (b + 1) * B)
                        te.wait_ge(sec.h, gd["bc"][par])
                        if b == 0:
                            te.wait_ge(vec.h, gd["gr_done"][gpar])
                            te.wait_ge(sec.h, gd["m0_done"][gpar])
                        te.matmul(BC0[par], IND4TB[:], M0R[gpar][:, blk],
                                  start=True, stop=True)
                        mm = te.matmul(BC1[par], IND4TB[:], GR[gpar][:, blk],
                                       start=True, stop=True)
                        tv = tec.inc(mm)
                        sc.wait_ge(tec.h, tv)
                        sc.wait_ge(vec.h, gd["bcv"][par])
                        sc.wait_ge(pec.h, gd["bcp"][par])
                        sc.copy(BCS[par][:, 0:B], BC0[par])
                        cpd = sec.inc(sc.copy(BCS[par][:, B:2 * B], BC1[par]))
                        gd["bc"][par] = cpd
                        v.wait_ge(sec.h, cpd)
                        v.wait_ge(st.h, gd["spill_st"])
                        v.wait_ge(pec.h, gd["rotp"][b])
                        v.tensor_sub(STG[par][:, 0:B], RA[:, blk],
                                     BCS[par][:, 0:B])
                        v.tensor_mul(STG[par][:, 0:B], STG[par][:, 0:B],
                                     BCS[par][:, B:2 * B])
                        ins = v.tensor_scalar(QNE[:, blk], STG[par][:, 0:B],
                                              lnsrc[:, 0:1], lnsrc[:, 1:2],
                                              ALU.mult, ALU.add)
                        gd["apply"][b] = gd["bcv"][par] = vec.inc(ins)
                        g.wait_ge(sec.h, cpd)
                        g.wait_ge(st.h, gd["spill_st"])
                        g.tensor_sub(STG[par][:, B:2 * B], RB[:, blk],
                                     BCS[par][:, 0:B])
                        g.tensor_mul(STG[par][:, B:2 * B], STG[par][:, B:2 * B],
                                     BCS[par][:, B:2 * B])
                        ins = g.tensor_scalar(QNO[:, blk], STG[par][:, B:2 * B],
                                              lnsrc[:, 2:3], lnsrc[:, 3:4],
                                              ALU.mult, ALU.add)
                        gd["applyp"][b] = gd["bcp"][par] = pec.inc(ins)
                        sc.wait_ge(vec.h, gd["apply"][b])
                        sc.wait_ge(pec.h, gd["applyp"][b])
                        sc.wait_ge(tec.h, gd["sq_read"][par])
                        sc.activation(SQP[par][:], QNE[:, blk], AF.Square)
                        s2 = sec.inc(sc.activation(SQ2P[par][:], QNO[:, blk],
                                                   AF.Square))
                        te.wait_ge(sec.h, max(s2, gd["p0c"][par]))
                        te.matmul(P0[par][64:68, 0:B], IND4B[:], SQP[par][:],
                                  start=True, stop=False)
                        mm = te.matmul(P0[par][64:68, 0:B], IND4B[:],
                                       SQ2P[par][:], start=False, stop=True)
                        tv = tec.inc(mm)
                        gd["sq_read"][par] = tv
                        sc.wait_ge(tec.h, tv)
                        sc.wait_ge(st.h, gd["cq_st"])
                        sc.wait_ge(vec.h, gd["cr_free"])
                        gd["p0c"][par] = sec.inc(
                            sc.copy(CR[:, blk], P0[par][64:68, 0:B]))

                    def d_tail(gi):
                        if is_q:
                            sq = sec.inc(sc.activation(CR[:], CR[:], AF.Sqrt))
                            v.wait_ge(sec.h, sq)
                            ins = v.tensor_scalar(CR[:], CR[:],
                                                  SVEC[0:4, 4 + gi:5 + gi],
                                                  None, ALU.mult)
                            gd["cr_free"] = vec.inc(ins)
                            g.wait_ge(vec.h, gd["cr_free"])
                            for h4 in range(4):
                                store(cq_hbm[4 * gi + h4:4 * gi + h4 + 1, :],
                                      CR[h4:h4 + 1, :])
                            gd["cq_st"] = st.n
                        else:
                            v.wait_ge(sec.h, max(gd["p0c"]))
                            v.tensor_reduce(sbt(0)[:, 0:1], CR[:], AX.X,
                                            ALU.max)
                            ins = v.tensor_max(BIAS[0:4, 16 + gi:17 + gi],
                                               BIAS[0:4, 16 + gi:17 + gi],
                                               sbt(0)[:, 0:1])
                            gd["cr_free"] = vec.inc(ins)
                        g.wait_ge(vec.h, gd["apply"][NB - 1])
                        g.wait_ge(pec.h, gd["applyp"][NB - 1])
                        for h4 in range(4):
                            rs = slice(32 * h4, 32 * (h4 + 1))
                            store(spill[4 * gi + h4, 0:32, :], QNE[rs, :])
                            store(spill[4 * gi + h4, 32:64, :], QNO[rs, :])
                        gd["spill_st"] = st.n

                    wstp = Pool(WST, p, "wst")
                    for c in range(CH):
                        prep_chunk(0, c)
                    prep_finish(0)
                    for b in range(NB):
                        bias_done = proj_block(0, b)
                        rotary_block(b, bias_done)
                        if NG > 1:
                            prep_chunk(1, b)
                        if b >= 1:
                            stats_a(0, b - 1)
                    stats_a(0, NB - 1)
                    derive_mid(0)
                    if NG > 1:
                        prep_finish(1)
                    for gi in range(NG):
                        for b in range(NB):
                            apply_block(gi, b)
                            if gi + 1 < NG:
                                bias_done = proj_block(gi + 1, b)
                                rotary_block(b, bias_done)
                                if gi + 2 < NG:
                                    prep_chunk(gi + 2, b)
                                if b >= 1:
                                    stats_a(gi + 1, b - 1)
                        gd["gr_read"][gi % 2] = tec.n
                        if gi + 1 < NG:
                            stats_a(gi + 1, NB - 1)
                            derive_mid(gi + 1)
                            if gi + 2 < NG:
                                prep_finish(gi + 2)
                        d_tail(gi)
                    g.wait_ge(st.h, st.n)
                    p.bar()
                    if not is_q:
                        ss = sec.inc(sc.activation(BIAS[0:4, 20:24],
                                                   BIAS[0:4, 16:20], AF.Sqrt))
                        v.wait_ge(sec.h, ss)
                        v.tensor_scalar(SVEC[0:4, 4:8], BIAS[0:4, 20:24],
                                        -1.0, None, ALU.mult)
                        p.bar()

            head_groups(is_q=False)
            head_groups(is_q=True)

            with p.phase() as alloc:
                WVB = alloc((128, CH * 1024), BF16)
                WST = [alloc((128, 1024), F32) for _ in range(2)]
                NWS = alloc((128, CH), F32)
                NBS = alloc((128, CH), F32)
                KVBR = alloc((1, 1024), F32)
                BVB = alloc((1, 1024), BF16)
                VST = [alloc((128, 512), BF16) for _ in range(2)]
                load(NWS[:], cd["AKVNW"][:])
                load(NBS[:], cd["AKVNB"][:])
                n3 = load(KVBR[:], cd["AKVBR"][:, 1024:2048])
                sy.wait_ge(ld.h, ld.n)
                p.bar()
                wstp = Pool(WST, p, "wst")
                prep_weight_scaled(WVB, cd["AKVW"], 1024, 1024, NWS[:], wstp,
                                   bias_rhs=NBS[:],
                                   bias_psum=[PS2[1][0:1, 0:512], PS2[1][0:1, 512:1024]])
                tv = tec.n
                v.wait_ge(tec.h, tv)
                v.tensor_add(KVBR[:], KVBR[:], PS2[1][0:1, 0:1024])
                nv = vec.inc(v.tensor_copy(BVB[:], KVBR[:]))
                p.bar()
                vp = Pool(VST, p, "vst")
                pp = Pool([ps[:, 0:512] for ps in PS1[0:3]])
                for tc in range(NTC):
                    tsl = slice(tc * 128, (tc + 1) * 128)
                    for vb2 in range(2):
                        bank = pp.get(te)
                        for c in range(CH):
                            te.matmul(bank.ap, YH[c][:, tsl],
                                      WVB[:, c * 1024 + 512 * vb2:c * 1024 + 512 * (vb2 + 1)],
                                      start=(c == 0), stop=False)
                        mm = te.matmul(bank.ap, ONEB[0:1, :],
                                       BVB[:, 512 * vb2:512 * (vb2 + 1)],
                                       start=False, stop=True)
                        tv = tec.inc(mm)
                        vbuf = vp.get(sc)
                        sc.wait_ge(tec.h, tv)
                        sv = sec.inc(sc.copy(vbuf.ap[:], bank.ap))
                        bank.release(sec, sv)
                        g.wait_ge(sec.h, sv)
                        n_st = vbuf.ctr.inc(g.dma_start(
                            v_sp[8 * vb2:8 * (vb2 + 1), tc:tc + 1, :, :].rearrange(
                                "h o pp dd -> (o pp) h dd"),
                            vbuf.ap.rearrange("pp (h dd) -> pp h dd", h=8)), 16)
                        vbuf.release(vbuf.ctr, n_st)
                g.wait_ge(st.h, st.n)
                p.bar()

        attn_core(causal=False)

        with p.phase() as alloc:
            OALL = [alloc((128, S), BF16) for _ in range(CH)]
            WOB = [alloc((128, CH * 128), BF16) for _ in range(2)]
            XT = [alloc((128, S), F32) for _ in range(2)]
            XO = [alloc((128, S), F32) for _ in range(2)]
            OBS = alloc((128, CH), F32)
            load(OBS[:], cd["AOB"][:])
            for c in range(CH):
                load(OALL[c][:], o_sp[c * 128:(c + 1) * 128, :])
            sy.wait_ge(ld.h, ld.n)
            p.bar()
            SQF = [alloc((128, B), F32) for _ in range(2)]
            SROWX = alloc((1, S), F32)
            QROWX = alloc((1, S), F32)
            xp = Pool(XT, p, "xr")
            xop = Pool(XO, p, "xo")
            wcc = p.ctr("wcc")
            pp = Pool([ps[:, 0:B] for ps in PS1[0:3]])
            SBK = [PS2[0][:, 0:B], PS2[0][:, B:2 * B]]
            sg_read = [0, 0]
            sq_rd = [0, 0]
            pend_stat = []

            def do_fold_stats(mch, xo, item):
                b, nv = item
                blk = slice(b * B, (b + 1) * B)
                bpar = b % 2
                sc.wait_ge(vec.h, nv)
                sc.wait_ge(tec.h, sq_rd[bpar])
                sq2 = sec.inc(sc.activation(SQF[bpar][:], xo.ap[:, blk],
                                            AF.Square))
                te.wait_ge(sec.h, sq2)
                te.wait_ge(vec.h, sg_read[bpar])
                te.matmul(SBK[bpar][0:1, 0:B], ONE[:, 0:1], xo.ap[:, blk],
                          start=True, stop=True)
                mm2 = te.matmul(SBK[bpar][32:33, 0:B], ONE[:, 0:1],
                                SQF[bpar][:], start=True, stop=True)
                sq_rd[bpar] = tv2 = tec.inc(mm2)
                v.wait_ge(tec.h, tv2)
                if mch == 0:
                    v.tensor_copy(SROWX[:, blk], SBK[bpar][0:1, 0:B])
                    ins = v.tensor_copy(QROWX[:, blk], SBK[bpar][32:33, 0:B])
                else:
                    v.tensor_add(SROWX[:, blk], SROWX[:, blk],
                                 SBK[bpar][0:1, 0:B])
                    ins = v.tensor_add(QROWX[:, blk], QROWX[:, blk],
                                       SBK[bpar][32:33, 0:B])
                sg_read[bpar] = vec.inc(ins)
            tgt = out_d if is_last else x_hbm
            wob_guard = [0, 0]
            for mch in range(CH):
                par = mch % 2
                g.wait_ge(tec.h, wob_guard[par])
                for c in range(CH):
                    wcc.inc(g.dma_start(
                        WOB[par][:, c * 128:(c + 1) * 128],
                        cd["AOW"][c * 128:(c + 1) * 128, mch * 128:(mch + 1) * 128]), 16)
                nw = wcc.n
                xb = xp.get(sy)
                n_ld = xb.ctr.inc(sy.dma_start(
                    xb.ap[:], x_hbm[mch * 128:(mch + 1) * 128, :]), 16)
                xo = xop.get(v)
                for b in range(NB):
                    blk = slice(b * B, (b + 1) * B)
                    bank = pp.get(te)
                    te.wait_ge(wcc.h, nw)
                    for c in range(CH):
                        mm = te.matmul(bank.ap, WOB[par][:, c * 128:(c + 1) * 128],
                                       OALL[c][:, blk], start=(c == 0),
                                       stop=(c == CH - 1))
                    tv = tec.inc(mm)
                    v.wait_ge(tec.h, tv)
                    v.wait_ge(xb.ctr.h, n_ld)
                    ins = v.scalar_tensor_tensor(xo.ap[:, blk], bank.ap,
                                                 OBS[:, mch:mch + 1], xb.ap[:, blk],
                                                 ALU.add, ALU.add)
                    nv = vec.inc(ins)
                    bank.release(vec, nv)
                    pend_stat.append((b, nv))
                    if len(pend_stat) > 1:
                        do_fold_stats(mch, xo, pend_stat.pop(0))
                while pend_stat:
                    do_fold_stats(mch, xo, pend_stat.pop(0))
                wob_guard[par] = tec.n
                xb.release(vec, vec.n)
                g.wait_ge(vec.h, vec.n)
                n_st = xo.ctr.inc(g.dma_start(
                    tgt[mch * 128:(mch + 1) * 128, :], xo.ap[:]), 16)
                xo.release(xo.ctr, n_st)
            if not is_last:
                g.wait_ge(vec.h, vec.n)
                vrb.inc(g.dma_start(vrow_hbm[0:1, :], SROWX[:]), 16)
                nvr = vrb.inc(g.dma_start(vrow_hbm[1:2, :], QROWX[:]), 16)
                g.wait_ge(vrb.h, nvr)
            for b_ in xop.bufs:
                g.wait_ge(b_.ctr.h, b_.ctr.n)
            p.bar()


    def ffn_layer(is_last):
        with p.phase() as alloc:
            x_stats(x_hbm, alloc, rows_ready=True)
        v.tensor_mul(vt(VT_T1), vt(VT_RHO), vt(VT_RHO))
        v.tensor_mul(vt(VT_T1), vt(VT_T1), vt(VT_VAR))
        rsqrt_ops(vt(VT_T2), vt(VT_T1), 1e-6, vt(VT_T3), vt(VT_T4))
        v.tensor_mul(vt(VT_SC), vt(VT_RHO), vt(VT_T2))
        p.bar()

        with p.phase() as alloc_m, contextlib.ExitStack() as kv_stack:
            CETF = alloc_m((8, S), BF16)
            alloc_kv = kv_stack.enter_context(p.phase())
            KCB = [alloc_kv((128, S), BF16) for _ in range(2)]
            VCT = alloc_kv((128, 256 * NTC), BF16)

            with p.phase() as alloc:
                YH = [alloc((128, S), BF16) for _ in range(CH)]
                G1S = alloc((128, CH), F32)
                n1 = load(G1S[:], cd["FG1"][:])
                sy.wait_ge(ld.h, n1)
                with p.phase() as alloc_n:
                    build_norm_operand(YH, VT_SC, x_hbm, alloc_n, gcol=G1S[:])
                WQB = alloc((128, CH * 1024), BF16)
                WKB = alloc((128, CH * 256), BF16)
                WVB = alloc((128, CH * 256), BF16)
                QST = [alloc((128, S), BF16) for _ in range(2)]
                SQ = alloc((128, B), F32)
                CQR = alloc((1, S), F32)
                prep_weight_cast(WQB, cd["FWQ"], 0, 1024)
                prep_weight_cast(WKB, cd["FWK"], 0, 256)
                prep_weight_cast(WVB, cd["FWV"], 0, 256)
                g.wait_ge(st.h, st.n)
                p.bar()
                pp = Pool([ps[:, 0:B] for ps in PS1[0:3]])
                for mch in range(2):
                    for b in range(NB):
                        blk = slice(b * B, (b + 1) * B)
                        bank = pp.get(te)
                        for c in range(CH):
                            mm = te.matmul(bank.ap,
                                           WKB[:, c * 256 + 128 * mch:c * 256 + 128 * (mch + 1)],
                                           YH[c][:, blk], start=(c == 0), stop=(c == CH - 1))
                        tv = tec.inc(mm)
                        sc.wait_ge(tec.h, tv)
                        ins = sc.copy(KCB[mch][:, blk], bank.ap)
                        bank.release(sec, sec.inc(ins))
                p.bar()
                for b in range(NB):
                    blk = slice(b * B, (b + 1) * B)
                    te.wait_ge(vec.h, vec.n)
                    for mch in range(2):
                        sv = sec.inc(sc.activation(SQ[:], KCB[mch][:, blk], AF.Square))
                        te.wait_ge(sec.h, sv)
                        mm = te.matmul(PS1[3][0:1, 0:B], ONE[:, 0:1], SQ[:],
                                       start=(mch == 0), stop=(mch == 1))
                    tv = tec.inc(mm)
                    v.wait_ge(tec.h, tv)
                    vec.inc(v.tensor_copy(CQR[:, blk], PS1[3][0:1, 0:B]))
                p.bar()
                v.tensor_reduce(S11[0:1, 1:2], CQR[:], AX.X, ALU.max)
                nv = vec.inc(v.engine_nop())
                sc.wait_ge(vec.h, nv)
                ss = sec.inc(sc.activation(S11[0:1, 2:3], S11[0:1, 1:2], AF.Sqrt))
                v.wait_ge(sec.h, ss)
                v.tensor_scalar(S11[0:1, 3:4], S11[0:1, 2:3], -1.0, None, ALU.mult)
                p.bar()
                bcast_scalar(SVEC[:, 1:2], S11[0:1, 3:4])
                p.bar()
                qsp = Pool(QST, p, "vst")
                for h in range(4):
                    g.wait_ge(vec.h, vec.n)
                    g.wait_ge(st.h, st.n)
                    gm = pec.inc(g.memset(CQR[:], 0.0))
                    for half in range(2):
                        mch = 2 * h + half
                        qsb = qsp.get(sc)
                        for b in range(NB):
                            blk = slice(b * B, (b + 1) * B)
                            bank = pp.get(te)
                            for c in range(CH):
                                mm = te.matmul(bank.ap,
                                               WQB[:, c * 1024 + 128 * mch:c * 1024 + 128 * (mch + 1)],
                                               YH[c][:, blk], start=(c == 0), stop=(c == CH - 1))
                            tv = tec.inc(mm)
                            sc.wait_ge(tec.h, tv)
                            ins = sc.activation(qsb.ap[:, blk], bank.ap, AF.Identity,
                                                scale=0.0625)
                            bank.release(sec, sec.inc(ins))
                        sv = sec.n
                        g.wait_ge(sec.h, sv)
                        n_st = qsb.ctr.inc(g.dma_start(
                            q_sp[h, 128 * half:128 * (half + 1), :], qsb.ap[:]), 16)
                        qsb.release(qsb.ctr, n_st)
                        for b in range(NB):
                            blk = slice(b * B, (b + 1) * B)
                            s2 = sec.inc(sc.activation(SQ[:], qsb.ap[:, blk], AF.Square))
                            bank = pp.get(te)
                            te.wait_ge(sec.h, s2)
                            mm = te.matmul(bank.ap[0:1, 0:B], ONE[:, 0:1], SQ[:],
                                           start=True, stop=True)
                            tv = tec.inc(mm)
                            v.wait_ge(tec.h, tv)
                            v.wait_ge(pec.h, gm)
                            ins = v.tensor_add(CQR[:, blk], CQR[:, blk],
                                               bank.ap[0:1, 0:B])
                            bank.release(vec, vec.inc(ins))
                    sc.wait_ge(vec.h, vec.n)
                    ss = sec.inc(sc.activation(CQR[:], CQR[:], AF.Sqrt))
                    v.wait_ge(sec.h, ss)
                    ins = v.tensor_scalar(CQR[:], CQR[:], SVEC[0:1, 1:2], None, ALU.mult)
                    nv = vec.inc(ins)
                    g.wait_ge(vec.h, nv)
                    ncq = store(cq_hbm[h:h + 1, :], CQR[:])
                g.wait_ge(st.h, st.n)
                p.bar()
                for tc in range(NTC):
                    tsl = slice(tc * 128, (tc + 1) * 128)
                    for vb2 in range(1):
                        bank = pp.get(te)
                        for c in range(CH):
                            mm = te.matmul(bank.ap[:, 0:256], YH[c][:, tsl],
                                           WVB[:, c * 256:(c + 1) * 256],
                                           start=(c == 0), stop=(c == CH - 1))
                        tv = tec.inc(mm)
                        sc.wait_ge(tec.h, tv)
                        ins = sc.copy(VCT[:, 256 * tc:256 * (tc + 1)], bank.ap[:, 0:256])
                        bank.release(sec, sec.inc(ins))
                p.bar()
            if FFN_CUT < 2:
                return
            with p.phase() as alloc:
                QHB = [alloc((128, 2 * S), BF16) for _ in range(2)]
                CQB = [alloc((1, S), BF16) for _ in range(2)]
                PT = alloc((128, NTC * B), BF16)
                OH2 = [alloc((128, 2 * S), BF16) for _ in range(2)]
                RDNR = alloc((1, S), F32)
                RDNB2 = alloc((1, S), BF16)
                p.bar()
                qhp = Pool(QHB, p, "qnp")
                cqp = Pool(CQB, p, "knp")
                ohp = Pool(OH2, p, "ohp")
                SLOT = [PS2[0][:, 0:B], PS2[0][:, B:2 * B],
                        PS2[1][:, 0:B], PS2[1][:, B:2 * B]]
                A0 = PS1[0][:, 0:B]
                A1 = PS1[1][:, 0:B]
                DEN = PS1[2][0:1, 0:B]
                BCST = PS1[3][:, 0:B]
                exp_val = {}
                state = {"evac": 0, "bcst": 0}
                todo = []

                def mk_recip(qsl, ev, nxt):
                    def fn():
                        sc.wait_ge(vec.h, ev)
                        sc.activation(RDNR[0:1, qsl], RDNR[0:1, qsl], AF.Ln)
                        rd = sec.inc(sc.activation(RDNR[0:1, qsl], RDNR[0:1, qsl],
                                                   AF.Exp, scale=-1.0))
                        v.wait_ge(sec.h, rd)
                        rdnb = vec.inc(v.tensor_copy(RDNB2[0:1, qsl],
                                                     RDNR[0:1, qsl]))
                        nxt(rdnb)
                    return fn

                def mk_norm(pob, ph, qblk, gready):
                    def sched(rdnb):
                        def fn():
                            nql = slice(qblk * B, (qblk + 1) * B)
                            te.wait_ge(vec.h, max(rdnb, state["bcst"]))
                            mm2 = te.matmul(BCST, ONEB[0:1, :],
                                            RDNB2[0:1, nql], start=True, stop=True)
                            tv2 = tec.inc(mm2)
                            v.wait_ge(tec.h, tv2)
                            v.tensor_mul(pob.ap[:, nql], pob.ap[:, nql], BCST)
                            ins = v.tensor_mul(
                                pob.ap[:, S + qblk * B:S + (qblk + 1) * B],
                                pob.ap[:, S + qblk * B:S + (qblk + 1) * B], BCST)
                            state["bcst"] = vec.inc(ins)
                            if qblk == NB - 1:
                                todo.append((gready + 4, mk_store(pob, ph)))
                        todo.append((gready + 2, fn))
                    return sched

                def mk_store(pob, ph):
                    def fn():
                        g.wait_ge(vec.h, state["bcst"])
                        pob.ctr.inc(g.dma_start(o_sp[ph * 256:ph * 256 + 128, :],
                                                pob.ap[:, 0:S]), 16)
                        n_st = pob.ctr.inc(g.dma_start(
                            o_sp[ph * 256 + 128:ph * 256 + 256, :],
                            pob.ap[:, S:2 * S]), 16)
                        pob.release(pob.ctr, n_st)
                    return fn

                nsteps = NB * NTC
                for h in range(4):
                    qb_ = qhp.get(sy)
                    qb_.ctr.inc(sy.dma_start(qb_.ap[:, 0:S], q_sp[h, 0:128, :]), 16)
                    nq = qb_.ctr.inc(sy.dma_start(qb_.ap[:, S:2 * S],
                                                  q_sp[h, 128:256, :]), 16)
                    cqb = cqp.get(sy)
                    ncq = cqb.ctr.inc(sy.dma_start(cqb.ap[:], cq_hbm[h:h + 1, :]), 16)
                    te.wait_ge(qb_.ctr.h, nq)
                    te.wait_ge(cqb.ctr.h, ncq)
                    ob = ohp.get(v)

                    def emit_av(step):
                        qblk, tc = divmod(step, NTC)
                        qsl = slice(qblk * B, (qblk + 1) * B)
                        te.wait_ge(sec.h, exp_val[h * nsteps + step])
                        if tc == 0 and state["evac"]:
                            te.wait_ge(vec.h, state["evac"])
                        st_, sp_ = (tc == 0), (tc == NTC - 1)
                        ptt = PT[:, tc * B:(tc + 1) * B]
                        te.matmul(A0, VCT[:, 256 * tc:256 * tc + 128], ptt,
                                  start=st_, stop=sp_)
                        te.matmul(A1, VCT[:, 256 * tc + 128:256 * tc + 256], ptt,
                                  start=st_, stop=sp_)
                        mm = te.matmul(DEN, ONEB[:, 0:1], ptt, start=st_, stop=sp_)
                        if sp_:
                            tv = tec.inc(mm)
                            v.wait_ge(tec.h, tv)
                            v.tensor_copy(ob.ap[:, qsl], A0)
                            v.tensor_copy(ob.ap[:, S + qblk * B:S + (qblk + 1) * B],
                                          A1)
                            ins = v.tensor_copy(RDNR[0:1, qsl], DEN)
                            state["evac"] = vec.inc(ins)
                            gnow = h * nsteps + step + 2
                            todo.append((gnow + 2, mk_recip(
                                qsl, state["evac"], mk_norm(ob, h, qblk, gnow + 2))))

                    for step in range(nsteps):
                        qblk, tc = divmod(step, NTC)
                        gstep = h * nsteps + step
                        slot = SLOT[step % 4]
                        if gstep >= 4:
                            te.wait_ge(sec.h, exp_val[gstep - 4])
                        qsl = slice(qblk * B, (qblk + 1) * B)
                        te.matmul(slot, KCB[0][:, tc * 128:(tc + 1) * 128],
                                  qb_.ap[:, qsl], start=True, stop=False)
                        te.matmul(slot, KCB[1][:, tc * 128:(tc + 1) * 128],
                                  qb_.ap[:, S + qblk * B:S + (qblk + 1) * B],
                                  start=False, stop=False)
                        mm = te.matmul(slot, ONEB[0:1, :], cqb.ap[:, qsl],
                                       start=False, stop=True)
                        tv = tec.inc(mm)
                        if step % 2 == 1:
                            sc.wait_ge(tec.h, tv)
                            ins = sc.activation(PT[:, (tc - 1) * B:(tc + 1) * B],
                                                PS2[(step // 2) % 2][:, 0:2 * B],
                                                AF.Exp)
                            sv = sec.inc(ins)
                            exp_val[gstep] = exp_val[gstep - 1] = sv
                        if step >= 2:
                            emit_av(step - 2)
                        while todo and todo[0][0] <= gstep:
                            todo.pop(0)[1]()
                    emit_av(nsteps - 2)
                    emit_av(nsteps - 1)
                    qb_.release(tec, tec.n)
                    cqb.release(tec, tec.n)
                while todo:
                    todo.pop(0)[1]()
                for b_ in ohp.bufs:
                    g.wait_ge(b_.ctr.h, b_.ctr.n)
                p.bar()

            if FFN_CUT < 3:
                return
            kv_stack.close()
            with p.phase() as alloc:
                OALL = [alloc((128, S), BF16) for _ in range(CH)]
                WOB = [alloc((128, CH * 128), BF16) for _ in range(2)]
                MUB = alloc((128, S), F32)
                SCB = alloc((128, S), F32)
                LGT = alloc((8, S), F32)
                SQROW = alloc((1, S), F32)
                g.memset(LGT[:], 0.0)
                g.memset(SQROW[:], 0.0)
                WRGF = alloc((128, CH * 8), F32)
                SQV = alloc((128, B), F32)
                SQS = [alloc((128, B), F32) for _ in range(2)]
                FG2S = alloc((128, CH), F32)
                for c in range(CH):
                    load(OALL[c][:], o_sp[c * 128:(c + 1) * 128, :])
                load(FG2S[:], cd["FG2"][:])
                load(WRGF[:].rearrange("p (c e) -> p c e", e=8),
                     cd["FWR"][:].rearrange("(c pp) e -> pp c e", pp=128))
                with p.phase() as alloc_r:
                    MR = alloc_r((1, S), F32)
                    RR = alloc_r((1, S), F32)
                    vt_to_rows2([(MR[:], VT_MU, 0), (RR[:], VT_RHO, 1)])
                    sy.wait_ge(ld.h, ld.n)
                    p.bar()
                    for c in range(CH):
                        v.tensor_scalar(WRGF[:, c * 8:(c + 1) * 8],
                                        WRGF[:, c * 8:(c + 1) * 8],
                                        FG2S[:, c:c + 1], None, ALU.mult)
                    bp = Pool([PS2[0][:, 0:B], PS2[0][:, B:2 * B],
                               PS2[1][:, 0:B], PS2[1][:, B:2 * B]])
                    for b in range(NB):
                        blk = slice(b * B, (b + 1) * B)
                        bm = bp.get(te)
                        bs = bp.get(te)
                        te.matmul(bm.ap, ONE[0:1, :], MR[:, blk],
                                  start=True, stop=True)
                        mm = te.matmul(bs.ap, ONE[0:1, :], RR[:, blk],
                                       start=True, stop=True)
                        tv = tec.inc(mm)
                        sc.wait_ge(tec.h, tv)
                        sc.copy(MUB[:, blk], bm.ap)
                        ss = sec.inc(sc.copy(SCB[:, blk], bs.ap))
                        bm.release(sec, ss)
                        bs.release(sec, ss)
                    p.bar()
                XT = [alloc((128, S), F32)]
                XO = [alloc((128, S), F32)]
                xp = Pool(XT, p, "xr")
                xop = Pool(XO, p, "xo")
                wcc = p.ctr("wcc")
                pp = Pool([PS2[0][:, 0:B], PS2[0][:, B:2 * B],
                           PS2[1][:, 0:B], PS2[1][:, B:2 * B]])
                STAT = [(PS1[0][0:1, 0:B], PS1[1][0:8, 0:B]),
                        (PS1[2][0:1, 0:B], PS1[3][0:8, 0:B])]
                stat_guard = [0, 0]
                wob_guard = [0, 0]
                sq_done = [0, 0]
                sq_read = [0, 0]
                for mch in range(CH):
                    par = mch % 2
                    g.wait_ge(tec.h, wob_guard[par])
                    for c in range(CH):
                        wcc.inc(g.dma_start(
                            WOB[par][:, c * 128:(c + 1) * 128],
                            cd["FWO"][c * 128:(c + 1) * 128, mch * 128:(mch + 1) * 128]), 16)
                    nw = wcc.n
                    xb = xp.get(sy)
                    n_ld = xb.ctr.inc(sy.dma_start(
                        xb.ap[:], x_hbm[mch * 128:(mch + 1) * 128, :]), 16)
                    xo = xop.get(v)

                    def do_stats(pb):
                        sr, lg = STAT[pb % 2]
                        blkp = slice(pb * B, (pb + 1) * B)
                        te.wait_ge(sec.h, sq_done[pb % 2])
                        te.wait_ge(vec.h, stat_guard[pb % 2])
                        te.matmul(sr, ONE[:, 0:1], SQS[pb % 2][:],
                                  start=True, stop=True)
                        mm3 = te.matmul(lg, WRGF[:, mch * 8:(mch + 1) * 8],
                                        xo.ap[:, blkp], start=True, stop=True)
                        tv3 = tec.inc(mm3)
                        sq_read[pb % 2] = tv3
                        v.wait_ge(tec.h, tv3)
                        v.tensor_add(SQROW[:, blkp], SQROW[:, blkp], sr)
                        ins = v.tensor_add(LGT[:, blkp], LGT[:, blkp], lg)
                        stat_guard[pb % 2] = vec.inc(ins)

                    for b in range(NB):
                        blk = slice(b * B, (b + 1) * B)
                        bank = pp.get(te)
                        te.wait_ge(wcc.h, nw)
                        for c in range(CH):
                            mm = te.matmul(bank.ap, WOB[par][:, c * 128:(c + 1) * 128],
                                           OALL[c][:, blk], start=(c == 0),
                                           stop=(c == CH - 1))
                        tv = tec.inc(mm)
                        v.wait_ge(tec.h, tv)
                        v.wait_ge(xb.ctr.h, n_ld)
                        v.tensor_sub(SQV[:], xb.ap[:, blk], MUB[:, blk])
                        v.tensor_mul(SQV[:], SQV[:], SCB[:, blk])
                        ins = v.tensor_add(xo.ap[:, blk], SQV[:], bank.ap)
                        nv = vec.inc(ins)
                        bank.release(vec, nv)
                        sc.wait_ge(vec.h, nv)
                        sc.wait_ge(tec.h, sq_read[b % 2])
                        sq_done[b % 2] = sec.inc(
                            sc.activation(SQS[b % 2][:], xo.ap[:, blk], AF.Square))
                        if b >= 1:
                            do_stats(b - 1)
                    do_stats(NB - 1)
                    wob_guard[par] = tec.n
                    xb.release(vec, vec.n)
                    g.wait_ge(vec.h, vec.n)
                    n_st = xo.ctr.inc(g.dma_start(
                        xmid_hbm[mch * 128:(mch + 1) * 128, :], xo.ap[:]), 16)
                    xo.release(xo.ctr, n_st)
                for b_ in xop.bufs:
                    g.wait_ge(b_.ctr.h, b_.ctr.n)
                p.bar()
                g.wait_ge(vec.h, vec.n)
                nlg = vrb.inc(g.dma_start(vrow_hbm[4:12, :], LGT[:]), 16)
                row_to_vt(VT_T1, SQROW[:], lane=0)
                sy.wait_ge(ld.h, ld.n)
                sy.wait_ge(vrb.h, nlg)
                p.bar()
            v.tensor_scalar(vt(VT_T1), vt(VT_T1), 1.0 / D, None, ALU.mult)
            rsqrt_ops(vt(VT_X1), vt(VT_T1), 1e-6, vt(VT_T2), vt(VT_T3))
            p.bar()
            if FFN_CUT < 4:
                return
            with p.phase() as alloc:
                R2R = alloc((1, S), F32)
                LGT = alloc((8, S), F32)
                load(LGT[:], vrow_hbm[4:12, :])
                XT = [alloc((128, S), F32) for _ in range(2)]
                ZST = [alloc((128, S), BF16) for _ in range(2)]
                SQ3 = alloc((128, B), F32)
                RT = [alloc((128, 8), F32) for _ in range(4)]
                RW = [alloc((128, 8), F32) for _ in range(4)]
                RW2 = [alloc((128, 8), F32) for _ in range(4)]
                MM1 = [alloc((128, 4), F32) for _ in range(4)]
                CMB = [alloc((128, 8), F32) for _ in range(4)]
                FG2Z = alloc((128, CH), F32)
                R2B = alloc((128, S), F32)
                FEBS = alloc((8, 1), F32)
                vt_to_row(R2R[:], VT_X1, lane=2)
                nz1 = load(FG2Z[:], cd["FG2"][:])
                load(FEBS[:], cd["FEB"][:])
                sy.wait_ge(ld.h, ld.n)
                p.bar()
                bp2 = Pool([ps[:, 0:B] for ps in PS1])
                for b in range(NB):
                    blk = slice(b * B, (b + 1) * B)
                    bank = bp2.get(te)
                    mm = te.matmul(bank.ap, ONE[0:1, :], R2R[:, blk],
                                   start=True, stop=True)
                    tv = tec.inc(mm)
                    sc.wait_ge(tec.h, tv)
                    ss = sec.inc(sc.copy(R2B[:, blk], bank.ap))
                    bank.release(sec, ss)
                r2b_done = sec.n
                te.wait_ge(sec.h, r2b_done)
                v.wait_ge(sec.h, r2b_done)
                lgt_done = {}
                for b in range(NB):
                    blk = slice(b * B, (b + 1) * B)
                    bank = PS1[2] if b % 2 == 0 else PS1[3]
                    if b >= 2:
                        te.wait_ge(vec.h, lgt_done[b - 2])
                    mm = te.matmul(bank[0:8, 0:B], ONE[0:1, 0:8], R2R[:, blk],
                                   start=True, stop=True)
                    tv = tec.inc(mm)
                    v.wait_ge(tec.h, tv)
                    v.tensor_mul(LGT[:, blk], LGT[:, blk], bank[0:8, 0:B])
                    ins = v.tensor_scalar(LGT[:, blk], LGT[:, blk], FEBS[:, 0:1],
                                          None, ALU.add)
                    lgt_done[b] = vec.inc(ins)
                xp = Pool(XT, p, "xr")
                zp = Pool(ZST, p, "xo")
                zf_items = []
                for c in range(CH):
                    for b in range(NB):
                        zf_items.append((c, b))
                zf_state = {"xb": None, "zb": None, "n_ld": 0}

                def zf_step():
                    if not zf_items:
                        return
                    c, b = zf_items.pop(0)
                    if b == 0:
                        zf_state["xb"] = xp.get(sy)
                        zf_state["n_ld"] = zf_state["xb"].ctr.inc(sy.dma_start(
                            zf_state["xb"].ap[:],
                            xmid_hbm[c * 128:(c + 1) * 128, :]), 16)
                        zf_state["zb"] = zp.get(v)
                    blk = slice(b * B, (b + 1) * B)
                    v.wait_ge(zf_state["xb"].ctr.h, zf_state["n_ld"])
                    v.tensor_mul(SQ3[:], zf_state["xb"].ap[:, blk], R2B[:, blk])
                    ins = v.tensor_scalar(zf_state["zb"].ap[:, blk], SQ3[:],
                                          FG2Z[:, c:c + 1], None, ALU.mult)
                    nv = vec.inc(ins)
                    if b == NB - 1:
                        zf_state["xb"].release(vec, nv)
                        g.wait_ge(vec.h, nv)
                        n_st = zf_state["zb"].ctr.inc(g.dma_start(
                            zf_hbm[c * 128:(c + 1) * 128, :],
                            zf_state["zb"].ap[:]), 16)
                        zf_state["zb"].release(zf_state["zb"].ctr, n_st)

                rt_copy = {}
                SLOTT = [PS1[0], PS1[1], PS2[0][:, 0:B], PS2[0][:, B:2 * B]]
                for tc in range(S // 128):
                    tsl = slice(tc * 128, (tc + 1) * 128)
                    i2 = tc % 4
                    slot = SLOTT[i2]
                    if tc >= 4:
                        te.wait_ge(sec.h, rt_copy[tc - 4])
                    te.wait_ge(vec.h, lgt_done[tc // 4])
                    mm = te.transpose(slot[0:128, 0:8], LGT[0:8, tsl],
                                      IDE[0:8, 0:8])
                    tv = tec.inc(mm)
                    sc.wait_ge(tec.h, tv)
                    rt_copy[tc] = sec.inc(sc.copy(RT[i2][:], slot[0:128, 0:8]))
                    ss = sec.inc(sc.activation(RW[i2][:], RT[i2][:], AF.Exp))
                    v.wait_ge(sec.h, ss)
                    v.tensor_reduce(MM1[i2][:, 1:2], RW[i2][:], AX.X, ALU.add)
                    v.reciprocal(MM1[i2][:, 1:2], MM1[i2][:, 1:2])
                    v.tensor_scalar(RW[i2][:], RW[i2][:], MM1[i2][:, 1:2],
                                    None, ALU.mult)
                    v.tensor_reduce(MM1[i2][:, 2:3], RW[i2][:], AX.X, ALU.max)
                    v.tensor_scalar(CMB[i2][:], RW[i2][:], 0.0, MM1[i2][:, 2:3],
                                    ALU.mult, ALU.add)
                    v.match_replace(RW2[i2][:], CMB[i2][:], RW[i2][:], -1e30)
                    v.tensor_reduce(MM1[i2][:, 3:4], RW2[i2][:], AX.X, ALU.max)
                    nv = vec.inc(v.tensor_sub(MM1[i2][:, 0:1], MM1[i2][:, 3:4],
                                              MM1[i2][:, 2:3]))
                    sc.wait_ge(vec.h, nv)
                    ss = sec.inc(sc.activation(MM1[i2][:, 0:1], MM1[i2][:, 0:1],
                                               AF.Sigmoid))
                    zf_step()
                    v.wait_ge(sec.h, ss)
                    v.tensor_scalar(MM1[i2][:, 1:2], MM1[i2][:, 0:1], -1.0, 1.0,
                                    ALU.mult, ALU.add)
                    v.tensor_scalar(CMB[i2][:], RW[i2][:], MM1[i2][:, 2:3],
                                    None, ALU.is_equal)
                    v.tensor_scalar(CMB[i2][:], CMB[i2][:], MM1[i2][:, 1:2],
                                    None, ALU.mult)
                    v.tensor_scalar(RW2[i2][:], RW2[i2][:], MM1[i2][:, 3:4],
                                    None, ALU.is_equal)
                    v.tensor_scalar(RW2[i2][:], RW2[i2][:], MM1[i2][:, 0:1],
                                    None, ALU.mult)
                    nv = vec.inc(v.tensor_add(CMB[i2][:], CMB[i2][:], RW2[i2][:]))
                    te.wait_ge(vec.h, nv)
                    mm = te.transpose(slot[0:8, 8:136], CMB[i2][:], IDE[:])
                    tv = tec.inc(mm)
                    sc.wait_ge(tec.h, tv)
                    rt_copy[tc] = sec.inc(sc.copy(CETF[0:8, tsl],
                                                  slot[0:8, 8:136]))
                    zf_step()
                while zf_items:
                    zf_step()
                sc.wait_ge(sec.h, sec.n)
                for b_ in zp.bufs:
                    g.wait_ge(b_.ctr.h, b_.ctr.n)
                p.bar()

            if FFN_CUT < 5:
                return
            if True:
                SH = S // 2
                BH = min(B, SH)
                NBH = SH // BH
                with p.phase() as alloc2:
                    ZF = [alloc2((128, SH), BF16) for _ in range(CH)]
                    SPEC = [alloc2((128, SH), F32) for _ in range(CH)]
                    EW1B = alloc2((128, CH * 1024), BF16)
                    EW3B = alloc2((128, CH * 1024), BF16)
                    EW2B = alloc2((128, CH * 1024), BF16)
                    HBK = alloc2((128, 2 * CH * BH), BF16)
                    SA = alloc2((128, 2 * BH), F32)
                    TB = alloc2((128, 2 * BH), F32)
                    CEBS = alloc2((128, NBH * BH), F32)
                    p.bar()
                    zlc = p.ctr("zlc")
                    wcc = p.ctr("wcc2")

                    def ew_src(e, which):
                        if e == 0:
                            return cd[{1: "SW1", 3: "SW3", 2: "SW2"}[which]]
                        return cd[{1: "EW1", 3: "EW3", 2: "EW2"}[which]][e - 1]

                    def prefetch_w13(e, w_done):
                        g.wait_ge(tec.h, w_done)
                        for c in range(CH):
                            wcc.inc(g.dma_start(
                                EW1B[:, c * 1024:(c + 1) * 1024],
                                ew_src(e, 1)[c * 128:(c + 1) * 128, :]), 16)
                        for c in range(CH):
                            wcc.inc(g.dma_start(
                                EW3B[:, c * 1024:(c + 1) * 1024],
                                ew_src(e, 3)[c * 128:(c + 1) * 128, :]), 16)
                        return wcc.n

                    def prefetch_w2(e, s_done):
                        g.wait_ge(tec.h, s_done)
                        for c in range(CH):
                            wcc.inc(g.dma_start(
                                EW2B[:, c * 1024:(c + 1) * 1024],
                                ew_src(e, 2)[c * 128:(c + 1) * 128, :]), 16)
                        return wcc.n

                    XT2 = [alloc2((128, BH), F32) for _ in range(2)]
                    XM2 = [alloc2((128, BH), F32) for _ in range(2)]
                    XN = [alloc2((128, BH), F32) for _ in range(2)]
                    xp2 = Pool(XT2, p, "xr")
                    xm2 = Pool(XM2, p, "xm2")
                    xn2 = Pool(XN, p, "xo")
                    silu_done = [0, 0]
                    tb_done = [0, 0]
                    h_done = [0, 0]
                    spec_tec = [0, 0]
                    ceb_read = 0
                    specp = Pool([PS1[0][:, 0:BH], PS1[1][:, 0:BH]])
                    for c in range(CH):
                        zlc.inc(sy.dma_start(
                            ZF[c][:], zf_hbm[c * 128:(c + 1) * 128, 0:SH]), 16)
                    nz = zlc.n
                    te.wait_ge(zlc.h, nz)
                    w13_ready = prefetch_w13(0, tec.n)
                    w2_cur = prefetch_w2(0, tec.n)
                    for tg in range(2):
                        toff = tg * SH
                        te.wait_ge(zlc.h, nz)
                        order = list(range(9)) if tg == 0 else list(range(8, -1, -1))
                        for ei, e in enumerate(order):
                            if e > 0:
                                te.wait_ge(vec.h, ceb_read)
                                cps = []
                                for b in range(NBH):
                                    bnk = PS1[2] if b % 2 == 0 else PS1[3]
                                    if b >= 2:
                                        te.wait_ge(sec.h, cps[b - 2])
                                    mm = te.matmul(bnk[:, 0:BH],
                                                   IND8B[:, (e - 1) * 128:e * 128],
                                                   CETF[:, toff + b * BH:toff + (b + 1) * BH],
                                                   start=True, stop=True)
                                    tv = tec.inc(mm)
                                    sc.wait_ge(tec.h, tv)
                                    ins = sc.copy(CEBS[:, b * BH:(b + 1) * BH],
                                                  bnk[:, 0:BH])
                                    cps.append(sec.inc(ins))
                            ceb_ready = sec.n

                            def emit_spec(b):
                                blk = slice(b * BH, (b + 1) * BH)
                                hoff = (b % 2) * CH * BH
                                te.wait_ge(wcc.h, w2_cur)
                                for mch in range(CH):
                                    bank = specp.get(te)
                                    te.wait_ge(pec.h, h_done_blk[b])
                                    for fc in range(CH):
                                        mm = te.matmul(
                                            bank.ap,
                                            EW2B[:, fc * 1024 + 128 * mch:fc * 1024 + 128 * (mch + 1)],
                                            HBK[:, hoff + fc * BH:hoff + (fc + 1) * BH],
                                            start=(fc == 0), stop=(fc == CH - 1))
                                    tv = tec.inc(mm)
                                    v.wait_ge(tec.h, tv)
                                    if ei == 0:
                                        ins = v.tensor_copy(SPEC[mch][:, blk],
                                                            bank.ap)
                                    else:
                                        ins = v.tensor_add(SPEC[mch][:, blk],
                                                           SPEC[mch][:, blk],
                                                           bank.ap)
                                    bank.release(vec, vec.inc(ins))
                                spec_tec[b % 2] = tec.n

                            h_done_blk = [0] * NBH
                            for b in range(NBH):
                                blk = slice(b * BH, (b + 1) * BH)
                                hoff = (b % 2) * CH * BH
                                for fc in range(CH):
                                    par = fc % 2
                                    psl = slice(par * BH, (par + 1) * BH)
                                    bankt = PS2[par]
                                    te.wait_ge(sec.h, silu_done[par])
                                    te.wait_ge(vec.h, tb_done[par])
                                    if fc == 0 and b == 0:
                                        te.wait_ge(wcc.h, w13_ready)
                                    for c in range(CH):
                                        te.matmul(bankt[:, 0:BH],
                                                  EW1B[:, c * 1024 + 128 * fc:c * 1024 + 128 * (fc + 1)],
                                                  ZF[c][:, blk], start=(c == 0),
                                                  stop=(c == CH - 1))
                                    for c in range(CH):
                                        mm = te.matmul(bankt[:, BH:2 * BH],
                                                       EW3B[:, c * 1024 + 128 * fc:c * 1024 + 128 * (fc + 1)],
                                                       ZF[c][:, blk], start=(c == 0),
                                                       stop=(c == CH - 1))
                                    tv = tec.inc(mm)
                                    sc.wait_ge(tec.h, tv)
                                    sc.wait_ge(pec.h, h_done[par])
                                    ss = sec.inc(sc.activation(SA[:, psl],
                                                               bankt[:, 0:BH], AF.Silu))
                                    silu_done[par] = ss
                                    v.wait_ge(tec.h, tv)
                                    v.wait_ge(pec.h, h_done[par])
                                    if BH < 512:
                                        v.wait_ge(sec.h, silu_done[par])
                                    if e > 0:
                                        v.wait_ge(sec.h, ceb_ready)
                                        ins = v.tensor_mul(TB[:, psl], bankt[:, BH:2 * BH],
                                                           CEBS[:, b * BH:(b + 1) * BH])
                                    else:
                                        ins = v.tensor_copy(TB[:, psl], bankt[:, BH:2 * BH])
                                    tb_done[par] = vec.inc(ins)
                                    g.wait_ge(sec.h, silu_done[par])
                                    g.wait_ge(vec.h, tb_done[par])
                                    g.wait_ge(tec.h, spec_tec[b % 2])
                                    ins = g.tensor_mul(HBK[:, hoff + fc * BH:hoff + (fc + 1) * BH],
                                                       SA[:, psl], TB[:, psl])
                                    h_done[par] = pec.inc(ins)
                                h_done_blk[b] = pec.n
                                if b > 0:
                                    emit_spec(b - 1)
                            w_done = tec.n
                            emit_spec(NBH - 1)
                            s_done = tec.n
                            ceb_read = vec.n
                            if ei < 8:
                                w13_ready = prefetch_w13(order[ei + 1], w_done)
                                w2_cur = prefetch_w2(order[ei + 1], s_done)
                        if tg == 0:
                            g.wait_ge(tec.h, w_done)
                            for c in range(CH):
                                zlc.inc(g.dma_start(
                                    ZF[c][:], zf_hbm[c * 128:(c + 1) * 128,
                                                     SH:2 * SH]), 16)
                            nz = zlc.n
                        tgt = out_d if is_last else x_hbm
                        for c in range(CH):
                            for b in range(NBH):
                                csl = slice(c * 128, (c + 1) * 128)
                                hsl = slice(toff + b * BH, toff + (b + 1) * BH)
                                xb = xp2.get(sy)
                                n1 = xb.ctr.inc(sy.dma_start(
                                    xb.ap[:], x_hbm[csl, hsl]), 16)
                                xmb = xm2.get(sy)
                                n2 = xmb.ctr.inc(sy.dma_start(
                                    xmb.ap[:], xmid_hbm[csl, hsl]), 16)
                                xnb = xn2.get(v)
                                v.wait_ge(xb.ctr.h, n1)
                                v.wait_ge(xmb.ctr.h, n2)
                                v.tensor_add(xnb.ap[:], xb.ap[:], xmb.ap[:])
                                ins = v.tensor_add(
                                    xnb.ap[:], xnb.ap[:],
                                    SPEC[c][:, b * BH:(b + 1) * BH])
                                nv = vec.inc(ins)
                                xb.release(vec, nv)
                                xmb.release(vec, nv)
                                g.wait_ge(vec.h, nv)
                                n_st = xnb.ctr.inc(g.dma_start(
                                    tgt[csl, hsl], xnb.ap[:]), 16)
                                xnb.release(xnb.ctr, n_st)
                    for b_ in xn2.bufs:
                        g.wait_ge(b_.ctr.h, b_.ctr.n)
                    p.bar()


    def attn_b(is_last):
        with p.phase() as alloc:
            x_stats(x_hbm, alloc)
        p.bar()

        with p.phase() as alloc_o:
            YH = [alloc_o((128, S), BF16) for _ in range(CH)]
            with p.phase() as alloc_n:
                build_norm_operand(YH, VT_RHO, x_hbm, alloc_n)

            g.memset(BIAS[0:4, 16:20], 0.0)
            p.bar()

            def b_groups(is_q):
                col0 = 0 if is_q else 1024
                spill = qn_sp if is_q else kn_sp
                qs = 0.125 if is_q else 1.0
                with p.phase() as alloc:
                    TCOS = alloc((128, S), BF16)
                    TSIN = alloc((128, S), BF16)
                    RA = alloc((128, S), F32)
                    RB = alloc((128, S), F32)
                    STG = [alloc((128, 2 * B), F32) for _ in range(2)]
                    SQP = [alloc((128, B), F32) for _ in range(2)]
                    SQ2P = [alloc((128, B), F32) for _ in range(2)]
                    WQG = [alloc((128, 2048), BF16) for _ in range(2)]
                    QNE = alloc((128, S), BF16)
                    QNO = alloc((128, S), BF16)
                    SB = alloc((4, 4 * B), F32)

                    def sbt(i, par):
                        return SB[:, (2 * par + i) * B:(2 * par + i + 1) * B]

                    store(TCOS[:], cd["COSB"][:])
                    store(TSIN[:], cd["SINB"][:])
                    g.wait_ge(st.h, st.n)
                    p.bar()
                    if is_q:
                        v.tensor_scalar(TCOS[:], TCOS[:], qs, None, ALU.mult)
                        v.tensor_scalar(TSIN[:], TSIN[:], qs, None, ALU.mult)
                        p.bar()
                    PJ = [PS2[1][:, 0:B], PS2[1][:, B:2 * B], PS2[2][:, 0:B]]
                    P0 = [PS2[0][0:4, 0:B], PS2[0][0:4, B:2 * B]]
                    pjp = Pool(PJ)
                    gd = {"spill_st": 0, "cq_st": [0, 0], "wq": [0, 0],
                          "rot": [0] * NB, "rotp": [0] * NB,
                          "sq_read": [0, 0], "p0q": [(None, 0), (None, 0)]}

                    def wq_chunk(gi, c):
                        par = gi % 2
                        if c == 0:
                            g.wait_ge(tec.h, gd["wq"][par])
                        store(WQG[par][:, c * 256:(c + 1) * 256],
                              cd["BQKV"][c * 128:(c + 1) * 128,
                                         col0 + 256 * gi:col0 + 256 * gi + 256])
                        return st.n

                    def proj_block(gi, b):
                        par = gi % 2
                        blk = slice(b * B, (b + 1) * B)
                        for half, dst in ((0, RA), (1, RB)):
                            bank = pjp.get(te)
                            if b == 0 and half == 0:
                                te.wait_ge(st.h, gd["wq_rdy"])
                            for c in range(CH):
                                mm = te.matmul(
                                    bank.ap,
                                    WQG[par][:, c * 256 + 128 * half:
                                              c * 256 + 128 * (half + 1)],
                                    YH[c][:, blk], start=(c == 0),
                                    stop=(c == CH - 1))
                            tv = tec.inc(mm)
                            sc.wait_ge(tec.h, tv)
                            if half == 0:
                                sc.wait_ge(vec.h, gd["rot"][b])
                                sc.wait_ge(pec.h, gd["rotp"][b])
                            ins = sc.copy(dst[:, blk], bank.ap)
                            bank.release(sec, sec.inc(ins))
                        gd["wq"][par] = tec.n
                        return sec.n

                    def rope_block(b, copy_done):
                        par = b % 2
                        blk = slice(b * B, (b + 1) * B)
                        v.wait_ge(sec.h, copy_done)
                        v.wait_ge(tec.h, gd["sq_read"][par])
                        v.wait_ge(st.h, gd["spill_st"])
                        v.tensor_mul(SQP[par][:], RA[:, blk], TCOS[:, blk])
                        v.tensor_mul(SQ2P[par][:], RB[:, blk], TSIN[:, blk])
                        g.wait_ge(sec.h, copy_done)
                        g.wait_ge(st.h, gd["spill_st"])
                        g.tensor_mul(STG[par][:, 0:B], RA[:, blk], TSIN[:, blk])
                        ins = g.tensor_mul(STG[par][:, B:2 * B], RB[:, blk],
                                           TCOS[:, blk])
                        gp = pec.inc(ins)
                        ins = v.tensor_sub(QNE[:, blk], SQP[par][:], SQ2P[par][:])
                        gd["rot"][b] = vec.inc(ins)
                        ins = g.tensor_add(QNO[:, blk], STG[par][:, 0:B],
                                           STG[par][:, B:2 * B])
                        gd["rotp"][b] = pec.inc(ins)

                    def cstat_block(gi, b):
                        par = b % 2
                        blk = slice(b * B, (b + 1) * B)
                        sc.wait_ge(vec.h, gd["rot"][b])
                        sc.wait_ge(pec.h, gd["rotp"][b])
                        sc.wait_ge(tec.h, gd["sq_read"][par])
                        sc.activation(SQP[par][:], QNE[:, blk], AF.Square)
                        s2 = sec.inc(sc.activation(SQ2P[par][:], QNO[:, blk],
                                                   AF.Square))
                        peng, pval = gd["p0q"][par]
                        if peng is not None:
                            te.wait_ge(peng.h, pval)
                        te.wait_ge(sec.h, s2)
                        te.matmul(P0[par], IND4[:], SQP[par][:],
                                  start=True, stop=False)
                        mm = te.matmul(P0[par], IND4[:], SQ2P[par][:],
                                       start=False, stop=True)
                        tv = tec.inc(mm)
                        gd["sq_read"][par] = tv
                        if is_q:
                            sc.wait_ge(tec.h, tv)
                            ss = sec.inc(sc.activation(sbt(0, par), P0[par],
                                                       AF.Sqrt))
                            gd["p0q"][par] = (sec, ss)
                            v.wait_ge(sec.h, ss)
                            v.wait_ge(st.h, gd["cq_st"][par])
                            ins = v.tensor_scalar(sbt(1, par), sbt(0, par),
                                                  SVEC[0:4, 4 + gi:5 + gi],
                                                  None, ALU.mult)
                            nv = vec.inc(ins)
                            g.wait_ge(vec.h, nv)
                            for h4 in range(4):
                                store(cq_hbm[4 * gi + h4:4 * gi + h4 + 1, blk],
                                      SB[h4:h4 + 1, (2 * par + 1) * B:
                                         (2 * par + 2) * B])
                            gd["cq_st"][par] = st.n
                        else:
                            v.wait_ge(tec.h, tv)
                            v.tensor_reduce(sbt(0, par)[:, 0:1], P0[par],
                                            AX.X, ALU.max)
                            ins = v.tensor_max(BIAS[0:4, 16 + gi:17 + gi],
                                               BIAS[0:4, 16 + gi:17 + gi],
                                               sbt(0, par)[:, 0:1])
                            gd["p0q"][par] = (vec, vec.inc(ins))

                    for c in range(CH):
                        gd["wq_rdy"] = wq_chunk(0, c)
                    for gi in range(NG):
                        for b in range(NB):
                            copy_done = proj_block(gi, b)
                            rope_block(b, copy_done)
                            if gi + 1 < NG:
                                gd["wq_rdy"] = wq_chunk(gi + 1, b)
                            if b >= 1:
                                cstat_block(gi, b - 1)
                        cstat_block(gi, NB - 1)
                        g.wait_ge(vec.h, gd["rot"][NB - 1])
                        g.wait_ge(pec.h, gd["rotp"][NB - 1])
                        for h4 in range(4):
                            rs = slice(32 * h4, 32 * (h4 + 1))
                            store(spill[4 * gi + h4, 0:32, :], QNE[rs, :])
                            store(spill[4 * gi + h4, 32:64, :], QNO[rs, :])
                        gd["spill_st"] = st.n
                    g.wait_ge(st.h, st.n)
                    p.bar()
                    if not is_q:
                        ss = sec.inc(sc.activation(BIAS[0:4, 20:24],
                                                   BIAS[0:4, 16:20], AF.Sqrt))
                        v.wait_ge(sec.h, ss)
                        v.tensor_scalar(SVEC[0:4, 4:8], BIAS[0:4, 20:24],
                                        -1.0, None, ALU.mult)
                        p.bar()

            b_groups(is_q=False)
            b_groups(is_q=True)

            with p.phase() as alloc:
                WVB = alloc((128, CH * 1024), BF16)
                VST = [alloc((128, 512), BF16) for _ in range(2)]
                g.wait_ge(tec.h, tec.n)
                for c in range(CH):
                    store(WVB[:, c * 1024:(c + 1) * 1024],
                          cd["BQKV"][c * 128:(c + 1) * 128, 2048:3072])
                nw = st.n
                p.bar()
                vp = Pool(VST, p, "vst")
                pp = Pool([ps[:, 0:512] for ps in PS1[0:3]])
                for tc in range(NTC):
                    tsl = slice(tc * 128, (tc + 1) * 128)
                    for vb2 in range(2):
                        bank = pp.get(te)
                        te.wait_ge(st.h, nw)
                        for c in range(CH):
                            mm = te.matmul(bank.ap, YH[c][:, tsl],
                                           WVB[:, c * 1024 + 512 * vb2:c * 1024 + 512 * (vb2 + 1)],
                                           start=(c == 0), stop=(c == CH - 1))
                        tv = tec.inc(mm)
                        vbuf = vp.get(sc)
                        sc.wait_ge(tec.h, tv)
                        sv = sec.inc(sc.copy(vbuf.ap[:], bank.ap))
                        bank.release(sec, sv)
                        g.wait_ge(sec.h, sv)
                        n_st = vbuf.ctr.inc(g.dma_start(
                            v_sp[8 * vb2:8 * (vb2 + 1), tc:tc + 1, :, :].rearrange(
                                "h o pp dd -> (o pp) h dd"),
                            vbuf.ap.rearrange("pp (h dd) -> pp h dd", h=8)), 16)
                        vbuf.release(vbuf.ctr, n_st)
                for b_ in vp.bufs:
                    g.wait_ge(b_.ctr.h, b_.ctr.n)
                p.bar()

        attn_core(causal=True)

        with p.phase() as alloc:
            OALL = [alloc((128, S), BF16) for _ in range(CH)]
            WOB = [alloc((128, CH * 128), BF16) for _ in range(2)]
            XT = [alloc((128, S), F32) for _ in range(2)]
            XO = [alloc((128, S), F32) for _ in range(2)]
            for c in range(CH):
                load(OALL[c][:], o_sp[c * 128:(c + 1) * 128, :])
            sy.wait_ge(ld.h, ld.n)
            p.bar()
            xp = Pool(XT, p, "xr")
            xop = Pool(XO, p, "xo")
            wcc = p.ctr("wcc")
            pp = Pool([ps[:, 0:B] for ps in PS1[0:3]])
            SQF = [alloc((128, B), F32) for _ in range(2)]
            SROWX = alloc((1, S), F32)
            QROWX = alloc((1, S), F32)
            SBK = [PS2[0][:, 0:B], PS2[0][:, B:2 * B]]
            sg_read = [0, 0]
            sq_rd = [0, 0]
            pend_stat = []

            def do_fold_stats(mch, xo, item):
                b, nv = item
                blk = slice(b * B, (b + 1) * B)
                bpar = b % 2
                sc.wait_ge(vec.h, nv)
                sc.wait_ge(tec.h, sq_rd[bpar])
                sq2 = sec.inc(sc.activation(SQF[bpar][:], xo.ap[:, blk],
                                            AF.Square))
                te.wait_ge(sec.h, sq2)
                te.wait_ge(vec.h, sg_read[bpar])
                te.matmul(SBK[bpar][0:1, 0:B], ONE[:, 0:1], xo.ap[:, blk],
                          start=True, stop=True)
                mm2 = te.matmul(SBK[bpar][32:33, 0:B], ONE[:, 0:1],
                                SQF[bpar][:], start=True, stop=True)
                sq_rd[bpar] = tv2 = tec.inc(mm2)
                v.wait_ge(tec.h, tv2)
                if mch == 0:
                    v.tensor_copy(SROWX[:, blk], SBK[bpar][0:1, 0:B])
                    ins = v.tensor_copy(QROWX[:, blk], SBK[bpar][32:33, 0:B])
                else:
                    v.tensor_add(SROWX[:, blk], SROWX[:, blk],
                                 SBK[bpar][0:1, 0:B])
                    ins = v.tensor_add(QROWX[:, blk], QROWX[:, blk],
                                       SBK[bpar][32:33, 0:B])
                sg_read[bpar] = vec.inc(ins)
            tgt = out_d if is_last else x_hbm
            wob_guard = [0, 0]
            for mch in range(CH):
                par = mch % 2
                g.wait_ge(tec.h, wob_guard[par])
                for c in range(CH):
                    wcc.inc(g.dma_start(
                        WOB[par][:, c * 128:(c + 1) * 128],
                        cd["BWO"][c * 128:(c + 1) * 128, mch * 128:(mch + 1) * 128]), 16)
                nw = wcc.n
                xb = xp.get(sy)
                n_ld = xb.ctr.inc(sy.dma_start(
                    xb.ap[:], x_hbm[mch * 128:(mch + 1) * 128, :]), 16)
                xo = xop.get(v)
                for b in range(NB):
                    blk = slice(b * B, (b + 1) * B)
                    bank = pp.get(te)
                    te.wait_ge(wcc.h, nw)
                    for c in range(CH):
                        mm = te.matmul(bank.ap, WOB[par][:, c * 128:(c + 1) * 128],
                                       OALL[c][:, blk], start=(c == 0),
                                       stop=(c == CH - 1))
                    tv = tec.inc(mm)
                    v.wait_ge(tec.h, tv)
                    v.wait_ge(xb.ctr.h, n_ld)
                    ins = v.tensor_add(xo.ap[:, blk], bank.ap, xb.ap[:, blk])
                    nv = vec.inc(ins)
                    bank.release(vec, nv)
                    pend_stat.append((b, nv))
                    if len(pend_stat) > 1:
                        do_fold_stats(mch, xo, pend_stat.pop(0))
                while pend_stat:
                    do_fold_stats(mch, xo, pend_stat.pop(0))
                wob_guard[par] = tec.n
                xb.release(vec, vec.n)
                g.wait_ge(vec.h, vec.n)
                n_st = xo.ctr.inc(g.dma_start(
                    tgt[mch * 128:(mch + 1) * 128, :], xo.ap[:]), 16)
                xo.release(xo.ctr, n_st)
            if not is_last:
                g.wait_ge(vec.h, vec.n)
                vrb.inc(g.dma_start(vrow_hbm[0:1, :], SROWX[:]), 16)
                nvr = vrb.inc(g.dma_start(vrow_hbm[1:2, :], QROWX[:]), 16)
                g.wait_ge(vrb.h, nvr)
            for b_ in xop.bufs:
                g.wait_ge(b_.ctr.h, b_.ctr.n)
            p.bar()

    attn_a(is_last=(trunc == 1))
    if trunc >= 2:
        ffn_layer(is_last=(trunc == 2))
    if trunc >= 3:
        attn_b(is_last=(trunc == 3))
    if trunc >= 4:
        ffn_layer(is_last=True)

    if debug_dump is not None:
        src = dbg_scratch[debug_dump]
        dbg = nc.dram_tensor("dbg", list(src.tensor.shape),
                             src.tensor.dtype, kind="ExternalOutput").ap()
        n = store(dbg[:], src[:])
        g.wait_ge(st.h, n)
        p.bar()

    if trunc < 1:
        with p.phase() as alloc:
            XT = alloc((128, S), F32)
            for c in range(CH):
                n = load(XT[:], x_hbm[c * 128:(c + 1) * 128, :])
                sy.wait_ge(ld.h, n)
                m_ = store(out_d[c * 128:(c + 1) * 128, :], XT[:])
                sy.wait_ge(st.h, m_)
            g.wait_ge(st.h, st.n)
    p.bar()
    g.wait_ge(st.h, st.n)
    sy.wait_ge(st.h, st.n)
    return nc, es


def kernel(**inputs):
    m, xT = _prep_inputs(inputs)
    nc, es = build_program()
    in_maps = []
    for core in range(NCORE):
        d = dict(m)
        d["x"] = xT[core]
        in_maps.append(d)
    res = run_bass_kernel_spmd(nc, in_maps, core_ids=list(range(NCORE)))
    es.close()
    out = np.stack([np.asarray(r["out"]) for r in res.results], axis=0)
    return np.ascontiguousarray(out.transpose(0, 2, 1))


# version 39
# speedup vs baseline: 1.0565x; 1.0323x over previous
import contextlib
import numpy as np

import concourse.bass as bass
import concourse.mybir as mybir
from concourse.bass_utils import run_bass_kernel_spmd

F32 = mybir.dt.float32
BF16 = mybir.dt.bfloat16
AF = mybir.ActivationFunctionType
ALU = mybir.AluOpType
AX = mybir.AxisListType

D = 1024
NCORE = 8
HEADS = 16
HD = 64
NG = 4
CH = D // 128

S = 4096
TRUNC = 4
DEBUG_DUMP = None
FFN_CUT = 99


def _derived():
    B = min(512, S)
    return B, S // B, S // 128



def _perm_eo_grouped():
    perm = []
    for g in range(NG):
        for h in range(4 * g, 4 * g + 4):
            perm.extend(h * HD + 2 * j for j in range(32))
        for h in range(4 * g, 4 * g + 4):
            perm.extend(h * HD + 2 * j + 1 for j in range(32))
    return np.array(perm, dtype=np.int64)


def _perm_half_grouped():
    perm = []
    for g in range(NG):
        for h in range(4 * g, 4 * g + 4):
            perm.extend(h * HD + j for j in range(32))
        for h in range(4 * g, 4 * g + 4):
            perm.extend(h * HD + 32 + j for j in range(32))
    return np.array(perm, dtype=np.int64)


def _host_consts():
    B, NB, NTC = _derived()
    c = {}
    mel_max = np.float32(2595.0 * np.log10(1.0 + 4000.0 / 200.0))
    scale = np.power(np.float32(10.0),
                     np.linspace(np.float32(0.0), mel_max, 32, dtype=np.float32)
                     / np.float32(2595.0)) - np.float32(1.0)
    fb = np.float32(200.0) * scale / np.float32(1000.0)
    t = np.arange(S, dtype=np.float32)
    freqs = (t[:, None] * fb[None, :]).astype(np.float32)
    c["COSA"] = np.ascontiguousarray(np.tile(np.cos(freqs).astype(np.float32).T, (4, 1)))
    c["SINA"] = np.ascontiguousarray(np.tile(np.sin(freqs).astype(np.float32).T, (4, 1)))
    inv_freq = (np.float32(1.0)
                / np.power(np.float32(10000.0),
                           np.arange(0, HD, 2, dtype=np.float32) / np.float32(HD)))
    ang = (t[:, None] * inv_freq[None, :]).astype(np.float32)
    c["COSB"] = np.ascontiguousarray(np.tile(np.cos(ang).astype(np.float32).T, (4, 1)))
    c["SINB"] = np.ascontiguousarray(np.tile(np.sin(ang).astype(np.float32).T, (4, 1)))
    kk = np.arange(128)[:, None]
    qq = np.arange(B)[None, :]
    c["CMASK"] = np.concatenate(
        [(kk <= qq - 128 * j).astype(np.float32) for j in range(4)], axis=1)
    c["IDENT"] = np.eye(128, dtype=np.float32)
    c["ONESF"] = np.ones((128, 128), dtype=np.float32)
    ind4 = np.zeros((128, 4), dtype=np.float32)
    for p in range(128):
        ind4[p, p // 32] = 1.0
    c["IND4"] = ind4
    c["IND4M"] = ind4 / np.float32(64.0)
    c["IND4T"] = np.ascontiguousarray(ind4.T)
    ind8 = np.zeros((8, 8 * 128), dtype=np.float32)
    for e in range(8):
        ind8[e, e * 128:(e + 1) * 128] = 1.0
    c["IND8"] = ind8
    return c


def _prep_inputs(inp):
    f = np.float32
    p_eo = _perm_eo_grouped()
    p_hf = _perm_half_grouped()
    m = dict(_host_consts())

    m["AQW"] = np.ascontiguousarray(np.asarray(inp["L0_attn_a_q_w"], dtype=f)[:, p_eo])
    kvw = np.asarray(inp["L0_attn_a_kv_w"], dtype=f)
    kperm = np.concatenate([p_eo, 1024 + np.arange(1024)])
    m["AKVW"] = np.ascontiguousarray(kvw[:, kperm])
    m["AOW"] = np.ascontiguousarray(np.asarray(inp["L0_attn_a_out_w"], dtype=f))
    m["ARW"] = np.ascontiguousarray(
        np.asarray(inp["L0_attn_a_rh_w"], dtype=f).reshape(CH, 128).T)
    m["ARB"] = np.asarray(inp["L0_attn_a_rh_b"], dtype=f).reshape(1, 1)

    def colvec(name, key, perm=None):
        val = np.asarray(inp[key], dtype=f).reshape(-1)
        if perm is not None:
            val = val[perm]
        m[name] = np.ascontiguousarray(val.reshape(-1, 128).T)

    def rowvec(name, key, perm=None):
        val = np.asarray(inp[key], dtype=f).reshape(-1)
        if perm is not None:
            val = val[perm]
        m[name] = np.ascontiguousarray(val.reshape(1, -1))

    rowvec("AQBR", "L0_attn_a_q_b", p_eo)
    rowvec("AKVBR", "L0_attn_a_kv_b", kperm)
    colvec("AOB", "L0_attn_a_out_b")
    colvec("AQNW", "L0_attn_a_qn_w")
    colvec("AQNB", "L0_attn_a_qn_b")
    colvec("AKVNW", "L0_attn_a_kvn_w")
    colvec("AKVNB", "L0_attn_a_kvn_b")
    lnw = np.asarray(inp["L0_attn_a_ln_w"], dtype=f)
    lnb = np.asarray(inp["L0_attn_a_ln_b"], dtype=f)
    m["ALN"] = np.ascontiguousarray(np.stack(
        [np.tile(lnw[0::2], 4), np.tile(lnb[0::2], 4),
         np.tile(lnw[1::2], 4), np.tile(lnb[1::2], 4)], axis=1))

    qkvT = np.asarray(inp["L0_attn_b_qkv"], dtype=f).T
    bperm = np.concatenate([p_hf, 1024 + p_hf, 2048 + np.arange(1024)])
    m["BQKV"] = np.ascontiguousarray(qkvT[:, bperm])
    m["BWO"] = np.ascontiguousarray(np.asarray(inp["L0_attn_b_wo"], dtype=f).T)

    m["FWQ"] = np.ascontiguousarray(np.asarray(inp["L0_ffn_Wq"], dtype=f).T)
    m["FWK"] = np.ascontiguousarray(np.asarray(inp["L0_ffn_Wkc"], dtype=f).T)
    m["FWV"] = np.ascontiguousarray(np.asarray(inp["L0_ffn_Wvc"], dtype=f).T)
    m["FWO"] = np.ascontiguousarray(np.asarray(inp["L0_ffn_Wo"], dtype=f).T)
    m["FWR"] = np.ascontiguousarray(np.asarray(inp["L0_ffn_Wr"], dtype=f).T)
    m["FEB"] = np.asarray(inp["L0_ffn_expert_bias"], dtype=f).reshape(8, 1)
    colvec("FG1", "L0_ffn_g1")
    colvec("FG2", "L0_ffn_g2")
    m["SW1"] = np.ascontiguousarray(np.asarray(inp["L0_ffn_sW1"], dtype=f).T)
    m["SW3"] = np.ascontiguousarray(np.asarray(inp["L0_ffn_sW3"], dtype=f).T)
    m["SW2"] = np.ascontiguousarray(np.asarray(inp["L0_ffn_sW2"], dtype=f).T)
    m["EW1"] = np.ascontiguousarray(
        np.asarray(inp["L0_ffn_eW1"], dtype=f).transpose(0, 2, 1))
    m["EW3"] = np.ascontiguousarray(
        np.asarray(inp["L0_ffn_eW3"], dtype=f).transpose(0, 2, 1))
    m["EW2"] = np.ascontiguousarray(
        np.asarray(inp["L0_ffn_eW2"], dtype=f).transpose(0, 2, 1))

    xT = np.ascontiguousarray(
        np.asarray(inp["L0_attn_a_x"], dtype=f).transpose(0, 2, 1))
    return m, xT



class Ctr:
    def __init__(self, nc, es, name):
        self.h = es.enter_context(nc.semaphore(name))
        self.n = 0
        self._last_auto_id = None

    def inc(self, instr, k=1):
        if instr is not None and id(instr) == self._last_auto_id:
            return self.n
        instr.then_inc(self.h, k)
        self.n += k
        return self.n

    def wait(self, eng, n=None):
        eng.wait_ge(self.h, self.n if n is None else n)


_WRAP_OPS = {
    "tensor_scalar", "tensor_tensor", "tensor_add", "tensor_sub", "tensor_mul",
    "tensor_max", "tensor_copy", "tensor_reduce", "reduce_max", "reduce_sum",
    "reciprocal", "match_replace", "scalar_tensor_tensor", "select", "memset",
    "memzero", "tensor_scalar_add", "tensor_scalar_mul", "tensor_scalar_max",
    "tensor_scalar_min", "tensor_single_scalar", "activation", "copy", "add",
    "mul", "square", "sqrt", "sign", "tensor_relu",
}


class EngProxy:
    def __init__(self, eng, ctr):
        self._e = eng
        self._c = ctr
        self._dirty = {}
        self._waited = 0

    def __getattr__(self, k):
        f = getattr(self._e, k)
        if k not in _WRAP_OPS:
            return f

        def call(*args, **kw):
            need = 0
            for a in args:
                t = getattr(a, "tensor", None)
                if t is not None:
                    need = max(need, self._dirty.get(t.name, 0))
            for a in kw.values():
                t = getattr(a, "tensor", None)
                if t is not None:
                    need = max(need, self._dirty.get(t.name, 0))
            if need > self._waited:
                self._e.wait_ge(self._c.h, need)
                self._waited = need
            instr = f(*args, **kw)
            instr.then_inc(self._c.h, 1)
            self._c.n += 1
            self._c._last_auto_id = id(instr)
            out = args[0]
            t = getattr(out, "tensor", None)
            if t is not None:
                self._dirty[t.name] = self._c.n
            return instr
        return call


class Buf:
    def __init__(self, ap):
        self.ap = ap
        self.rel = []

    def wait(self, *engs):
        for ctr, val in self.rel:
            for e in engs:
                e.wait_ge(ctr.h, val)
        self.rel = []

    def release(self, ctr, val=None):
        self.rel.append((ctr, ctr.n if val is None else val))


class Pool:
    def __init__(self, aps, p=None, tag=None):
        self.bufs = [Buf(a) for a in aps]
        self.i = 0
        if p is not None:
            for idx, b in enumerate(self.bufs):
                b.ctr = p.ctr(f"{tag}{idx}")

    def get(self, *engs):
        b = self.bufs[self.i % len(self.bufs)]
        self.i += 1
        b.wait(*engs)
        return b


class Prog:
    def __init__(self, nc, es):
        self.nc = nc
        self.es = es
        self.sy = nc.sync
        self.g = nc.gpsimd
        self.v = nc.vector
        self.sc = nc.scalar
        self.te = nc.tensor
        self._n = 0
        self._named = {}

    def ctr(self, key=None):
        if key is not None and key in self._named:
            return self._named[key]
        self._n += 1
        c = Ctr(self.nc, self.es, f"c{self._n}")
        if key is not None:
            self._named[key] = c
        return c

    def bar(self):
        self.nc.all_engine_barrier()

    @contextlib.contextmanager
    def phase(self):
        with contextlib.ExitStack() as ps:
            def alloc(shape, dtype, name=None):
                self._n += 1
                nm = name or f"t{self._n}"
                return ps.enter_context(
                    self.nc.sbuf_tensor(nm, list(shape), dtype))
            yield alloc



def build_program(trunc=None, debug_dump=None):
    if trunc is None:
        trunc = TRUNC
    if debug_dump is None:
        debug_dump = DEBUG_DUMP
    B, NB, NTC = _derived()

    es = contextlib.ExitStack()
    nc = bass.Bass("TRN2", target_bir_lowering=False,
                   detect_race_conditions=False)
    p = Prog(nc, es)
    ld = p.ctr("ld")
    st = p.ctr("st")
    tec = p.ctr("tec")
    vec = p.ctr("vec")
    sec = p.ctr("sec")
    aux = p.ctr("aux")
    pec = p.ctr("pec")
    v = p.v = EngProxy(nc.vector, vec)
    sc = p.sc = EngProxy(nc.scalar, sec)
    g = p.g = EngProxy(nc.gpsimd, pec)
    sy, te = p.sy, p.te

    def load(dst, src):
        return ld.inc(sy.dma_start(dst, src), 16)

    def store(dst, src):
        return st.inc(g.dma_start(dst, src), 16)

    def din(name, shape):
        return nc.dram_tensor(name, list(shape), F32, kind="ExternalInput").ap()

    x_in = din("x", (D, S))
    cd = {}
    for name, shape in [
        ("COSA", (128, S)), ("SINA", (128, S)), ("COSB", (128, S)), ("SINB", (128, S)),
        ("CMASK", (128, 4 * B)), ("IDENT", (128, 128)), ("ONESF", (128, 128)),
        ("IND4", (128, 4)), ("IND4M", (128, 4)), ("IND4T", (4, 128)),
        ("IND8", (8, 8 * 128)),
        ("AQW", (D, D)), ("AKVW", (D, 2 * D)), ("AOW", (D, D)), ("ARW", (128, CH)),
        ("ARB", (1, 1)), ("AQBR", (1, D)), ("AKVBR", (1, 2 * D)),
        ("AOB", (128, CH)), ("AQNW", (128, CH)), ("AQNB", (128, CH)),
        ("AKVNW", (128, CH)), ("AKVNB", (128, CH)), ("ALN", (128, 4)),
        ("BQKV", (D, 3 * D)), ("BWO", (D, D)),
        ("FWQ", (D, D)), ("FWK", (D, 256)), ("FWV", (D, 256)), ("FWO", (D, D)),
        ("FWR", (D, 8)), ("FEB", (8, 1)), ("FG1", (128, CH)), ("FG2", (128, CH)),
        ("SW1", (D, D)), ("SW3", (D, D)), ("SW2", (D, D)),
        ("EW1", (8, D, D)), ("EW3", (8, D, D)), ("EW2", (8, D, D)),
    ]:
        cd[name] = din(name, shape)
    out_d = nc.dram_tensor("out", [D, S], F32, kind="ExternalOutput").ap()

    x_hbm = nc.dram_tensor("x_hbm", [D, S], F32).ap()
    qn_sp = nc.dram_tensor("qn_sp", [HEADS, HD, S], BF16).ap()
    kn_sp = nc.dram_tensor("kn_sp", [HEADS, HD, S], BF16).ap()
    v_sp = nc.dram_tensor("v_sp", [HEADS, NTC, 128, HD], BF16).ap()
    o_sp = nc.dram_tensor("o_sp", [D, S], BF16).ap()
    cq_hbm = nc.dram_tensor("cq_hbm", [HEADS, S], BF16).ap()
    xmid_hbm = nc.dram_tensor("xmid_hbm", [D, S], F32).ap()
    zf_hbm = nc.dram_tensor("zf_hbm", [D, S], BF16).ap()
    vrow_hbm = nc.dram_tensor("vrow_hbm", [16, S], F32).ap()
    q_sp = nc.dram_tensor("q_sp", [4, 256, S], BF16).ap()
    dbg_scratch = {"x_hbm": x_hbm, "qn_sp": qn_sp, "kn_sp": kn_sp, "v_sp": v_sp,
                   "o_sp": o_sp, "cq_hbm": cq_hbm, "xmid_hbm": xmid_hbm,
                   "zf_hbm": zf_hbm, "q_sp": q_sp, "vrow_hbm": vrow_hbm}

    sb = nc.alloc_sbuf_tensor
    IDE = sb("ide", [128, 128], F32)
    ONE = sb("one", [128, 128], F32)
    ONEB = sb("oneb", [128, 128], BF16)
    IND4 = sb("ind4", [128, 4], F32)
    IND4M = sb("ind4m", [128, 4], F32)
    IND4T = sb("ind4t", [36, 128], F32)
    IND4TB = sb("ind4tb", [4, 128], BF16)
    IND4B = sb("ind4b", [128, 4], BF16)
    IND4MB = sb("ind4mb", [128, 4], BF16)
    IND8B = sb("ind8b", [8, 8 * 128], BF16)
    ALN = sb("aln", [128, 4], F32)
    SVEC = sb("svec", [128, 16], F32)
    BIAS = sb("biasp", [128, 32], F32)
    S11 = sb("s11", [1, 8], F32)
    NVT = 16
    VT = sb("vtt", [128, NTC * NVT], F32)

    def vt(i):
        return VT[:, NTC * i:NTC * (i + 1)]

    PS2 = [nc.alloc_psum_tensor(f"ps2_{i}", [128, 1024], F32) for i in range(3)]
    PS1 = [nc.alloc_psum_tensor("ps1_0", [128, 512], F32),
           nc.alloc_psum_tensor("ps1_1", [128, 512], F32),
           PS2[2][:, 0:512], PS2[2][:, 512:1024]]

    for dst, name in [(IDE, "IDENT"), (ONE, "ONESF"), (IND4, "IND4"),
                      (IND4M, "IND4M"), (ALN, "ALN")]:
        load(dst[:], cd[name][:])
    load(IND4T[0:4, :], cd["IND4T"][:])
    load(IND4T[32:36, :], cd["IND4T"][:])
    store(IND4TB[:], cd["IND4T"][:])
    store(IND4B[:], cd["IND4"][:])
    store(IND4MB[:], cd["IND4M"][:])
    store(ONEB[:], cd["ONESF"][:])
    store(IND8B[:], cd["IND8"][:])
    load(x_hbm[:], x_in[:])
    sy.wait_ge(ld.h, ld.n)
    g.wait_ge(st.h, st.n)
    p.bar()

    VT_MU, VT_VAR, VT_RHO, VT_SC, VT_T1, VT_T2, VT_T3, VT_T4 = range(8)
    VT_X1, VT_X2, VT_X3 = 8, 9, 10


    def bcast_scalar(dst_col, src11):
        te.wait_ge(sec.h, sec.n)
        mm = te.matmul(PS1[3][:, 0:1], ONE[0:1, :], src11, start=True, stop=True)
        tv = tec.inc(mm)
        sc.wait_ge(tec.h, tv)
        sec.inc(sc.copy(dst_col, PS1[3][:, 0:1]))

    def rsqrt_ops(dst, src, eps, tmp1, tmp2, newton=True):
        v.tensor_scalar(tmp1, src, float(eps), None, ALU.add)
        nv = vec.inc(v.tensor_copy(tmp2, tmp1))
        sc.wait_ge(vec.h, nv)
        sc.activation(tmp2, tmp2, AF.Ln)
        ns = sec.inc(sc.activation(dst, tmp2, AF.Exp, scale=-0.5))
        v.wait_ge(sec.h, ns)
        if not newton:
            return v.engine_nop()
        v.tensor_mul(tmp2, dst, dst)
        v.tensor_mul(tmp2, tmp2, tmp1)
        v.tensor_scalar(tmp2, tmp2, -0.5, 1.5, ALU.mult, ALU.add)
        return v.tensor_mul(dst, dst, tmp2)

    vrb = p.ctr("vrb")

    def row_to_vt(slot, row_ap, lane=0):
        g.wait_ge(ld.h, ld.n)
        n = vrb.inc(g.dma_start(vrow_hbm[lane:lane + 1, :], row_ap), 16)
        sy.wait_ge(vrb.h, n)
        return load(vt(slot),
                    vrow_hbm[lane:lane + 1, :].rearrange(
                        "o (pp tc) -> (o pp) tc", tc=NTC))

    def rows_to_vt2(items):
        g.wait_ge(ld.h, ld.n)
        n = 0
        for slot_, row_ap, lane in items:
            n = vrb.inc(g.dma_start(vrow_hbm[lane:lane + 1, :], row_ap), 16)
        sy.wait_ge(vrb.h, n)
        for slot_, row_ap, lane in items:
            load(vt(slot_), vrow_hbm[lane:lane + 1, :].rearrange(
                "o (pp tc) -> (o pp) tc", tc=NTC))

    def vt_to_rows2(items):
        g.wait_ge(ld.h, ld.n)
        n = 0
        for row_ap, slot_, lane in items:
            n = vrb.inc(g.dma_start(vrow_hbm[lane:lane + 1, :].rearrange(
                "o (pp tc) -> (o pp) tc", tc=NTC), vt(slot_)), 16)
        sy.wait_ge(vrb.h, n)
        for row_ap, slot_, lane in items:
            load(row_ap, vrow_hbm[lane:lane + 1, :])

    def vt_to_row(row_ap, slot, lane=0):
        g.wait_ge(ld.h, ld.n)
        n = vrb.inc(g.dma_start(vrow_hbm[lane:lane + 1, :].rearrange(
            "o (pp tc) -> (o pp) tc", tc=NTC), vt(slot)), 16)
        sy.wait_ge(vrb.h, n)
        return load(row_ap, vrow_hbm[lane:lane + 1, :])

    def x_stats(src, alloc, rows_ready=False):
        if rows_ready:
            load(vt(VT_T1), vrow_hbm[0:1, :].rearrange(
                "o (pp tc) -> (o pp) tc", tc=NTC))
            load(vt(VT_T2), vrow_hbm[1:2, :].rearrange(
                "o (pp tc) -> (o pp) tc", tc=NTC))
            sy.wait_ge(ld.h, ld.n)
            p.bar()
            v.tensor_scalar(vt(VT_MU), vt(VT_T1), 1.0 / D, None, ALU.mult)
            v.tensor_scalar(vt(VT_T2), vt(VT_T2), 1.0 / D, None, ALU.mult)
            v.tensor_mul(vt(VT_T3), vt(VT_MU), vt(VT_MU))
            v.tensor_sub(vt(VT_VAR), vt(VT_T2), vt(VT_T3))
            rsqrt_ops(vt(VT_RHO), vt(VT_VAR), 1e-5, vt(VT_T1), vt(VT_T2))
            p.bar()
            return
        XT = [alloc((128, S), F32) for _ in range(2)]
        SQT = [alloc((128, B), F32) for _ in range(2)]
        SROW = alloc((1, S), F32)
        QROW = alloc((1, S), F32)
        xp, sqp = Pool(XT, p, "xr"), Pool(SQT)
        BKS = [PS1[0], PS1[1], PS2[0][:, 0:B], PS2[0][:, B:2 * B],
               PS2[1][:, 0:B], PS2[1][:, B:2 * B], PS1[2], PS1[3]]
        sq_mm = [0, 0]
        for c in range(CH):
            xb = xp.get(sy)
            n_ld = xb.ctr.inc(sy.dma_start(xb.ap[:], src[c * 128:(c + 1) * 128, :]), 16)
            sc.wait_ge(xb.ctr.h, n_ld)
            te.wait_ge(xb.ctr.h, n_ld)
            for b in range(NB):
                blk = slice(b * B, (b + 1) * B)
                sq = sqp.get(sc)
                sc.wait_ge(tec.h, sq_mm[b % 2])
                sv = sec.inc(sc.activation(sq.ap[:], xb.ap[:, blk], AF.Square))
                te.matmul(BKS[b][0:1, 0:B], ONE[:, 0:1], xb.ap[:, blk],
                          start=(c == 0), stop=(c == CH - 1))
                te.wait_ge(sec.h, sv)
                mm = te.matmul(BKS[b][32:33, 0:B], ONE[:, 0:1], sq.ap[:],
                               start=(c == 0), stop=(c == CH - 1))
                sq_mm[b % 2] = tec.inc(mm)
                if c == CH - 1:
                    v.wait_ge(tec.h, sq_mm[b % 2])
                    v.tensor_copy(SROW[:, blk], BKS[b][0:1, 0:B])
                    vec.inc(v.tensor_copy(QROW[:, blk], BKS[b][32:33, 0:B]))
            xb.release(tec, tec.n)
            xb.release(sec, sec.n)
        p.bar()
        rows_to_vt2([(VT_T1, SROW[:], 0), (VT_T2, QROW[:], 1)])
        sy.wait_ge(ld.h, ld.n)
        p.bar()
        v.tensor_scalar(vt(VT_MU), vt(VT_T1), 1.0 / D, None, ALU.mult)
        v.tensor_scalar(vt(VT_T2), vt(VT_T2), 1.0 / D, None, ALU.mult)
        v.tensor_mul(vt(VT_T3), vt(VT_MU), vt(VT_MU))
        v.tensor_sub(vt(VT_VAR), vt(VT_T2), vt(VT_T3))
        rsqrt_ops(vt(VT_RHO), vt(VT_VAR), 1e-5, vt(VT_T1), vt(VT_T2))
        p.bar()

    def build_norm_operand(YH, scale_slot, src, alloc, gcol=None):
        MUB = alloc((128, S), F32)
        SCB = alloc((128, S), F32)
        with p.phase() as alloc_r:
            MROW = alloc_r((1, S), F32)
            CROW = alloc_r((1, S), F32)
            vt_to_rows2([(MROW[:], VT_MU, 0), (CROW[:], scale_slot, 1)])
            sy.wait_ge(ld.h, ld.n)
            p.bar()
            bp = Pool([ps[:, 0:B] for ps in PS1])
            for b in range(NB):
                blk = slice(b * B, (b + 1) * B)
                bm = bp.get(te)
                bs = bp.get(te)
                te.matmul(bm.ap, ONE[0:1, :], MROW[:, blk], start=True, stop=True)
                mm = te.matmul(bs.ap, ONE[0:1, :], CROW[:, blk], start=True, stop=True)
                tv = tec.inc(mm)
                sc.wait_ge(tec.h, tv)
                sc.copy(MUB[:, blk], bm.ap)
                ss = sec.inc(sc.copy(SCB[:, blk], bs.ap))
                bm.release(sec, ss)
                bs.release(sec, ss)
            p.bar()
        XT = [alloc((128, S), F32) for _ in range(2)]
        STG = alloc((128, B), F32)
        STGP = alloc((128, B), F32)
        xp = Pool(XT, p, "xr")
        for c in range(CH):
            xb = xp.get(sy)
            n_ld = xb.ctr.inc(sy.dma_start(xb.ap[:], src[c * 128:(c + 1) * 128, :]), 16)
            e, scr = (v, STG) if c % 8 < 7 else (g, STGP)
            ec = vec if c % 8 < 7 else pec
            for b in range(NB):
                blk = slice(b * B, (b + 1) * B)
                e.wait_ge(xb.ctr.h, n_ld)
                e.tensor_sub(scr[:], xb.ap[:, blk], MUB[:, blk])
                if gcol is None:
                    ins = e.tensor_mul(YH[c][:, blk], scr[:], SCB[:, blk])
                else:
                    e.tensor_mul(scr[:], scr[:], SCB[:, blk])
                    ins = e.tensor_scalar(YH[c][:, blk], scr[:],
                                          gcol[:, c:c + 1], None, ALU.mult)
                ec.inc(ins)
            xb.release(ec, ec.n)
        p.bar()

    def prep_weight_cast(dst_bf, w_dram, col0, ncols):
        for c in range(CH):
            store(dst_bf[:, c * ncols:(c + 1) * ncols],
                  w_dram[c * 128:(c + 1) * 128, col0:col0 + ncols])
        return st.n

    def prep_weight_scaled(dst_bf, w_dram, col0, ncols, rowscale, wst_pool,
                           bias_rhs=None, bias_psum=None):
        if bias_rhs is not None:
            te.wait_ge(vec.h, vec.n)
        for c in range(CH):
            wb = wst_pool.get(sy)
            n_ld = wb.ctr.inc(sy.dma_start(
                wb.ap[:, 0:ncols],
                w_dram[c * 128:(c + 1) * 128, col0:col0 + ncols]), 16)
            if bias_rhs is not None:
                te.wait_ge(wb.ctr.h, n_ld)
                off = 0
                mm = None
                for psl in bias_psum:
                    w_ = min(512, ncols - off)
                    mm = te.matmul(psl[:, 0:w_], bias_rhs[:, c:c + 1],
                                   wb.ap[:, off:off + w_],
                                   start=(c == 0), stop=(c == CH - 1))
                    off += w_
                wb.release(tec, tec.inc(mm))
            v.wait_ge(wb.ctr.h, n_ld)
            ins = v.tensor_scalar(dst_bf[:, c * ncols:(c + 1) * ncols],
                                  wb.ap[:, 0:ncols], rowscale[:, c:c + 1],
                                  None, ALU.mult)
            wb.release(vec, vec.inc(ins))

    def attn_core(causal):
        with p.phase() as alloc:
            KNB = [alloc((65, S), BF16) for _ in range(2)]
            QNB = [alloc((65, S), BF16) for _ in range(2)]
            VBUF = [alloc((128, 65 * NTC), BF16) for _ in range(2)]
            PT = alloc((128, NTC * B), BF16)
            OH = [alloc((64, S), BF16) for _ in range(2)]
            RDNR = alloc((1, S), F32)
            RDNB2 = alloc((1, S), BF16)
            if causal:
                CMB2 = alloc((128, 4 * B), BF16)
                store(CMB2[:], cd["CMASK"][:])
            for t_ in KNB + QNB:
                g.memset(t_[64:65, :], 1.0)
            for t_ in VBUF:
                g.memset(t_[:], 0.0)
            g.wait_ge(st.h, st.n)
            p.bar()
            for t_ in VBUF:
                for j in range(NTC):
                    g.memset(t_[:, 65 * j + 64:65 * j + 65], 1.0)
            p.bar()
            knp = Pool(KNB, p, "knp")
            qnp = Pool(QNB, p, "qnp")
            vbp = Pool(VBUF, p, "vbp")
            ohp = Pool(OH, p, "ohp")
            SLOT3 = [PS2[0][:, 0:2 * B], PS2[1][:, 0:2 * B], PS2[2][:, 0:2 * B]]
            ACCF = [PS1[0], PS1[1]]
            st8 = {"evac": 0, "bcst": 0, "guard": [0, 0], "gg": 0,
                   "slot_guard": [0, 0, 0]}
            todo = []
            exp_val = {}

            def mk_recip(qsl, ev, nxt):
                def fn():
                    v.reciprocal(RDNR[0:1, qsl], RDNR[0:1, qsl])
                    rdnb = vec.inc(v.tensor_copy(RDNB2[0:1, qsl], RDNR[0:1, qsl]))
                    nxt(rdnb)
                return fn

            def mk_norm(ob, h, qblk, gready):
                def sched(rdnb):
                    def fn():
                        nql = slice(qblk * B, (qblk + 1) * B)
                        gnow = st8["gg"]
                        t_ = (gnow + 2) % 3
                        gl = next(gnow - d for d in range(3)
                                  if (gnow - d) % 3 == t_)
                        te.wait_ge(vec.h, max(rdnb, st8["bcst"]))
                        if gl in exp_val:
                            te.wait_ge(sec.h, exp_val[gl])
                        mm2 = te.matmul(SLOT3[t_][64:128, 0:B], ONEB[0:1, 0:64],
                                        RDNB2[0:1, nql], start=True, stop=True)
                        tv2 = tec.inc(mm2)
                        v.wait_ge(tec.h, tv2)
                        ins = v.tensor_mul(ob.ap[:, nql], ob.ap[:, nql],
                                           SLOT3[t_][64:128, 0:B])
                        st8["bcst"] = st8["slot_guard"][t_] = vec.inc(ins)
                        if qblk == NB - 1:
                            todo.append((gready + 2, mk_store(ob, h)))
                    todo.append((gready + 2, fn))
                return sched

            def mk_store(ob, h):
                def fn():
                    g.wait_ge(vec.h, st8["bcst"])
                    n_st = ob.ctr.inc(g.dma_start(
                        o_sp[h * 64:(h + 1) * 64, :], ob.ap[:]), 16)
                    ob.release(ob.ctr, n_st)
                return fn

            for h in range(HEADS):
                kb = knp.get(sy)
                nk = kb.ctr.inc(sy.dma_start(kb.ap[0:64, :], kn_sp[h, :, :]), 16)
                qb = qnp.get(sy)
                qb.ctr.inc(sy.dma_start(qb.ap[0:64, :], qn_sp[h, :, :]), 16)
                nq = qb.ctr.inc(sy.dma_start(qb.ap[64:65, :], cq_hbm[h:h + 1, :]), 16)
                vb = vbp.get(sy)
                nv3 = vb.ctr.inc(sy.dma_start(
                    vb.ap.rearrange("pp (tc cc) -> pp tc cc", cc=65)[:, :, 0:64],
                    v_sp[h, :, :, :].rearrange("tc pp dd -> pp tc dd")), 16)
                te.wait_ge(kb.ctr.h, nk)
                te.wait_ge(qb.ctr.h, nq)
                te.wait_ge(vb.ctr.h, nv3)
                ob = ohp.get(v)
                groups = []
                for qblk in range(NB):
                    ntc_v = 4 * (qblk + 1) if causal else NTC
                    for gi in range(ntc_v // 2):
                        groups.append((qblk, gi, ntc_v))
                ggs = []
                mask_done = {}
                maskw = {}

                def emit_av_pair(idx):
                    qblk, gi, ntc_v = groups[idx]
                    acc_i = qblk % 2
                    qsl = slice(qblk * B, (qblk + 1) * B)
                    gg = ggs[idx]
                    for k2 in range(2):
                        tc = 2 * gi + k2
                        if (qblk, tc) in mask_done:
                            te.wait_ge(pec.h, mask_done[(qblk, tc)])
                        else:
                            te.wait_ge(sec.h, exp_val[gg])
                        if tc == 0:
                            te.wait_ge(vec.h, st8["guard"][acc_i])
                        mm = te.matmul(ACCF[acc_i][0:65, 0:B],
                                       vb.ap[:, 65 * tc:65 * tc + 65],
                                       PT[:, tc * B:(tc + 1) * B],
                                       start=(tc == 0), stop=(tc == ntc_v - 1))
                        if tc == ntc_v - 1:
                            tv = tec.inc(mm)
                            v.wait_ge(tec.h, tv)
                            v.tensor_copy(ob.ap[:, qsl], ACCF[acc_i][0:64, 0:B])
                            ins = v.tensor_copy(RDNR[0:1, qsl],
                                                ACCF[acc_i][64:65, 0:B])
                            st8["evac"] = st8["guard"][acc_i] = vec.inc(ins)
                            gnow = st8["gg"]
                            todo.append((gnow + 2, mk_recip(
                                qsl, st8["evac"], mk_norm(ob, h, qblk, gnow + 2))))

                for idx, (qblk, gi, ntc_v) in enumerate(groups):
                    gg = st8["gg"]
                    st8["gg"] += 1
                    ggs.append(gg)
                    slot = SLOT3[gg % 3]
                    if gg >= 3:
                        te.wait_ge(sec.h, exp_val[gg - 3])
                    sg = st8["slot_guard"][gg % 3]
                    if sg:
                        te.wait_ge(vec.h, sg)
                        st8["slot_guard"][gg % 3] = 0
                    qsl = slice(qblk * B, (qblk + 1) * B)
                    for k2 in range(2):
                        tc = 2 * gi + k2
                        mm = te.matmul(slot[:, B * k2:B * (k2 + 1)],
                                       kb.ap[:, tc * 128:(tc + 1) * 128],
                                       qb.ap[:, qsl], start=True, stop=True)
                    tv = tec.inc(mm)
                    mw = max(maskw.get(2 * gi, 0), maskw.get(2 * gi + 1, 0))
                    if mw:
                        sc.wait_ge(pec.h, mw)
                    sc.wait_ge(tec.h, tv)
                    ins = sc.activation(PT[:, 2 * gi * B:(2 * gi + 2) * B],
                                        slot[:, 0:2 * B], AF.Exp)
                    exp_val[gg] = sec.inc(ins)
                    if causal:
                        for k2 in range(2):
                            tc = 2 * gi + k2
                            jm = tc - 4 * qblk
                            if jm >= 0:
                                g.wait_ge(sec.h, exp_val[gg])
                                insm = g.tensor_mul(
                                    PT[:, tc * B:(tc + 1) * B],
                                    PT[:, tc * B:(tc + 1) * B],
                                    CMB2[:, jm * B:(jm + 1) * B])
                                mask_done[(qblk, tc)] = maskw[tc] = pec.inc(insm)
                    lag = 3 if causal else 2
                    if idx >= lag:
                        emit_av_pair(idx - lag)
                    while todo and todo[0][0] <= st8["gg"]:
                        todo.pop(0)[1]()
                lag = 3 if causal else 2
                for j_ in range(lag):
                    emit_av_pair(len(groups) - lag + j_)
                kb.release(tec, tec.n)
                qb.release(tec, tec.n)
                vb.release(tec, tec.n)
            while todo:
                todo.pop(0)[1]()
            for b_ in ohp.bufs:
                g.wait_ge(b_.ctr.h, b_.ctr.n)
            p.bar()


    def attn_a(is_last):
        with p.phase() as alloc:
            x_stats(x_hbm, alloc)
        if debug_dump == "vrow_hbm":
            n = vrb.inc(g.dma_start(vrow_hbm[13:15, :], vrow_hbm[0:2, :]), 16)
            g.wait_ge(vrb.h, n)
            p.bar()
        v.tensor_mul(vt(VT_T1), vt(VT_RHO), vt(VT_RHO))
        v.tensor_mul(vt(VT_T1), vt(VT_T1), vt(VT_VAR))
        rsqrt_ops(vt(VT_T2), vt(VT_T1), 1e-5, vt(VT_T3), vt(VT_T4))
        v.tensor_mul(vt(VT_SC), vt(VT_RHO), vt(VT_T2))
        v.reciprocal(vt(VT_X1), vt(VT_T2))
        p.bar()
        if debug_dump == "vrow_hbm":
            for lane_, slot_ in ((8, VT_MU), (9, VT_VAR), (10, VT_RHO), (11, VT_SC)):
                g.wait_ge(ld.h, ld.n)
                nn = vrb.inc(g.dma_start(vrow_hbm[lane_:lane_ + 1, :].rearrange(
                    "o (pp tc) -> (o pp) tc", tc=NTC), vt(slot_)), 16)
            g.wait_ge(vrb.h, vrb.n)
            p.bar()

        with p.phase() as alloc_o:
            YH = [alloc_o((128, S), BF16) for _ in range(CH)]
            with p.phase() as alloc:
                build_norm_operand(YH, VT_SC, x_hbm, alloc)
            if debug_dump == "zf_hbm":
                for c in range(CH):
                    store(zf_hbm[c * 128:(c + 1) * 128, :], YH[c][:])
                g.wait_ge(st.h, st.n)
                p.bar()

            S4ALL = alloc_o((4, S), BF16)
            n = load(S11[0:1, 0:1], cd["ARB"][:])
            sy.wait_ge(ld.h, n)
            p.bar()
            bcast_scalar(SVEC[:, 0:1], S11[0:1, 0:1])
            with p.phase() as alloc:
                ARWS = alloc((128, CH), F32)
                ARWB = alloc((128, CH), BF16)
                PROW = alloc((1, S), F32)
                SROW = alloc((1, S), F32)
                n = load(ARWS[:], cd["ARW"][:])
                v.wait_ge(ld.h, n)
                v.tensor_copy(ARWB[:], ARWS[:])
                g.memset(PROW[:], 0.0)
                p.bar()
                bp = Pool([ps[0:1, 0:B] for ps in PS1])
                for c in range(CH):
                    for b in range(NB):
                        blk = slice(b * B, (b + 1) * B)
                        bank = bp.get(te)
                        mm = te.matmul(bank.ap, ARWB[:, c:c + 1], YH[c][:, blk],
                                       start=True, stop=True)
                        tv = tec.inc(mm)
                        v.wait_ge(tec.h, tv)
                        ins = v.tensor_add(PROW[:, blk], PROW[:, blk], bank.ap)
                        bank.release(vec, vec.inc(ins))
                p.bar()
                row_to_vt(VT_T1, PROW[:])
                sy.wait_ge(ld.h, ld.n)
                p.bar()
                v.tensor_mul(vt(VT_T1), vt(VT_T1), vt(VT_X1))
                v.tensor_scalar(vt(VT_T1), vt(VT_T1), SVEC[:, 0:1], None, ALU.add)
                v.tensor_scalar(vt(VT_T1), vt(VT_T1), 0.01, None, ALU.mult)
                v.tensor_scalar(vt(VT_T1), vt(VT_T1), 0.5, None, ALU.max)
                v.tensor_scalar(vt(VT_X2), vt(VT_T1), 2.0, None, ALU.min)
                p.bar()
                vt_to_row(SROW[:], VT_X2)
                sy.wait_ge(ld.h, ld.n)
                p.bar()
                for b in range(NB):
                    blk = slice(b * B, (b + 1) * B)
                    te.wait_ge(sec.h, sec.n)
                    mm = te.matmul(PS1[0][0:4, 0:B], ONE[0:1, 0:4], SROW[:, blk],
                                   start=True, stop=True)
                    tv = tec.inc(mm)
                    sc.wait_ge(tec.h, tv)
                    sec.inc(sc.copy(S4ALL[:, blk], PS1[0][0:4, 0:B]))
                p.bar()
            v.tensor_scalar(BIAS[:, 0:4], ALN[:, 0:4], 0.125, None, ALU.mult)
            g.memset(BIAS[0:4, 16:20], 0.0)
            p.bar()

            def head_groups(is_q):
                wsrc = cd["AQW"] if is_q else cd["AKVW"]
                nw = cd["AQNW"] if is_q else cd["AKVNW"]
                nb_ = cd["AQNB"] if is_q else cd["AKVNB"]
                br = cd["AQBR"] if is_q else cd["AKVBR"]
                spill = qn_sp if is_q else kn_sp
                lnsrc = BIAS if is_q else ALN
                with p.phase() as alloc:
                    NWS = alloc((128, CH), F32)
                    NBS = alloc((128, CH), F32)
                    BRS = [alloc((1, 256), F32) for _ in range(2)]
                    TCOS = alloc((128, S), BF16)
                    TSIN = alloc((128, S), BF16)
                    RA = alloc((128, S), BF16)
                    RB = alloc((128, S), BF16)
                    STG = [alloc((128, 2 * B), BF16) for _ in range(2)]
                    SQP = [alloc((128, B), BF16) for _ in range(2)]
                    SQ2P = [alloc((128, B), BF16) for _ in range(2)]
                    WQG = [alloc((128, 2048), BF16) for _ in range(2)]
                    WST = [alloc((128, 256), F32) for _ in range(2)]
                    QNE = alloc((128, S), BF16)
                    QNO = alloc((128, S), BF16)
                    BCS = [alloc((128, 2 * B), BF16) for _ in range(2)]
                    M0R = [alloc((4, S), BF16) for _ in range(2)]
                    GR = [alloc((4, S), BF16) for _ in range(2)]
                    CR = alloc((4, S), BF16)
                    SB = alloc((4, 2 * B), F32)

                    def sbt(par):
                        return SB[:, par * B:(par + 1) * B]

                    load(NWS[:], nw[:])
                    load(NBS[:], nb_[:])
                    store(TCOS[:], cd["COSA"][:])
                    store(TSIN[:], cd["SINA"][:])
                    sy.wait_ge(ld.h, ld.n)
                    g.wait_ge(st.h, st.n)
                    p.bar()
                    PJ = [PS2[1][:, 0:B], PS2[1][:, B:2 * B]]
                    P0 = [PS2[0][:, 0:B], PS2[0][:, B:2 * B]]
                    BC0 = [PS1[0][:, 0:B], PS1[1][:, 0:B]]
                    BC1 = [PS1[2][:, 0:B], PS1[3][:, 0:B]]
                    pjp = Pool(PJ)
                    gd = {"bias_ld": 0, "spill_st": 0, "cq_st": 0, "cr_free": 0,
                          "apply": [0] * NB, "applyp": [0] * NB,
                          "rot": [0] * NB, "rotp": [0] * NB,
                          "proj": [0, 0], "bc": [0, 0], "bcp": [0, 0],
                          "bcv": [0, 0], "sq_read": [0, 0],
                          "p0v": [0, 0], "p0c": [0, 0],
                          "gr_done": [0, 0], "m0_done": [0, 0],
                          "gr_read": [0, 0]}

                    def prep_chunk(gi, c):
                        par = gi % 2
                        wb = wstp.get(sy)
                        n_ld = wb.ctr.inc(sy.dma_start(
                            wb.ap[:, 0:256],
                            wsrc[c * 128:(c + 1) * 128,
                                 256 * gi:256 * gi + 256]), 16)
                        if c == 0:
                            n_br = load(BRS[par][:],
                                        br[:, 256 * gi:256 * gi + 256])
                            gd["brld"] = n_br
                        bank = pjp.get(te)
                        te.wait_ge(wb.ctr.h, n_ld)
                        mm = te.matmul(bank.ap[0:1, 0:256], NBS[:, c:c + 1],
                                       wb.ap[:, 0:256], start=True, stop=True)
                        tv = tec.inc(mm)
                        v.wait_ge(tec.h, tv)
                        if c == 0:
                            v.wait_ge(ld.h, gd["brld"])
                        ins = v.tensor_add(BRS[par][:], BRS[par][:],
                                           bank.ap[0:1, 0:256])
                        bank.release(vec, vec.inc(ins))
                        v.wait_ge(wb.ctr.h, n_ld)
                        if c == 0:
                            v.wait_ge(tec.h, gd["proj"][par])
                        ins = v.tensor_scalar(
                            WQG[par][:, c * 256:(c + 1) * 256],
                            wb.ap[:, 0:256], NWS[:, c:c + 1], None, ALU.mult)
                        wb.release(vec, vec.inc(ins))

                    def prep_finish(gi):
                        par = gi % 2
                        g.wait_ge(vec.h, vec.n)
                        lane = 2 + par
                        n1 = vrb.inc(g.dma_start(
                            vrow_hbm[lane:lane + 1, 0:256], BRS[par][:]), 16)
                        sy.wait_ge(vrb.h, n1)
                        load(BIAS[:, 8 + 2 * par:9 + 2 * par],
                             vrow_hbm[lane:lane + 1, 0:128].rearrange("a b -> b a"))
                        gd["bias_ld"] = load(
                            BIAS[:, 9 + 2 * par:10 + 2 * par],
                            vrow_hbm[lane:lane + 1, 128:256].rearrange("a b -> b a"))

                    def proj_block(gi, b):
                        par = gi % 2
                        blk = slice(b * B, (b + 1) * B)
                        for half, dst in ((0, RA), (1, RB)):
                            bank = pjp.get(te)
                            for c in range(CH):
                                mm = te.matmul(
                                    bank.ap,
                                    WQG[par][:, c * 256 + 128 * half:
                                              c * 256 + 128 * (half + 1)],
                                    YH[c][:, blk], start=(c == 0),
                                    stop=(c == CH - 1))
                            tv = tec.inc(mm)
                            if b == 0 and half == 0:
                                sc.wait_ge(ld.h, gd["bias_ld"])
                            sc.wait_ge(tec.h, tv)
                            if half == 0:
                                sc.wait_ge(vec.h, gd["apply"][b])
                                sc.wait_ge(pec.h, gd["applyp"][b])
                            ins = sc.activation(
                                dst[:, blk], bank.ap, AF.Identity,
                                bias=BIAS[:, 8 + 2 * par + half:
                                          9 + 2 * par + half])
                            bank.release(sec, sec.inc(ins))
                        gd["proj"][par] = tec.n
                        return sec.n

                    def rotary_block(b, bias_done):
                        par = b % 2
                        blk = slice(b * B, (b + 1) * B)
                        v.wait_ge(sec.h, bias_done)
                        v.wait_ge(tec.h, gd["sq_read"][par])
                        v.tensor_mul(SQP[par][:], RA[:, blk], TCOS[:, blk])
                        v.tensor_mul(SQ2P[par][:], RB[:, blk], TSIN[:, blk])
                        g.wait_ge(sec.h, bias_done)
                        g.wait_ge(vec.h, gd["apply"][b])
                        g.tensor_mul(STG[par][:, 0:B], RA[:, blk], TSIN[:, blk])
                        ins = g.tensor_mul(STG[par][:, B:2 * B], RB[:, blk],
                                           TCOS[:, blk])
                        gr = pec.inc(ins)
                        v.wait_ge(pec.h, gr)
                        ins = v.tensor_sub(RA[:, blk], SQP[par][:], SQ2P[par][:])
                        gd["rot"][b] = vec.inc(ins)
                        g.wait_ge(vec.h, gd["rot"][b])
                        ins = g.tensor_add(RB[:, blk], STG[par][:, 0:B],
                                           STG[par][:, B:2 * B])
                        gd["rotp"][b] = pec.inc(ins)

                    def stats_a(gi, b):
                        par = b % 2
                        gpar = gi % 2
                        blk = slice(b * B, (b + 1) * B)
                        sc.wait_ge(vec.h, gd["rot"][b])
                        sc.wait_ge(pec.h, gd["rotp"][b])
                        sc.wait_ge(tec.h, gd["sq_read"][par])
                        sc.activation(SQP[par][:], RA[:, blk], AF.Square)
                        s2 = sec.inc(sc.activation(SQ2P[par][:], RB[:, blk],
                                                   AF.Square))
                        te.wait_ge(sec.h, max(s2, gd["p0c"][par]))
                        te.wait_ge(vec.h, gd["p0v"][par])
                        te.matmul(P0[par][0:4, 0:B], IND4MB[:], RA[:, blk],
                                  start=True, stop=False)
                        te.matmul(P0[par][0:4, 0:B], IND4MB[:], RB[:, blk],
                                  start=False, stop=True)
                        te.matmul(P0[par][32:36, 0:B], IND4MB[:], SQP[par][:],
                                  start=True, stop=False)
                        mm = te.matmul(P0[par][32:36, 0:B], IND4MB[:],
                                       SQ2P[par][:], start=False, stop=True)
                        tv = tec.inc(mm)
                        gd["sq_read"][par] = tv
                        sc.wait_ge(tec.h, max(tv, gd["gr_read"][gpar]))
                        m0c = sec.inc(sc.copy(M0R[gpar][:, blk],
                                              P0[par][0:4, 0:B]))
                        gd["m0_done"][gpar] = m0c
                        v.wait_ge(sec.h, m0c)
                        v.wait_ge(tec.h, gd["gr_read"][gpar])
                        v.tensor_mul(sbt(par), M0R[gpar][:, blk],
                                     M0R[gpar][:, blk])
                        v.tensor_sub(sbt(par), P0[par][32:36, 0:B], sbt(par))
                        v.tensor_mul(sbt(par), sbt(par), S4ALL[:, blk])
                        v.tensor_mul(sbt(par), sbt(par), S4ALL[:, blk])
                        ins = v.tensor_scalar(GR[gpar][:, blk], sbt(par), 1e-5,
                                              None, ALU.add)
                        gd["p0v"][par] = vec.inc(ins)

                    def derive_mid(gi):
                        gpar = gi % 2
                        sc.wait_ge(vec.h, gd["p0v"][0])
                        sc.wait_ge(vec.h, gd["p0v"][1])
                        sc.activation(GR[gpar][:], GR[gpar][:], AF.Ln)
                        lg = sec.inc(sc.activation(GR[gpar][:], GR[gpar][:],
                                                   AF.Exp, scale=-0.5))
                        v.wait_ge(sec.h, lg)
                        gd["gr_done"][gpar] = vec.inc(
                            v.tensor_mul(GR[gpar][:], GR[gpar][:], S4ALL[:]))

                    def apply_block(gi, b):
                        par = b % 2
                        gpar = gi % 2
                        blk = slice(b * B, (b + 1) * B)
                        te.wait_ge(sec.h, gd["bc"][par])
                        if b == 0:
                            te.wait_ge(vec.h, gd["gr_done"][gpar])
                            te.wait_ge(sec.h, gd["m0_done"][gpar])
                        te.matmul(BC0[par], IND4TB[:], M0R[gpar][:, blk],
                                  start=True, stop=True)
                        mm = te.matmul(BC1[par], IND4TB[:], GR[gpar][:, blk],
                                       start=True, stop=True)
                        tv = tec.inc(mm)
                        sc.wait_ge(tec.h, tv)
                        sc.wait_ge(vec.h, gd["bcv"][par])
                        sc.wait_ge(pec.h, gd["bcp"][par])
                        sc.copy(BCS[par][:, 0:B], BC0[par])
                        cpd = sec.inc(sc.copy(BCS[par][:, B:2 * B], BC1[par]))
                        gd["bc"][par] = cpd
                        v.wait_ge(sec.h, cpd)
                        v.wait_ge(st.h, gd["spill_st"])
                        v.wait_ge(pec.h, gd["rotp"][b])
                        v.tensor_sub(STG[par][:, 0:B], RA[:, blk],
                                     BCS[par][:, 0:B])
                        v.tensor_mul(STG[par][:, 0:B], STG[par][:, 0:B],
                                     BCS[par][:, B:2 * B])
                        ins = v.tensor_scalar(QNE[:, blk], STG[par][:, 0:B],
                                              lnsrc[:, 0:1], lnsrc[:, 1:2],
                                              ALU.mult, ALU.add)
                        gd["apply"][b] = gd["bcv"][par] = vec.inc(ins)
                        g.wait_ge(sec.h, cpd)
                        g.wait_ge(st.h, gd["spill_st"])
                        g.tensor_sub(STG[par][:, B:2 * B], RB[:, blk],
                                     BCS[par][:, 0:B])
                        g.tensor_mul(STG[par][:, B:2 * B], STG[par][:, B:2 * B],
                                     BCS[par][:, B:2 * B])
                        ins = g.tensor_scalar(QNO[:, blk], STG[par][:, B:2 * B],
                                              lnsrc[:, 2:3], lnsrc[:, 3:4],
                                              ALU.mult, ALU.add)
                        gd["applyp"][b] = gd["bcp"][par] = pec.inc(ins)
                        sc.wait_ge(vec.h, gd["apply"][b])
                        sc.wait_ge(pec.h, gd["applyp"][b])
                        sc.wait_ge(tec.h, gd["sq_read"][par])
                        sc.activation(SQP[par][:], QNE[:, blk], AF.Square)
                        s2 = sec.inc(sc.activation(SQ2P[par][:], QNO[:, blk],
                                                   AF.Square))
                        te.wait_ge(sec.h, max(s2, gd["p0c"][par]))
                        te.matmul(P0[par][64:68, 0:B], IND4B[:], SQP[par][:],
                                  start=True, stop=False)
                        mm = te.matmul(P0[par][64:68, 0:B], IND4B[:],
                                       SQ2P[par][:], start=False, stop=True)
                        tv = tec.inc(mm)
                        gd["sq_read"][par] = tv
                        sc.wait_ge(tec.h, tv)
                        sc.wait_ge(st.h, gd["cq_st"])
                        sc.wait_ge(vec.h, gd["cr_free"])
                        gd["p0c"][par] = sec.inc(
                            sc.copy(CR[:, blk], P0[par][64:68, 0:B]))

                    def d_tail(gi):
                        if is_q:
                            sq = sec.inc(sc.activation(CR[:], CR[:], AF.Sqrt))
                            v.wait_ge(sec.h, sq)
                            ins = v.tensor_scalar(CR[:], CR[:],
                                                  SVEC[0:4, 4 + gi:5 + gi],
                                                  None, ALU.mult)
                            gd["cr_free"] = vec.inc(ins)
                            g.wait_ge(vec.h, gd["cr_free"])
                            for h4 in range(4):
                                store(cq_hbm[4 * gi + h4:4 * gi + h4 + 1, :],
                                      CR[h4:h4 + 1, :])
                            gd["cq_st"] = st.n
                        else:
                            v.wait_ge(sec.h, max(gd["p0c"]))
                            v.tensor_reduce(sbt(0)[:, 0:1], CR[:], AX.X,
                                            ALU.max)
                            ins = v.tensor_max(BIAS[0:4, 16 + gi:17 + gi],
                                               BIAS[0:4, 16 + gi:17 + gi],
                                               sbt(0)[:, 0:1])
                            gd["cr_free"] = vec.inc(ins)
                        g.wait_ge(vec.h, gd["apply"][NB - 1])
                        g.wait_ge(pec.h, gd["applyp"][NB - 1])
                        for h4 in range(4):
                            rs = slice(32 * h4, 32 * (h4 + 1))
                            store(spill[4 * gi + h4, 0:32, :], QNE[rs, :])
                            store(spill[4 * gi + h4, 32:64, :], QNO[rs, :])
                        gd["spill_st"] = st.n

                    wstp = Pool(WST, p, "wst")
                    for c in range(CH):
                        prep_chunk(0, c)
                    prep_finish(0)
                    for b in range(NB):
                        bias_done = proj_block(0, b)
                        rotary_block(b, bias_done)
                        if NG > 1:
                            prep_chunk(1, b)
                        if b >= 1:
                            stats_a(0, b - 1)
                    stats_a(0, NB - 1)
                    derive_mid(0)
                    if NG > 1:
                        prep_finish(1)
                    for gi in range(NG):
                        for b in range(NB):
                            apply_block(gi, b)
                            if gi + 1 < NG:
                                bias_done = proj_block(gi + 1, b)
                                rotary_block(b, bias_done)
                                if gi + 2 < NG:
                                    prep_chunk(gi + 2, b)
                                if b >= 1:
                                    stats_a(gi + 1, b - 1)
                        gd["gr_read"][gi % 2] = tec.n
                        if gi + 1 < NG:
                            stats_a(gi + 1, NB - 1)
                            derive_mid(gi + 1)
                            if gi + 2 < NG:
                                prep_finish(gi + 2)
                        d_tail(gi)
                    g.wait_ge(st.h, st.n)
                    p.bar()
                    if not is_q:
                        ss = sec.inc(sc.activation(BIAS[0:4, 20:24],
                                                   BIAS[0:4, 16:20], AF.Sqrt))
                        v.wait_ge(sec.h, ss)
                        v.tensor_scalar(SVEC[0:4, 4:8], BIAS[0:4, 20:24],
                                        -1.0, None, ALU.mult)
                        p.bar()

            head_groups(is_q=False)
            head_groups(is_q=True)

            with p.phase() as alloc:
                WVB = alloc((128, CH * 1024), BF16)
                WST = [alloc((128, 1024), F32) for _ in range(2)]
                NWS = alloc((128, CH), F32)
                NBS = alloc((128, CH), F32)
                KVBR = alloc((1, 1024), F32)
                BVB = alloc((1, 1024), BF16)
                VST = [alloc((128, 512), BF16) for _ in range(2)]
                load(NWS[:], cd["AKVNW"][:])
                load(NBS[:], cd["AKVNB"][:])
                n3 = load(KVBR[:], cd["AKVBR"][:, 1024:2048])
                sy.wait_ge(ld.h, ld.n)
                p.bar()
                wstp = Pool(WST, p, "wst")
                prep_weight_scaled(WVB, cd["AKVW"], 1024, 1024, NWS[:], wstp,
                                   bias_rhs=NBS[:],
                                   bias_psum=[PS2[1][0:1, 0:512], PS2[1][0:1, 512:1024]])
                tv = tec.n
                v.wait_ge(tec.h, tv)
                v.tensor_add(KVBR[:], KVBR[:], PS2[1][0:1, 0:1024])
                nv = vec.inc(v.tensor_copy(BVB[:], KVBR[:]))
                p.bar()
                vp = Pool(VST, p, "vst")
                pp = Pool([ps[:, 0:512] for ps in PS1[0:3]])
                for tc in range(NTC):
                    tsl = slice(tc * 128, (tc + 1) * 128)
                    for vb2 in range(2):
                        bank = pp.get(te)
                        for c in range(CH):
                            te.matmul(bank.ap, YH[c][:, tsl],
                                      WVB[:, c * 1024 + 512 * vb2:c * 1024 + 512 * (vb2 + 1)],
                                      start=(c == 0), stop=False)
                        mm = te.matmul(bank.ap, ONEB[0:1, :],
                                       BVB[:, 512 * vb2:512 * (vb2 + 1)],
                                       start=False, stop=True)
                        tv = tec.inc(mm)
                        vbuf = vp.get(sc)
                        sc.wait_ge(tec.h, tv)
                        sv = sec.inc(sc.copy(vbuf.ap[:], bank.ap))
                        bank.release(sec, sv)
                        g.wait_ge(sec.h, sv)
                        n_st = vbuf.ctr.inc(g.dma_start(
                            v_sp[8 * vb2:8 * (vb2 + 1), tc:tc + 1, :, :].rearrange(
                                "h o pp dd -> (o pp) h dd"),
                            vbuf.ap.rearrange("pp (h dd) -> pp h dd", h=8)), 16)
                        vbuf.release(vbuf.ctr, n_st)
                g.wait_ge(st.h, st.n)
                p.bar()

        attn_core(causal=False)

        with p.phase() as alloc:
            OALL = [alloc((128, S), BF16) for _ in range(CH)]
            WOB = [alloc((128, CH * 128), BF16) for _ in range(2)]
            XT = [alloc((128, S), F32) for _ in range(2)]
            XO = [alloc((128, S), F32) for _ in range(2)]
            OBS = alloc((128, CH), F32)
            load(OBS[:], cd["AOB"][:])
            for c in range(CH):
                load(OALL[c][:], o_sp[c * 128:(c + 1) * 128, :])
            sy.wait_ge(ld.h, ld.n)
            p.bar()
            SQF = [alloc((128, B), F32) for _ in range(2)]
            SROWX = alloc((1, S), F32)
            QROWX = alloc((1, S), F32)
            xp = Pool(XT, p, "xr")
            xop = Pool(XO, p, "xo")
            wcc = p.ctr("wcc")
            pp = Pool([ps[:, 0:B] for ps in PS1[0:3]])
            SBK = [PS2[0][:, 0:B], PS2[0][:, B:2 * B]]
            sg_read = [0, 0]
            sq_rd = [0, 0]
            pend_stat = []

            def do_fold_stats(mch, xo, item):
                b, nv = item
                blk = slice(b * B, (b + 1) * B)
                bpar = b % 2
                sc.wait_ge(vec.h, nv)
                sc.wait_ge(tec.h, sq_rd[bpar])
                sq2 = sec.inc(sc.activation(SQF[bpar][:], xo.ap[:, blk],
                                            AF.Square))
                te.wait_ge(sec.h, sq2)
                te.wait_ge(vec.h, sg_read[bpar])
                te.matmul(SBK[bpar][0:1, 0:B], ONE[:, 0:1], xo.ap[:, blk],
                          start=True, stop=True)
                mm2 = te.matmul(SBK[bpar][32:33, 0:B], ONE[:, 0:1],
                                SQF[bpar][:], start=True, stop=True)
                sq_rd[bpar] = tv2 = tec.inc(mm2)
                v.wait_ge(tec.h, tv2)
                if mch == 0:
                    v.tensor_copy(SROWX[:, blk], SBK[bpar][0:1, 0:B])
                    ins = v.tensor_copy(QROWX[:, blk], SBK[bpar][32:33, 0:B])
                else:
                    v.tensor_add(SROWX[:, blk], SROWX[:, blk],
                                 SBK[bpar][0:1, 0:B])
                    ins = v.tensor_add(QROWX[:, blk], QROWX[:, blk],
                                       SBK[bpar][32:33, 0:B])
                sg_read[bpar] = vec.inc(ins)
            tgt = out_d if is_last else x_hbm
            wob_guard = [0, 0]
            for mch in range(CH):
                par = mch % 2
                g.wait_ge(tec.h, wob_guard[par])
                for c in range(CH):
                    wcc.inc(g.dma_start(
                        WOB[par][:, c * 128:(c + 1) * 128],
                        cd["AOW"][c * 128:(c + 1) * 128, mch * 128:(mch + 1) * 128]), 16)
                nw = wcc.n
                xb = xp.get(sy)
                n_ld = xb.ctr.inc(sy.dma_start(
                    xb.ap[:], x_hbm[mch * 128:(mch + 1) * 128, :]), 16)
                xo = xop.get(v)
                for b in range(NB):
                    blk = slice(b * B, (b + 1) * B)
                    bank = pp.get(te)
                    te.wait_ge(wcc.h, nw)
                    for c in range(CH):
                        mm = te.matmul(bank.ap, WOB[par][:, c * 128:(c + 1) * 128],
                                       OALL[c][:, blk], start=(c == 0),
                                       stop=(c == CH - 1))
                    tv = tec.inc(mm)
                    v.wait_ge(tec.h, tv)
                    v.wait_ge(xb.ctr.h, n_ld)
                    ins = v.scalar_tensor_tensor(xo.ap[:, blk], bank.ap,
                                                 OBS[:, mch:mch + 1], xb.ap[:, blk],
                                                 ALU.add, ALU.add)
                    nv = vec.inc(ins)
                    bank.release(vec, nv)
                    pend_stat.append((b, nv))
                    if len(pend_stat) > 1:
                        do_fold_stats(mch, xo, pend_stat.pop(0))
                while pend_stat:
                    do_fold_stats(mch, xo, pend_stat.pop(0))
                wob_guard[par] = tec.n
                xb.release(vec, vec.n)
                g.wait_ge(vec.h, vec.n)
                n_st = xo.ctr.inc(g.dma_start(
                    tgt[mch * 128:(mch + 1) * 128, :], xo.ap[:]), 16)
                xo.release(xo.ctr, n_st)
            if not is_last:
                g.wait_ge(vec.h, vec.n)
                vrb.inc(g.dma_start(vrow_hbm[0:1, :], SROWX[:]), 16)
                nvr = vrb.inc(g.dma_start(vrow_hbm[1:2, :], QROWX[:]), 16)
                g.wait_ge(vrb.h, nvr)
            for b_ in xop.bufs:
                g.wait_ge(b_.ctr.h, b_.ctr.n)
            p.bar()


    def ffn_layer(is_last):
        with p.phase() as alloc:
            x_stats(x_hbm, alloc, rows_ready=True)
        v.tensor_mul(vt(VT_T1), vt(VT_RHO), vt(VT_RHO))
        v.tensor_mul(vt(VT_T1), vt(VT_T1), vt(VT_VAR))
        rsqrt_ops(vt(VT_T2), vt(VT_T1), 1e-6, vt(VT_T3), vt(VT_T4))
        v.tensor_mul(vt(VT_SC), vt(VT_RHO), vt(VT_T2))
        p.bar()

        with p.phase() as alloc_m, contextlib.ExitStack() as kv_stack:
            CETF = alloc_m((8, S), BF16)
            alloc_kv = kv_stack.enter_context(p.phase())
            KCB = [alloc_kv((128, S), BF16) for _ in range(2)]
            VCT = alloc_kv((128, 256 * NTC), BF16)

            with p.phase() as alloc:
                YH = [alloc((128, S), BF16) for _ in range(CH)]
                G1S = alloc((128, CH), F32)
                n1 = load(G1S[:], cd["FG1"][:])
                sy.wait_ge(ld.h, n1)
                with p.phase() as alloc_n:
                    build_norm_operand(YH, VT_SC, x_hbm, alloc_n, gcol=G1S[:])
                WQB = alloc((128, CH * 1024), BF16)
                WKB = alloc((128, CH * 256), BF16)
                WVB = alloc((128, CH * 256), BF16)
                QST = [alloc((128, S), BF16) for _ in range(2)]
                SQ = alloc((128, B), F32)
                CQR = alloc((1, S), F32)
                prep_weight_cast(WQB, cd["FWQ"], 0, 1024)
                prep_weight_cast(WKB, cd["FWK"], 0, 256)
                prep_weight_cast(WVB, cd["FWV"], 0, 256)
                g.wait_ge(st.h, st.n)
                p.bar()
                pp = Pool([ps[:, 0:B] for ps in PS1[0:3]])
                for mch in range(2):
                    for b in range(NB):
                        blk = slice(b * B, (b + 1) * B)
                        bank = pp.get(te)
                        for c in range(CH):
                            mm = te.matmul(bank.ap,
                                           WKB[:, c * 256 + 128 * mch:c * 256 + 128 * (mch + 1)],
                                           YH[c][:, blk], start=(c == 0), stop=(c == CH - 1))
                        tv = tec.inc(mm)
                        sc.wait_ge(tec.h, tv)
                        ins = sc.copy(KCB[mch][:, blk], bank.ap)
                        bank.release(sec, sec.inc(ins))
                p.bar()
                for b in range(NB):
                    blk = slice(b * B, (b + 1) * B)
                    te.wait_ge(vec.h, vec.n)
                    for mch in range(2):
                        sv = sec.inc(sc.activation(SQ[:], KCB[mch][:, blk], AF.Square))
                        te.wait_ge(sec.h, sv)
                        mm = te.matmul(PS1[3][0:1, 0:B], ONE[:, 0:1], SQ[:],
                                       start=(mch == 0), stop=(mch == 1))
                    tv = tec.inc(mm)
                    v.wait_ge(tec.h, tv)
                    vec.inc(v.tensor_copy(CQR[:, blk], PS1[3][0:1, 0:B]))
                p.bar()
                v.tensor_reduce(S11[0:1, 1:2], CQR[:], AX.X, ALU.max)
                nv = vec.inc(v.engine_nop())
                sc.wait_ge(vec.h, nv)
                ss = sec.inc(sc.activation(S11[0:1, 2:3], S11[0:1, 1:2], AF.Sqrt))
                v.wait_ge(sec.h, ss)
                v.tensor_scalar(S11[0:1, 3:4], S11[0:1, 2:3], -1.0, None, ALU.mult)
                p.bar()
                bcast_scalar(SVEC[:, 1:2], S11[0:1, 3:4])
                p.bar()
                qsp = Pool(QST, p, "vst")
                for h in range(4):
                    g.wait_ge(vec.h, vec.n)
                    g.wait_ge(st.h, st.n)
                    gm = pec.inc(g.memset(CQR[:], 0.0))
                    for half in range(2):
                        mch = 2 * h + half
                        qsb = qsp.get(sc)
                        for b in range(NB):
                            blk = slice(b * B, (b + 1) * B)
                            bank = pp.get(te)
                            for c in range(CH):
                                mm = te.matmul(bank.ap,
                                               WQB[:, c * 1024 + 128 * mch:c * 1024 + 128 * (mch + 1)],
                                               YH[c][:, blk], start=(c == 0), stop=(c == CH - 1))
                            tv = tec.inc(mm)
                            sc.wait_ge(tec.h, tv)
                            ins = sc.activation(qsb.ap[:, blk], bank.ap, AF.Identity,
                                                scale=0.0625)
                            bank.release(sec, sec.inc(ins))
                        sv = sec.n
                        g.wait_ge(sec.h, sv)
                        n_st = qsb.ctr.inc(g.dma_start(
                            q_sp[h, 128 * half:128 * (half + 1), :], qsb.ap[:]), 16)
                        qsb.release(qsb.ctr, n_st)
                        for b in range(NB):
                            blk = slice(b * B, (b + 1) * B)
                            s2 = sec.inc(sc.activation(SQ[:], qsb.ap[:, blk], AF.Square))
                            bank = pp.get(te)
                            te.wait_ge(sec.h, s2)
                            mm = te.matmul(bank.ap[0:1, 0:B], ONE[:, 0:1], SQ[:],
                                           start=True, stop=True)
                            tv = tec.inc(mm)
                            v.wait_ge(tec.h, tv)
                            v.wait_ge(pec.h, gm)
                            ins = v.tensor_add(CQR[:, blk], CQR[:, blk],
                                               bank.ap[0:1, 0:B])
                            bank.release(vec, vec.inc(ins))
                    sc.wait_ge(vec.h, vec.n)
                    ss = sec.inc(sc.activation(CQR[:], CQR[:], AF.Sqrt))
                    v.wait_ge(sec.h, ss)
                    ins = v.tensor_scalar(CQR[:], CQR[:], SVEC[0:1, 1:2], None, ALU.mult)
                    nv = vec.inc(ins)
                    g.wait_ge(vec.h, nv)
                    ncq = store(cq_hbm[h:h + 1, :], CQR[:])
                g.wait_ge(st.h, st.n)
                p.bar()
                for tc in range(NTC):
                    tsl = slice(tc * 128, (tc + 1) * 128)
                    for vb2 in range(1):
                        bank = pp.get(te)
                        for c in range(CH):
                            mm = te.matmul(bank.ap[:, 0:256], YH[c][:, tsl],
                                           WVB[:, c * 256:(c + 1) * 256],
                                           start=(c == 0), stop=(c == CH - 1))
                        tv = tec.inc(mm)
                        sc.wait_ge(tec.h, tv)
                        ins = sc.copy(VCT[:, 256 * tc:256 * (tc + 1)], bank.ap[:, 0:256])
                        bank.release(sec, sec.inc(ins))
                p.bar()
            if FFN_CUT < 2:
                return
            with p.phase() as alloc:
                QHB = [alloc((128, 2 * S), BF16) for _ in range(2)]
                CQB = [alloc((1, S), BF16) for _ in range(2)]
                PT = alloc((128, NTC * B), BF16)
                OH2 = [alloc((128, 2 * S), BF16) for _ in range(2)]
                RDNR = alloc((1, S), F32)
                RDNB2 = alloc((1, S), BF16)
                p.bar()
                qhp = Pool(QHB, p, "qnp")
                cqp = Pool(CQB, p, "knp")
                ohp = Pool(OH2, p, "ohp")
                SLOT = [PS2[0][:, 0:B], PS2[0][:, B:2 * B],
                        PS2[1][:, 0:B], PS2[1][:, B:2 * B]]
                A0 = PS1[0][:, 0:B]
                A1 = PS1[1][:, 0:B]
                DEN = PS1[2][0:1, 0:B]
                BCST = PS1[3][:, 0:B]
                exp_val = {}
                state = {"evac": 0, "bcst": 0}
                todo = []

                def mk_recip(qsl, ev, nxt):
                    def fn():
                        sc.wait_ge(vec.h, ev)
                        sc.activation(RDNR[0:1, qsl], RDNR[0:1, qsl], AF.Ln)
                        rd = sec.inc(sc.activation(RDNR[0:1, qsl], RDNR[0:1, qsl],
                                                   AF.Exp, scale=-1.0))
                        v.wait_ge(sec.h, rd)
                        rdnb = vec.inc(v.tensor_copy(RDNB2[0:1, qsl],
                                                     RDNR[0:1, qsl]))
                        nxt(rdnb)
                    return fn

                def mk_norm(pob, ph, qblk, gready):
                    def sched(rdnb):
                        def fn():
                            nql = slice(qblk * B, (qblk + 1) * B)
                            te.wait_ge(vec.h, max(rdnb, state["bcst"]))
                            mm2 = te.matmul(BCST, ONEB[0:1, :],
                                            RDNB2[0:1, nql], start=True, stop=True)
                            tv2 = tec.inc(mm2)
                            v.wait_ge(tec.h, tv2)
                            v.tensor_mul(pob.ap[:, nql], pob.ap[:, nql], BCST)
                            ins = v.tensor_mul(
                                pob.ap[:, S + qblk * B:S + (qblk + 1) * B],
                                pob.ap[:, S + qblk * B:S + (qblk + 1) * B], BCST)
                            state["bcst"] = vec.inc(ins)
                            if qblk == NB - 1:
                                todo.append((gready + 4, mk_store(pob, ph)))
                        todo.append((gready + 2, fn))
                    return sched

                def mk_store(pob, ph):
                    def fn():
                        g.wait_ge(vec.h, state["bcst"])
                        pob.ctr.inc(g.dma_start(o_sp[ph * 256:ph * 256 + 128, :],
                                                pob.ap[:, 0:S]), 16)
                        n_st = pob.ctr.inc(g.dma_start(
                            o_sp[ph * 256 + 128:ph * 256 + 256, :],
                            pob.ap[:, S:2 * S]), 16)
                        pob.release(pob.ctr, n_st)
                    return fn

                nsteps = NB * NTC
                for h in range(4):
                    qb_ = qhp.get(sy)
                    qb_.ctr.inc(sy.dma_start(qb_.ap[:, 0:S], q_sp[h, 0:128, :]), 16)
                    nq = qb_.ctr.inc(sy.dma_start(qb_.ap[:, S:2 * S],
                                                  q_sp[h, 128:256, :]), 16)
                    cqb = cqp.get(sy)
                    ncq = cqb.ctr.inc(sy.dma_start(cqb.ap[:], cq_hbm[h:h + 1, :]), 16)
                    te.wait_ge(qb_.ctr.h, nq)
                    te.wait_ge(cqb.ctr.h, ncq)
                    ob = ohp.get(v)

                    def emit_av(step):
                        qblk, tc = divmod(step, NTC)
                        qsl = slice(qblk * B, (qblk + 1) * B)
                        te.wait_ge(sec.h, exp_val[h * nsteps + step])
                        if tc == 0 and state["evac"]:
                            te.wait_ge(vec.h, state["evac"])
                        st_, sp_ = (tc == 0), (tc == NTC - 1)
                        ptt = PT[:, tc * B:(tc + 1) * B]
                        te.matmul(A0, VCT[:, 256 * tc:256 * tc + 128], ptt,
                                  start=st_, stop=sp_)
                        te.matmul(A1, VCT[:, 256 * tc + 128:256 * tc + 256], ptt,
                                  start=st_, stop=sp_)
                        mm = te.matmul(DEN, ONEB[:, 0:1], ptt, start=st_, stop=sp_)
                        if sp_:
                            tv = tec.inc(mm)
                            v.wait_ge(tec.h, tv)
                            v.tensor_copy(ob.ap[:, qsl], A0)
                            v.tensor_copy(ob.ap[:, S + qblk * B:S + (qblk + 1) * B],
                                          A1)
                            ins = v.tensor_copy(RDNR[0:1, qsl], DEN)
                            state["evac"] = vec.inc(ins)
                            gnow = h * nsteps + step + 2
                            todo.append((gnow + 2, mk_recip(
                                qsl, state["evac"], mk_norm(ob, h, qblk, gnow + 2))))

                    for step in range(nsteps):
                        qblk, tc = divmod(step, NTC)
                        gstep = h * nsteps + step
                        slot = SLOT[step % 4]
                        if gstep >= 4:
                            te.wait_ge(sec.h, exp_val[gstep - 4])
                        qsl = slice(qblk * B, (qblk + 1) * B)
                        te.matmul(slot, KCB[0][:, tc * 128:(tc + 1) * 128],
                                  qb_.ap[:, qsl], start=True, stop=False)
                        te.matmul(slot, KCB[1][:, tc * 128:(tc + 1) * 128],
                                  qb_.ap[:, S + qblk * B:S + (qblk + 1) * B],
                                  start=False, stop=False)
                        mm = te.matmul(slot, ONEB[0:1, :], cqb.ap[:, qsl],
                                       start=False, stop=True)
                        tv = tec.inc(mm)
                        if step % 2 == 1:
                            sc.wait_ge(tec.h, tv)
                            ins = sc.activation(PT[:, (tc - 1) * B:(tc + 1) * B],
                                                PS2[(step // 2) % 2][:, 0:2 * B],
                                                AF.Exp)
                            sv = sec.inc(ins)
                            exp_val[gstep] = exp_val[gstep - 1] = sv
                        if step >= 2:
                            emit_av(step - 2)
                        while todo and todo[0][0] <= gstep:
                            todo.pop(0)[1]()
                    emit_av(nsteps - 2)
                    emit_av(nsteps - 1)
                    qb_.release(tec, tec.n)
                    cqb.release(tec, tec.n)
                while todo:
                    todo.pop(0)[1]()
                for b_ in ohp.bufs:
                    g.wait_ge(b_.ctr.h, b_.ctr.n)
                p.bar()

            if FFN_CUT < 3:
                return
            kv_stack.close()
            with p.phase() as alloc:
                OALL = [alloc((128, S), BF16) for _ in range(CH)]
                WOB = [alloc((128, CH * 128), BF16) for _ in range(2)]
                MUB = alloc((128, S), F32)
                SCB = alloc((128, S), F32)
                LGT = alloc((8, S), F32)
                SQROW = alloc((1, S), F32)
                g.memset(LGT[:], 0.0)
                g.memset(SQROW[:], 0.0)
                WRGF = alloc((128, CH * 8), F32)
                SQV = alloc((128, B), F32)
                SQS = [alloc((128, B), F32) for _ in range(2)]
                FG2S = alloc((128, CH), F32)
                for c in range(CH):
                    load(OALL[c][:], o_sp[c * 128:(c + 1) * 128, :])
                load(FG2S[:], cd["FG2"][:])
                load(WRGF[:].rearrange("p (c e) -> p c e", e=8),
                     cd["FWR"][:].rearrange("(c pp) e -> pp c e", pp=128))
                with p.phase() as alloc_r:
                    MR = alloc_r((1, S), F32)
                    RR = alloc_r((1, S), F32)
                    vt_to_rows2([(MR[:], VT_MU, 0), (RR[:], VT_RHO, 1)])
                    sy.wait_ge(ld.h, ld.n)
                    p.bar()
                    for c in range(CH):
                        v.tensor_scalar(WRGF[:, c * 8:(c + 1) * 8],
                                        WRGF[:, c * 8:(c + 1) * 8],
                                        FG2S[:, c:c + 1], None, ALU.mult)
                    bp = Pool([PS2[0][:, 0:B], PS2[0][:, B:2 * B],
                               PS2[1][:, 0:B], PS2[1][:, B:2 * B]])
                    for b in range(NB):
                        blk = slice(b * B, (b + 1) * B)
                        bm = bp.get(te)
                        bs = bp.get(te)
                        te.matmul(bm.ap, ONE[0:1, :], MR[:, blk],
                                  start=True, stop=True)
                        mm = te.matmul(bs.ap, ONE[0:1, :], RR[:, blk],
                                       start=True, stop=True)
                        tv = tec.inc(mm)
                        sc.wait_ge(tec.h, tv)
                        sc.copy(MUB[:, blk], bm.ap)
                        ss = sec.inc(sc.copy(SCB[:, blk], bs.ap))
                        bm.release(sec, ss)
                        bs.release(sec, ss)
                    p.bar()
                XT = [alloc((128, S), F32)]
                XO = [alloc((128, S), F32)]
                xp = Pool(XT, p, "xr")
                xop = Pool(XO, p, "xo")
                wcc = p.ctr("wcc")
                pp = Pool([PS2[0][:, 0:B], PS2[0][:, B:2 * B],
                           PS2[1][:, 0:B], PS2[1][:, B:2 * B]])
                STAT = [(PS1[0][0:1, 0:B], PS1[1][0:8, 0:B]),
                        (PS1[2][0:1, 0:B], PS1[3][0:8, 0:B])]
                stat_guard = [0, 0]
                wob_guard = [0, 0]
                sq_done = [0, 0]
                sq_read = [0, 0]
                for mch in range(CH):
                    par = mch % 2
                    g.wait_ge(tec.h, wob_guard[par])
                    for c in range(CH):
                        wcc.inc(g.dma_start(
                            WOB[par][:, c * 128:(c + 1) * 128],
                            cd["FWO"][c * 128:(c + 1) * 128, mch * 128:(mch + 1) * 128]), 16)
                    nw = wcc.n
                    xb = xp.get(sy)
                    n_ld = xb.ctr.inc(sy.dma_start(
                        xb.ap[:], x_hbm[mch * 128:(mch + 1) * 128, :]), 16)
                    xo = xop.get(v)

                    def do_stats(pb):
                        sr, lg = STAT[pb % 2]
                        blkp = slice(pb * B, (pb + 1) * B)
                        te.wait_ge(sec.h, sq_done[pb % 2])
                        te.wait_ge(vec.h, stat_guard[pb % 2])
                        te.matmul(sr, ONE[:, 0:1], SQS[pb % 2][:],
                                  start=True, stop=True)
                        mm3 = te.matmul(lg, WRGF[:, mch * 8:(mch + 1) * 8],
                                        xo.ap[:, blkp], start=True, stop=True)
                        tv3 = tec.inc(mm3)
                        sq_read[pb % 2] = tv3
                        v.wait_ge(tec.h, tv3)
                        v.tensor_add(SQROW[:, blkp], SQROW[:, blkp], sr)
                        ins = v.tensor_add(LGT[:, blkp], LGT[:, blkp], lg)
                        stat_guard[pb % 2] = vec.inc(ins)

                    for b in range(NB):
                        blk = slice(b * B, (b + 1) * B)
                        bank = pp.get(te)
                        te.wait_ge(wcc.h, nw)
                        for c in range(CH):
                            mm = te.matmul(bank.ap, WOB[par][:, c * 128:(c + 1) * 128],
                                           OALL[c][:, blk], start=(c == 0),
                                           stop=(c == CH - 1))
                        tv = tec.inc(mm)
                        v.wait_ge(tec.h, tv)
                        v.wait_ge(xb.ctr.h, n_ld)
                        v.tensor_sub(SQV[:], xb.ap[:, blk], MUB[:, blk])
                        v.tensor_mul(SQV[:], SQV[:], SCB[:, blk])
                        ins = v.tensor_add(xo.ap[:, blk], SQV[:], bank.ap)
                        nv = vec.inc(ins)
                        bank.release(vec, nv)
                        sc.wait_ge(vec.h, nv)
                        sc.wait_ge(tec.h, sq_read[b % 2])
                        sq_done[b % 2] = sec.inc(
                            sc.activation(SQS[b % 2][:], xo.ap[:, blk], AF.Square))
                        if b >= 1:
                            do_stats(b - 1)
                    do_stats(NB - 1)
                    wob_guard[par] = tec.n
                    xb.release(vec, vec.n)
                    g.wait_ge(vec.h, vec.n)
                    n_st = xo.ctr.inc(g.dma_start(
                        xmid_hbm[mch * 128:(mch + 1) * 128, :], xo.ap[:]), 16)
                    xo.release(xo.ctr, n_st)
                for b_ in xop.bufs:
                    g.wait_ge(b_.ctr.h, b_.ctr.n)
                p.bar()
                g.wait_ge(vec.h, vec.n)
                nlg = vrb.inc(g.dma_start(vrow_hbm[4:12, :], LGT[:]), 16)
                row_to_vt(VT_T1, SQROW[:], lane=0)
                sy.wait_ge(ld.h, ld.n)
                sy.wait_ge(vrb.h, nlg)
                p.bar()
            v.tensor_scalar(vt(VT_T1), vt(VT_T1), 1.0 / D, None, ALU.mult)
            rsqrt_ops(vt(VT_X1), vt(VT_T1), 1e-6, vt(VT_T2), vt(VT_T3))
            p.bar()
            if FFN_CUT < 4:
                return
            with p.phase() as alloc:
                R2R = alloc((1, S), F32)
                LGT = alloc((8, S), F32)
                load(LGT[:], vrow_hbm[4:12, :])
                XT = [alloc((128, S), F32) for _ in range(2)]
                ZST = [alloc((128, S), BF16) for _ in range(2)]
                SQ3 = alloc((128, B), F32)
                RT = [alloc((128, 8), F32) for _ in range(3)]
                RW = [alloc((128, 8), F32) for _ in range(3)]
                RW2 = [alloc((128, 8), F32) for _ in range(3)]
                MM1 = [alloc((128, 4), F32) for _ in range(3)]
                CMB = [alloc((128, 8), F32) for _ in range(3)]
                FG2Z = alloc((128, CH), F32)
                R2B = alloc((128, S), F32)
                FEBS = alloc((8, 1), F32)
                vt_to_row(R2R[:], VT_X1, lane=2)
                nz1 = load(FG2Z[:], cd["FG2"][:])
                load(FEBS[:], cd["FEB"][:])
                sy.wait_ge(ld.h, ld.n)
                p.bar()
                bp2 = Pool([ps[:, 0:B] for ps in PS1])
                for b in range(NB):
                    blk = slice(b * B, (b + 1) * B)
                    bank = bp2.get(te)
                    mm = te.matmul(bank.ap, ONE[0:1, :], R2R[:, blk],
                                   start=True, stop=True)
                    tv = tec.inc(mm)
                    sc.wait_ge(tec.h, tv)
                    ss = sec.inc(sc.copy(R2B[:, blk], bank.ap))
                    bank.release(sec, ss)
                r2b_done = sec.n
                te.wait_ge(sec.h, r2b_done)
                v.wait_ge(sec.h, r2b_done)
                lgt_done = {}
                for b in range(NB):
                    blk = slice(b * B, (b + 1) * B)
                    bank = PS1[2] if b % 2 == 0 else PS1[3]
                    if b >= 2:
                        te.wait_ge(vec.h, lgt_done[b - 2])
                    mm = te.matmul(bank[0:8, 0:B], ONE[0:1, 0:8], R2R[:, blk],
                                   start=True, stop=True)
                    tv = tec.inc(mm)
                    v.wait_ge(tec.h, tv)
                    v.tensor_mul(LGT[:, blk], LGT[:, blk], bank[0:8, 0:B])
                    ins = v.tensor_scalar(LGT[:, blk], LGT[:, blk], FEBS[:, 0:1],
                                          None, ALU.add)
                    lgt_done[b] = vec.inc(ins)
                xp = Pool(XT, p, "xr")
                zp = Pool(ZST, p, "xo")
                zf_items = []
                for c in range(CH):
                    for b in range(NB):
                        zf_items.append((c, b))
                zf_state = {"xb": None, "zb": None, "n_ld": 0}

                def zf_step():
                    if not zf_items:
                        return
                    c, b = zf_items.pop(0)
                    if b == 0:
                        zf_state["xb"] = xp.get(sy)
                        zf_state["n_ld"] = zf_state["xb"].ctr.inc(sy.dma_start(
                            zf_state["xb"].ap[:],
                            xmid_hbm[c * 128:(c + 1) * 128, :]), 16)
                        zf_state["zb"] = zp.get(v)
                    blk = slice(b * B, (b + 1) * B)
                    v.wait_ge(zf_state["xb"].ctr.h, zf_state["n_ld"])
                    v.tensor_mul(SQ3[:], zf_state["xb"].ap[:, blk], R2B[:, blk])
                    ins = v.tensor_scalar(zf_state["zb"].ap[:, blk], SQ3[:],
                                          FG2Z[:, c:c + 1], None, ALU.mult)
                    nv = vec.inc(ins)
                    if b == NB - 1:
                        zf_state["xb"].release(vec, nv)
                        g.wait_ge(vec.h, nv)
                        n_st = zf_state["zb"].ctr.inc(g.dma_start(
                            zf_hbm[c * 128:(c + 1) * 128, :],
                            zf_state["zb"].ap[:]), 16)
                        zf_state["zb"].release(zf_state["zb"].ctr, n_st)

                rt_copy = {}
                SLOTT = [PS1[0], PS1[1], PS2[0][:, 0:B]]
                for tc in range(S // 128):
                    tsl = slice(tc * 128, (tc + 1) * 128)
                    i2 = tc % 3
                    slot = SLOTT[i2]
                    if tc >= 3:
                        te.wait_ge(sec.h, rt_copy[tc - 3])
                    te.wait_ge(vec.h, lgt_done[tc // 4])
                    mm = te.transpose(slot[0:128, 0:8], LGT[0:8, tsl],
                                      IDE[0:8, 0:8])
                    tv = tec.inc(mm)
                    sc.wait_ge(tec.h, tv)
                    rt_copy[tc] = sec.inc(sc.copy(RT[i2][:], slot[0:128, 0:8]))
                    ss = sec.inc(sc.activation(RW[i2][:], RT[i2][:], AF.Exp))
                    v.wait_ge(sec.h, ss)
                    v.tensor_reduce(MM1[i2][:, 1:2], RW[i2][:], AX.X, ALU.add)
                    v.reciprocal(MM1[i2][:, 1:2], MM1[i2][:, 1:2])
                    v.tensor_scalar(RW[i2][:], RW[i2][:], MM1[i2][:, 1:2],
                                    None, ALU.mult)
                    v.tensor_reduce(MM1[i2][:, 2:3], RW[i2][:], AX.X, ALU.max)
                    v.tensor_scalar(CMB[i2][:], RW[i2][:], 0.0, MM1[i2][:, 2:3],
                                    ALU.mult, ALU.add)
                    v.match_replace(RW2[i2][:], CMB[i2][:], RW[i2][:], -1e30)
                    v.tensor_reduce(MM1[i2][:, 3:4], RW2[i2][:], AX.X, ALU.max)
                    nv = vec.inc(v.tensor_sub(MM1[i2][:, 0:1], MM1[i2][:, 3:4],
                                              MM1[i2][:, 2:3]))
                    sc.wait_ge(vec.h, nv)
                    ss = sec.inc(sc.activation(MM1[i2][:, 0:1], MM1[i2][:, 0:1],
                                               AF.Sigmoid))
                    zf_step()
                    v.wait_ge(sec.h, ss)
                    v.tensor_scalar(MM1[i2][:, 1:2], MM1[i2][:, 0:1], -1.0, 1.0,
                                    ALU.mult, ALU.add)
                    v.tensor_scalar(CMB[i2][:], RW[i2][:], MM1[i2][:, 2:3],
                                    None, ALU.is_equal)
                    v.tensor_scalar(CMB[i2][:], CMB[i2][:], MM1[i2][:, 1:2],
                                    None, ALU.mult)
                    v.tensor_scalar(RW2[i2][:], RW2[i2][:], MM1[i2][:, 3:4],
                                    None, ALU.is_equal)
                    v.tensor_scalar(RW2[i2][:], RW2[i2][:], MM1[i2][:, 0:1],
                                    None, ALU.mult)
                    nv = vec.inc(v.tensor_add(CMB[i2][:], CMB[i2][:], RW2[i2][:]))
                    te.wait_ge(vec.h, nv)
                    mm = te.transpose(slot[0:8, 8:136], CMB[i2][:], IDE[:])
                    tv = tec.inc(mm)
                    sc.wait_ge(tec.h, tv)
                    rt_copy[tc] = sec.inc(sc.copy(CETF[0:8, tsl],
                                                  slot[0:8, 8:136]))
                    zf_step()
                while zf_items:
                    zf_step()
                sc.wait_ge(sec.h, sec.n)
                for b_ in zp.bufs:
                    g.wait_ge(b_.ctr.h, b_.ctr.n)
                p.bar()

            if FFN_CUT < 5:
                return
            if True:
                SH = S // 2
                BH = min(B, SH)
                NBH = SH // BH
                with p.phase() as alloc2:
                    ZF = [alloc2((128, SH), BF16) for _ in range(CH)]
                    SPEC = [alloc2((128, SH), F32) for _ in range(CH)]
                    EW1B = alloc2((128, CH * 1024), BF16)
                    EW3B = alloc2((128, CH * 1024), BF16)
                    EW2B = alloc2((128, CH * 1024), BF16)
                    HBK = alloc2((128, 2 * CH * BH), BF16)
                    SA = alloc2((128, 2 * BH), F32)
                    TB = alloc2((128, 2 * BH), F32)
                    CEBS = alloc2((128, NBH * BH), F32)
                    p.bar()
                    zlc = p.ctr("zlc")
                    wcc = p.ctr("wcc2")

                    def ew_src(e, which):
                        if e == 0:
                            return cd[{1: "SW1", 3: "SW3", 2: "SW2"}[which]]
                        return cd[{1: "EW1", 3: "EW3", 2: "EW2"}[which]][e - 1]

                    def prefetch_w13(e, w_done):
                        g.wait_ge(tec.h, w_done)
                        for c in range(CH):
                            wcc.inc(g.dma_start(
                                EW1B[:, c * 1024:(c + 1) * 1024],
                                ew_src(e, 1)[c * 128:(c + 1) * 128, :]), 16)
                        for c in range(CH):
                            wcc.inc(g.dma_start(
                                EW3B[:, c * 1024:(c + 1) * 1024],
                                ew_src(e, 3)[c * 128:(c + 1) * 128, :]), 16)
                        return wcc.n

                    def prefetch_w2(e, s_done):
                        g.wait_ge(tec.h, s_done)
                        for c in range(CH):
                            wcc.inc(g.dma_start(
                                EW2B[:, c * 1024:(c + 1) * 1024],
                                ew_src(e, 2)[c * 128:(c + 1) * 128, :]), 16)
                        return wcc.n

                    XT2 = [alloc2((128, BH), F32) for _ in range(2)]
                    XM2 = [alloc2((128, BH), F32) for _ in range(2)]
                    XN = [alloc2((128, BH), F32) for _ in range(2)]
                    xp2 = Pool(XT2, p, "xr")
                    xm2 = Pool(XM2, p, "xm2")
                    xn2 = Pool(XN, p, "xo")
                    silu_done = [0, 0]
                    tb_done = [0, 0]
                    h_done = [0, 0]
                    spec_tec = [0, 0]
                    ceb_read = 0
                    specp = Pool([PS1[0][:, 0:BH], PS1[1][:, 0:BH]])
                    for c in range(CH):
                        zlc.inc(sy.dma_start(
                            ZF[c][:], zf_hbm[c * 128:(c + 1) * 128, 0:SH]), 16)
                    nz = zlc.n
                    te.wait_ge(zlc.h, nz)
                    w13_ready = prefetch_w13(0, tec.n)
                    w2_cur = prefetch_w2(0, tec.n)
                    for tg in range(2):
                        toff = tg * SH
                        te.wait_ge(zlc.h, nz)
                        order = list(range(9)) if tg == 0 else list(range(8, -1, -1))
                        for ei, e in enumerate(order):
                            if e > 0:
                                te.wait_ge(vec.h, ceb_read)
                                cps = []
                                for b in range(NBH):
                                    bnk = PS1[2] if b % 2 == 0 else PS1[3]
                                    if b >= 2:
                                        te.wait_ge(sec.h, cps[b - 2])
                                    mm = te.matmul(bnk[:, 0:BH],
                                                   IND8B[:, (e - 1) * 128:e * 128],
                                                   CETF[:, toff + b * BH:toff + (b + 1) * BH],
                                                   start=True, stop=True)
                                    tv = tec.inc(mm)
                                    sc.wait_ge(tec.h, tv)
                                    ins = sc.copy(CEBS[:, b * BH:(b + 1) * BH],
                                                  bnk[:, 0:BH])
                                    cps.append(sec.inc(ins))
                            ceb_ready = sec.n

                            def emit_spec(b):
                                blk = slice(b * BH, (b + 1) * BH)
                                hoff = (b % 2) * CH * BH
                                te.wait_ge(wcc.h, w2_cur)
                                for mch in range(CH):
                                    bank = specp.get(te)
                                    te.wait_ge(pec.h, h_done_blk[b])
                                    for fc in range(CH):
                                        mm = te.matmul(
                                            bank.ap,
                                            EW2B[:, fc * 1024 + 128 * mch:fc * 1024 + 128 * (mch + 1)],
                                            HBK[:, hoff + fc * BH:hoff + (fc + 1) * BH],
                                            start=(fc == 0), stop=(fc == CH - 1))
                                    tv = tec.inc(mm)
                                    v.wait_ge(tec.h, tv)
                                    if ei == 0:
                                        ins = v.tensor_copy(SPEC[mch][:, blk],
                                                            bank.ap)
                                    else:
                                        ins = v.tensor_add(SPEC[mch][:, blk],
                                                           SPEC[mch][:, blk],
                                                           bank.ap)
                                    bank.release(vec, vec.inc(ins))
                                spec_tec[b % 2] = tec.n

                            h_done_blk = [0] * NBH
                            for b in range(NBH):
                                blk = slice(b * BH, (b + 1) * BH)
                                hoff = (b % 2) * CH * BH
                                for fc in range(CH):
                                    par = fc % 2
                                    psl = slice(par * BH, (par + 1) * BH)
                                    bankt = PS2[par]
                                    te.wait_ge(sec.h, silu_done[par])
                                    te.wait_ge(vec.h, tb_done[par])
                                    if fc == 0 and b == 0:
                                        te.wait_ge(wcc.h, w13_ready)
                                    for c in range(CH):
                                        te.matmul(bankt[:, 0:BH],
                                                  EW1B[:, c * 1024 + 128 * fc:c * 1024 + 128 * (fc + 1)],
                                                  ZF[c][:, blk], start=(c == 0),
                                                  stop=(c == CH - 1))
                                    for c in range(CH):
                                        mm = te.matmul(bankt[:, BH:2 * BH],
                                                       EW3B[:, c * 1024 + 128 * fc:c * 1024 + 128 * (fc + 1)],
                                                       ZF[c][:, blk], start=(c == 0),
                                                       stop=(c == CH - 1))
                                    tv = tec.inc(mm)
                                    sc.wait_ge(tec.h, tv)
                                    sc.wait_ge(pec.h, h_done[par])
                                    ss = sec.inc(sc.activation(SA[:, psl],
                                                               bankt[:, 0:BH], AF.Silu))
                                    silu_done[par] = ss
                                    v.wait_ge(tec.h, tv)
                                    v.wait_ge(pec.h, h_done[par])
                                    if BH < 512:
                                        v.wait_ge(sec.h, silu_done[par])
                                    if e > 0:
                                        v.wait_ge(sec.h, ceb_ready)
                                        ins = v.tensor_mul(TB[:, psl], bankt[:, BH:2 * BH],
                                                           CEBS[:, b * BH:(b + 1) * BH])
                                    else:
                                        ins = v.tensor_copy(TB[:, psl], bankt[:, BH:2 * BH])
                                    tb_done[par] = vec.inc(ins)
                                    g.wait_ge(sec.h, silu_done[par])
                                    g.wait_ge(vec.h, tb_done[par])
                                    g.wait_ge(tec.h, spec_tec[b % 2])
                                    ins = g.tensor_mul(HBK[:, hoff + fc * BH:hoff + (fc + 1) * BH],
                                                       SA[:, psl], TB[:, psl])
                                    h_done[par] = pec.inc(ins)
                                h_done_blk[b] = pec.n
                                if b > 0:
                                    emit_spec(b - 1)
                            w_done = tec.n
                            emit_spec(NBH - 1)
                            s_done = tec.n
                            ceb_read = vec.n
                            if ei < 8:
                                w13_ready = prefetch_w13(order[ei + 1], w_done)
                                w2_cur = prefetch_w2(order[ei + 1], s_done)
                        if tg == 0:
                            g.wait_ge(tec.h, w_done)
                            for c in range(CH):
                                zlc.inc(g.dma_start(
                                    ZF[c][:], zf_hbm[c * 128:(c + 1) * 128,
                                                     SH:2 * SH]), 16)
                            nz = zlc.n
                        tgt = out_d if is_last else x_hbm
                        for c in range(CH):
                            for b in range(NBH):
                                csl = slice(c * 128, (c + 1) * 128)
                                hsl = slice(toff + b * BH, toff + (b + 1) * BH)
                                xb = xp2.get(sy)
                                n1 = xb.ctr.inc(sy.dma_start(
                                    xb.ap[:], x_hbm[csl, hsl]), 16)
                                xmb = xm2.get(sy)
                                n2 = xmb.ctr.inc(sy.dma_start(
                                    xmb.ap[:], xmid_hbm[csl, hsl]), 16)
                                xnb = xn2.get(v)
                                v.wait_ge(xb.ctr.h, n1)
                                v.wait_ge(xmb.ctr.h, n2)
                                v.tensor_add(xnb.ap[:], xb.ap[:], xmb.ap[:])
                                ins = v.tensor_add(
                                    xnb.ap[:], xnb.ap[:],
                                    SPEC[c][:, b * BH:(b + 1) * BH])
                                nv = vec.inc(ins)
                                xb.release(vec, nv)
                                xmb.release(vec, nv)
                                g.wait_ge(vec.h, nv)
                                n_st = xnb.ctr.inc(g.dma_start(
                                    tgt[csl, hsl], xnb.ap[:]), 16)
                                xnb.release(xnb.ctr, n_st)
                    for b_ in xn2.bufs:
                        g.wait_ge(b_.ctr.h, b_.ctr.n)
                    p.bar()


    def attn_b(is_last):
        with p.phase() as alloc:
            x_stats(x_hbm, alloc)
        p.bar()

        with p.phase() as alloc_o:
            YH = [alloc_o((128, S), BF16) for _ in range(CH)]
            with p.phase() as alloc_n:
                build_norm_operand(YH, VT_RHO, x_hbm, alloc_n)

            g.memset(BIAS[0:4, 16:20], 0.0)
            p.bar()

            def b_groups(is_q):
                col0 = 0 if is_q else 1024
                spill = qn_sp if is_q else kn_sp
                qs = 0.125 if is_q else 1.0
                with p.phase() as alloc:
                    TCOS = alloc((128, S), BF16)
                    TSIN = alloc((128, S), BF16)
                    RA = alloc((128, S), F32)
                    RB = alloc((128, S), F32)
                    STG = [alloc((128, 2 * B), F32) for _ in range(2)]
                    SQP = [alloc((128, B), F32) for _ in range(2)]
                    SQ2P = [alloc((128, B), F32) for _ in range(2)]
                    WQG = [alloc((128, 2048), BF16) for _ in range(2)]
                    QNE = alloc((128, S), BF16)
                    QNO = alloc((128, S), BF16)
                    SB = alloc((4, 4 * B), F32)

                    def sbt(i, par):
                        return SB[:, (2 * par + i) * B:(2 * par + i + 1) * B]

                    store(TCOS[:], cd["COSB"][:])
                    store(TSIN[:], cd["SINB"][:])
                    g.wait_ge(st.h, st.n)
                    p.bar()
                    if is_q:
                        v.tensor_scalar(TCOS[:], TCOS[:], qs, None, ALU.mult)
                        v.tensor_scalar(TSIN[:], TSIN[:], qs, None, ALU.mult)
                        p.bar()
                    PJ = [PS2[1][:, 0:B], PS2[1][:, B:2 * B], PS2[2][:, 0:B]]
                    P0 = [PS2[0][0:4, 0:B], PS2[0][0:4, B:2 * B]]
                    pjp = Pool(PJ)
                    gd = {"spill_st": 0, "cq_st": [0, 0], "wq": [0, 0],
                          "rot": [0] * NB, "rotp": [0] * NB,
                          "sq_read": [0, 0], "p0q": [(None, 0), (None, 0)]}

                    def wq_chunk(gi, c):
                        par = gi % 2
                        if c == 0:
                            g.wait_ge(tec.h, gd["wq"][par])
                        store(WQG[par][:, c * 256:(c + 1) * 256],
                              cd["BQKV"][c * 128:(c + 1) * 128,
                                         col0 + 256 * gi:col0 + 256 * gi + 256])
                        return st.n

                    def proj_block(gi, b):
                        par = gi % 2
                        blk = slice(b * B, (b + 1) * B)
                        for half, dst in ((0, RA), (1, RB)):
                            bank = pjp.get(te)
                            if b == 0 and half == 0:
                                te.wait_ge(st.h, gd["wq_rdy"])
                            for c in range(CH):
                                mm = te.matmul(
                                    bank.ap,
                                    WQG[par][:, c * 256 + 128 * half:
                                              c * 256 + 128 * (half + 1)],
                                    YH[c][:, blk], start=(c == 0),
                                    stop=(c == CH - 1))
                            tv = tec.inc(mm)
                            sc.wait_ge(tec.h, tv)
                            if half == 0:
                                sc.wait_ge(vec.h, gd["rot"][b])
                                sc.wait_ge(pec.h, gd["rotp"][b])
                            ins = sc.copy(dst[:, blk], bank.ap)
                            bank.release(sec, sec.inc(ins))
                        gd["wq"][par] = tec.n
                        return sec.n

                    def rope_block(b, copy_done):
                        par = b % 2
                        blk = slice(b * B, (b + 1) * B)
                        v.wait_ge(sec.h, copy_done)
                        v.wait_ge(tec.h, gd["sq_read"][par])
                        v.wait_ge(st.h, gd["spill_st"])
                        v.tensor_mul(SQP[par][:], RA[:, blk], TCOS[:, blk])
                        v.tensor_mul(SQ2P[par][:], RB[:, blk], TSIN[:, blk])
                        g.wait_ge(sec.h, copy_done)
                        g.wait_ge(st.h, gd["spill_st"])
                        g.tensor_mul(STG[par][:, 0:B], RA[:, blk], TSIN[:, blk])
                        ins = g.tensor_mul(STG[par][:, B:2 * B], RB[:, blk],
                                           TCOS[:, blk])
                        gp = pec.inc(ins)
                        ins = v.tensor_sub(QNE[:, blk], SQP[par][:], SQ2P[par][:])
                        gd["rot"][b] = vec.inc(ins)
                        ins = g.tensor_add(QNO[:, blk], STG[par][:, 0:B],
                                           STG[par][:, B:2 * B])
                        gd["rotp"][b] = pec.inc(ins)

                    def cstat_block(gi, b):
                        par = b % 2
                        blk = slice(b * B, (b + 1) * B)
                        sc.wait_ge(vec.h, gd["rot"][b])
                        sc.wait_ge(pec.h, gd["rotp"][b])
                        sc.wait_ge(tec.h, gd["sq_read"][par])
                        sc.activation(SQP[par][:], QNE[:, blk], AF.Square)
                        s2 = sec.inc(sc.activation(SQ2P[par][:], QNO[:, blk],
                                                   AF.Square))
                        peng, pval = gd["p0q"][par]
                        if peng is not None:
                            te.wait_ge(peng.h, pval)
                        te.wait_ge(sec.h, s2)
                        te.matmul(P0[par], IND4[:], SQP[par][:],
                                  start=True, stop=False)
                        mm = te.matmul(P0[par], IND4[:], SQ2P[par][:],
                                       start=False, stop=True)
                        tv = tec.inc(mm)
                        gd["sq_read"][par] = tv
                        if is_q:
                            sc.wait_ge(tec.h, tv)
                            ss = sec.inc(sc.activation(sbt(0, par), P0[par],
                                                       AF.Sqrt))
                            gd["p0q"][par] = (sec, ss)
                            v.wait_ge(sec.h, ss)
                            v.wait_ge(st.h, gd["cq_st"][par])
                            ins = v.tensor_scalar(sbt(1, par), sbt(0, par),
                                                  SVEC[0:4, 4 + gi:5 + gi],
                                                  None, ALU.mult)
                            nv = vec.inc(ins)
                            g.wait_ge(vec.h, nv)
                            for h4 in range(4):
                                store(cq_hbm[4 * gi + h4:4 * gi + h4 + 1, blk],
                                      SB[h4:h4 + 1, (2 * par + 1) * B:
                                         (2 * par + 2) * B])
                            gd["cq_st"][par] = st.n
                        else:
                            v.wait_ge(tec.h, tv)
                            v.tensor_reduce(sbt(0, par)[:, 0:1], P0[par],
                                            AX.X, ALU.max)
                            ins = v.tensor_max(BIAS[0:4, 16 + gi:17 + gi],
                                               BIAS[0:4, 16 + gi:17 + gi],
                                               sbt(0, par)[:, 0:1])
                            gd["p0q"][par] = (vec, vec.inc(ins))

                    for c in range(CH):
                        gd["wq_rdy"] = wq_chunk(0, c)
                    for gi in range(NG):
                        for b in range(NB):
                            copy_done = proj_block(gi, b)
                            rope_block(b, copy_done)
                            if gi + 1 < NG:
                                gd["wq_rdy"] = wq_chunk(gi + 1, b)
                            if b >= 1:
                                cstat_block(gi, b - 1)
                        cstat_block(gi, NB - 1)
                        g.wait_ge(vec.h, gd["rot"][NB - 1])
                        g.wait_ge(pec.h, gd["rotp"][NB - 1])
                        for h4 in range(4):
                            rs = slice(32 * h4, 32 * (h4 + 1))
                            store(spill[4 * gi + h4, 0:32, :], QNE[rs, :])
                            store(spill[4 * gi + h4, 32:64, :], QNO[rs, :])
                        gd["spill_st"] = st.n
                    g.wait_ge(st.h, st.n)
                    p.bar()
                    if not is_q:
                        ss = sec.inc(sc.activation(BIAS[0:4, 20:24],
                                                   BIAS[0:4, 16:20], AF.Sqrt))
                        v.wait_ge(sec.h, ss)
                        v.tensor_scalar(SVEC[0:4, 4:8], BIAS[0:4, 20:24],
                                        -1.0, None, ALU.mult)
                        p.bar()

            b_groups(is_q=False)
            b_groups(is_q=True)

            with p.phase() as alloc:
                WVB = alloc((128, CH * 1024), BF16)
                VST = [alloc((128, 512), BF16) for _ in range(2)]
                g.wait_ge(tec.h, tec.n)
                for c in range(CH):
                    store(WVB[:, c * 1024:(c + 1) * 1024],
                          cd["BQKV"][c * 128:(c + 1) * 128, 2048:3072])
                nw = st.n
                p.bar()
                vp = Pool(VST, p, "vst")
                pp = Pool([ps[:, 0:512] for ps in PS1[0:3]])
                for tc in range(NTC):
                    tsl = slice(tc * 128, (tc + 1) * 128)
                    for vb2 in range(2):
                        bank = pp.get(te)
                        te.wait_ge(st.h, nw)
                        for c in range(CH):
                            mm = te.matmul(bank.ap, YH[c][:, tsl],
                                           WVB[:, c * 1024 + 512 * vb2:c * 1024 + 512 * (vb2 + 1)],
                                           start=(c == 0), stop=(c == CH - 1))
                        tv = tec.inc(mm)
                        vbuf = vp.get(sc)
                        sc.wait_ge(tec.h, tv)
                        sv = sec.inc(sc.copy(vbuf.ap[:], bank.ap))
                        bank.release(sec, sv)
                        g.wait_ge(sec.h, sv)
                        n_st = vbuf.ctr.inc(g.dma_start(
                            v_sp[8 * vb2:8 * (vb2 + 1), tc:tc + 1, :, :].rearrange(
                                "h o pp dd -> (o pp) h dd"),
                            vbuf.ap.rearrange("pp (h dd) -> pp h dd", h=8)), 16)
                        vbuf.release(vbuf.ctr, n_st)
                for b_ in vp.bufs:
                    g.wait_ge(b_.ctr.h, b_.ctr.n)
                p.bar()

        attn_core(causal=True)

        with p.phase() as alloc:
            OALL = [alloc((128, S), BF16) for _ in range(CH)]
            WOB = [alloc((128, CH * 128), BF16) for _ in range(2)]
            XT = [alloc((128, S), F32) for _ in range(2)]
            XO = [alloc((128, S), F32) for _ in range(2)]
            for c in range(CH):
                load(OALL[c][:], o_sp[c * 128:(c + 1) * 128, :])
            sy.wait_ge(ld.h, ld.n)
            p.bar()
            xp = Pool(XT, p, "xr")
            xop = Pool(XO, p, "xo")
            wcc = p.ctr("wcc")
            pp = Pool([ps[:, 0:B] for ps in PS1[0:3]])
            SQF = [alloc((128, B), F32) for _ in range(2)]
            SROWX = alloc((1, S), F32)
            QROWX = alloc((1, S), F32)
            SBK = [PS2[0][:, 0:B], PS2[0][:, B:2 * B]]
            sg_read = [0, 0]
            sq_rd = [0, 0]
            pend_stat = []

            def do_fold_stats(mch, xo, item):
                b, nv = item
                blk = slice(b * B, (b + 1) * B)
                bpar = b % 2
                sc.wait_ge(vec.h, nv)
                sc.wait_ge(tec.h, sq_rd[bpar])
                sq2 = sec.inc(sc.activation(SQF[bpar][:], xo.ap[:, blk],
                                            AF.Square))
                te.wait_ge(sec.h, sq2)
                te.wait_ge(vec.h, sg_read[bpar])
                te.matmul(SBK[bpar][0:1, 0:B], ONE[:, 0:1], xo.ap[:, blk],
                          start=True, stop=True)
                mm2 = te.matmul(SBK[bpar][32:33, 0:B], ONE[:, 0:1],
                                SQF[bpar][:], start=True, stop=True)
                sq_rd[bpar] = tv2 = tec.inc(mm2)
                v.wait_ge(tec.h, tv2)
                if mch == 0:
                    v.tensor_copy(SROWX[:, blk], SBK[bpar][0:1, 0:B])
                    ins = v.tensor_copy(QROWX[:, blk], SBK[bpar][32:33, 0:B])
                else:
                    v.tensor_add(SROWX[:, blk], SROWX[:, blk],
                                 SBK[bpar][0:1, 0:B])
                    ins = v.tensor_add(QROWX[:, blk], QROWX[:, blk],
                                       SBK[bpar][32:33, 0:B])
                sg_read[bpar] = vec.inc(ins)
            tgt = out_d if is_last else x_hbm
            wob_guard = [0, 0]
            for mch in range(CH):
                par = mch % 2
                g.wait_ge(tec.h, wob_guard[par])
                for c in range(CH):
                    wcc.inc(g.dma_start(
                        WOB[par][:, c * 128:(c + 1) * 128],
                        cd["BWO"][c * 128:(c + 1) * 128, mch * 128:(mch + 1) * 128]), 16)
                nw = wcc.n
                xb = xp.get(sy)
                n_ld = xb.ctr.inc(sy.dma_start(
                    xb.ap[:], x_hbm[mch * 128:(mch + 1) * 128, :]), 16)
                xo = xop.get(v)
                for b in range(NB):
                    blk = slice(b * B, (b + 1) * B)
                    bank = pp.get(te)
                    te.wait_ge(wcc.h, nw)
                    for c in range(CH):
                        mm = te.matmul(bank.ap, WOB[par][:, c * 128:(c + 1) * 128],
                                       OALL[c][:, blk], start=(c == 0),
                                       stop=(c == CH - 1))
                    tv = tec.inc(mm)
                    v.wait_ge(tec.h, tv)
                    v.wait_ge(xb.ctr.h, n_ld)
                    ins = v.tensor_add(xo.ap[:, blk], bank.ap, xb.ap[:, blk])
                    nv = vec.inc(ins)
                    bank.release(vec, nv)
                    pend_stat.append((b, nv))
                    if len(pend_stat) > 1:
                        do_fold_stats(mch, xo, pend_stat.pop(0))
                while pend_stat:
                    do_fold_stats(mch, xo, pend_stat.pop(0))
                wob_guard[par] = tec.n
                xb.release(vec, vec.n)
                g.wait_ge(vec.h, vec.n)
                n_st = xo.ctr.inc(g.dma_start(
                    tgt[mch * 128:(mch + 1) * 128, :], xo.ap[:]), 16)
                xo.release(xo.ctr, n_st)
            if not is_last:
                g.wait_ge(vec.h, vec.n)
                vrb.inc(g.dma_start(vrow_hbm[0:1, :], SROWX[:]), 16)
                nvr = vrb.inc(g.dma_start(vrow_hbm[1:2, :], QROWX[:]), 16)
                g.wait_ge(vrb.h, nvr)
            for b_ in xop.bufs:
                g.wait_ge(b_.ctr.h, b_.ctr.n)
            p.bar()

    attn_a(is_last=(trunc == 1))
    if trunc >= 2:
        ffn_layer(is_last=(trunc == 2))
    if trunc >= 3:
        attn_b(is_last=(trunc == 3))
    if trunc >= 4:
        ffn_layer(is_last=True)

    if debug_dump is not None:
        src = dbg_scratch[debug_dump]
        dbg = nc.dram_tensor("dbg", list(src.tensor.shape),
                             src.tensor.dtype, kind="ExternalOutput").ap()
        n = store(dbg[:], src[:])
        g.wait_ge(st.h, n)
        p.bar()

    if trunc < 1:
        with p.phase() as alloc:
            XT = alloc((128, S), F32)
            for c in range(CH):
                n = load(XT[:], x_hbm[c * 128:(c + 1) * 128, :])
                sy.wait_ge(ld.h, n)
                m_ = store(out_d[c * 128:(c + 1) * 128, :], XT[:])
                sy.wait_ge(st.h, m_)
            g.wait_ge(st.h, st.n)
    p.bar()
    g.wait_ge(st.h, st.n)
    sy.wait_ge(st.h, st.n)
    return nc, es


def kernel(**inputs):
    m, xT = _prep_inputs(inputs)
    nc, es = build_program()
    in_maps = []
    for core in range(NCORE):
        d = dict(m)
        d["x"] = xT[core]
        in_maps.append(d)
    res = run_bass_kernel_spmd(nc, in_maps, core_ids=list(range(NCORE)))
    es.close()
    out = np.stack([np.asarray(r["out"]) for r in res.results], axis=0)
    return np.ascontiguousarray(out.transpose(0, 2, 1))
